# version 02
# baseline (speedup 1.0000x reference)
import numpy as np
import ml_dtypes
import concourse.bass as bass
import concourse.mybir as mybir
from concourse.bass_utils import run_bass_kernel_spmd

F32 = mybir.dt.float32
BF16 = mybir.dt.bfloat16
I32 = mybir.dt.int32
AF = mybir.ActivationFunctionType
ALU = mybir.AluOpType
AX = mybir.AxisListType.X

S = 4096
D = 2048
H = 16
DH = 128
IFF = 5504
NCH = D // 128
ICH = IFF // 128
NT128 = S // 128
NT512 = S // 512
L0_SCALE = float(np.float32(1.0) / np.sqrt(np.sqrt(np.float32(DH), dtype=np.float32)))
L1_KSCALE2 = 1.0 / DH
MASK = -10000.0
L1_NEG_BOUND = -12.0


class Sem:
    def __init__(self, es, nc, name):
        self.h = es.enter_context(nc.semaphore(name))
        self.n = 0

    def dma(self, instr):
        self.n += 16
        instr.then_inc(self.h, 16)
        return self.n

    def inc(self, instr):
        self.n += 1
        instr.then_inc(self.h, 1)
        return self.n


class SemPool:
    def __init__(self, es, nc):
        self.es, self.nc = es, nc
        self.free = []
        self.k = 0

    def get(self):
        if self.free:
            return self.free.pop(0)
        s = Sem(self.es, self.nc, f"sp{self.k}")
        self.k += 1
        return s

    def put(self, *sems):
        self.free.extend(sems)


CAST_KEYS = ("wq0", "wk0", "wv0", "wo0", "wq1", "wk1", "wv1", "wo1")


def p_cast(nc, pes, sp, t):
    jobs = []
    gathers = []
    for key in CAST_KEYS:
        src, dst = t[key + "s"], t[key + "_loc"]
        for r in range(2):
            jobs.append((src, dst, src.shape[1], r, 128, False))
        gathers.append((t[key + "_loc"], t[key + "_bf"]))
    for r in range(2):
        jobs.append((t["w1ffns"], t["bw1_loc"], 2048, r, 128, True))
    gathers.append((t["bw1_loc"], t["bw1_bf"]))

    ld, cv, st = sp.get(), sp.get(), sp.get()
    with nc.Block("pcast") as blk, \
         nc.sbuf_tensor("caf0", [128, 2048], F32) as f0, \
         nc.sbuf_tensor("caf1", [128, 2048], F32) as f1, \
         nc.sbuf_tensor("cab0", [128, 2048], BF16) as b0, \
         nc.sbuf_tensor("cab1", [128, 2048], BF16) as b1, \
         nc.sbuf_tensor("cacol", [128, 4], F32) as col, \
         nc.sbuf_tensor("casgn", [128, 2048], F32) as sgn:
        fb, ob = [f0, f1], [b0, b1]
        ld0, cv0, st0 = ld.n, cv.n, st.n
        cc = sp.get()
        cc0 = cc.n
        done_at = []
        acc = cv0
        for (_s, _d, _c, _r, _rw, _b) in jobs:
            acc += 3 if _b else 1
            done_at.append(acc)

        @blk.sync
        def _(sync):
            for i, (src, dst, cols, r, rw, _) in enumerate(jobs):
                if i >= 2:
                    sync.wait_ge(cv.h, done_at[i - 2])
                ld.dma(sync.dma_start(out=fb[i % 2][0:rw, 0:cols],
                                      in_=src[r * 128:r * 128 + rw, :]))

        @blk.vector
        def _(vector):
            for i, (src, dst, cols, r, rw, binz) in enumerate(jobs):
                vector.wait_ge(ld.h, ld0 + 16 * (i + 1))
                if i >= 2:
                    vector.wait_ge(st.h, st0 + 16 * (i - 1))
                if binz:
                    r_ = vector.tensor_reduce(col[:, 0:1], fb[i % 2][:, 0:cols], AX,
                                              ALU.add, apply_absolute_value=True)
                    cv.inc(r_)
                    vector.wait_ge(cv.h, cv.n)
                    t_ = vector.tensor_scalar(col[:, 2:3], col[:, 0:1], 1.0 / cols, None,
                                              op0=ALU.mult)
                    cv.inc(t_)
                    vector.wait_ge(cv.h, cv.n)
                    vector.tensor_scalar(sgn[:, 0:cols], fb[i % 2][:, 0:cols], 0.0, None,
                                         op0=ALU.is_ge)
                    vector.tensor_scalar(sgn[:, 0:cols], sgn[:, 0:cols], 2.0, -1.0,
                                         op0=ALU.mult, op1=ALU.add)
                    vector.tensor_scalar(ob[i % 2][:, 0:cols], sgn[:, 0:cols],
                                         col[:, 2:3], None, op0=ALU.mult)
                else:
                    vector.tensor_copy(ob[i % 2][:, 0:cols], fb[i % 2][:, 0:cols])
                cv.inc(vector.engine_nop())
                assert cv.n == done_at[i]

        @blk.gpsimd
        def _(gpsimd):
            for i, (src, dst, cols, r, rw, _) in enumerate(jobs):
                gpsimd.wait_ge(cv.h, done_at[i])
                st.dma(gpsimd.dma_start(out=dst[r * 128:r * 128 + rw, :],
                                        in_=ob[i % 2][0:rw, 0:cols]))
            gpsimd.wait_ge(st.h, st.n)
            for locT, fullT in gathers:
                cc.inc(gpsimd.collective_compute(
                    "AllGather", ALU.bypass, replica_groups=[list(range(8))],
                    ins=[locT.opt()], outs=[fullT.opt()]))
            gpsimd.wait_ge(cc.h, cc.n)
    sp.put(ld, cv, st, cc)


def p_wgen(nc, pes, sp, t):
    jobs = ([(t["gidxs"], t["wg_loc"], IFF, r, 128) for r in range(2)]
            + [(t["uidxs"], t["wu_loc"], IFF, r, 128) for r in range(2)]
            + [(t["didxs"], t["wd_loc"], 2048, r, (128 if r < 5 else 48))
               for r in range(6)])
    gathers = [(t["wg_loc"], t["wg_bf"]), (t["wu_loc"], t["wu_bf"]),
               (t["wd_loc"], t["wd_bf"])]

    ld, pre, cv, st = sp.get(), sp.get(), sp.get(), sp.get()
    with nc.Block("pwgen") as blk, \
         nc.sbuf_tensor("wgi0", [128, IFF], I32) as i0, \
         nc.sbuf_tensor("wgi1", [128, IFF], I32) as i1, \
         nc.sbuf_tensor("wgf", [128, IFF], BF16) as fbuf, \
         nc.sbuf_tensor("wgt", [128, IFF], BF16) as term, \
         nc.sbuf_tensor("wga0", [128, IFF], BF16) as a0, \
         nc.sbuf_tensor("wga1", [128, IFF], BF16) as a1, \
         nc.sbuf_tensor("wgl", [1, 16], F32) as lut_s, \
         nc.sbuf_tensor("wgon", [1, 128], F32) as ones_r, \
         nc.psum_tensor("wglp", [128, 16], F32) as lut_ps, \
         nc.sbuf_tensor("wglb", [128, 16], F32) as lut_bc:
        ib, ab = [i0, i1], [a0, a1]
        cc = sp.get()
        ld0, pre0, cv0, st0, cc0 = ld.n, pre.n, cv.n, st.n, cc.n

        @blk.sync
        def _(sync):
            pre.dma(sync.dma_start(out=lut_s[0:1, :], in_=t["lut"][:, :]))
            for i, (src, dst, cols, r, rw) in enumerate(jobs):
                if i >= 2:
                    sync.wait_ge(cv.h, cv0 + i - 1)
                ld.dma(sync.dma_start(out=ib[i % 2][0:rw, 0:cols],
                                      in_=src[r * 128:r * 128 + rw, :]))

        @blk.tensor
        def _(tensor):
            tensor.wait_ge(pre.h, pre0 + 17)
            pre.n = pre0 + 17
            pre.inc(tensor.matmul(lut_ps[:, :], ones_r[0:1, :], lut_s[0:1, :],
                                  start=True, stop=True))
            pre.n += 1

        @blk.vector
        def _(vector):
            vector.memset(ones_r[0:1, :], 1.0)
            vector.engine_nop().then_inc(pre.h, 1)
            vector.wait_ge(pre.h, pre0 + 18)
            vector.tensor_copy(lut_bc[:, :], lut_ps[:, :]).then_inc(pre.h, 1)
            vector.wait_ge(pre.h, pre0 + 19)
            for i, (src, dst, cols, r, rw) in enumerate(jobs):
                vector.wait_ge(ld.h, ld0 + 16 * (i + 1))
                if i >= 2:
                    vector.wait_ge(st.h, st0 + 16 * (i - 1))
                a = ab[i % 2]
                vector.tensor_copy(fbuf[0:rw, 0:cols], ib[i % 2][0:rw, 0:cols])
                vector.tensor_scalar(a[0:rw, 0:cols], fbuf[0:rw, 0:cols], 0.0,
                                     lut_bc[0:rw, 0:1], op0=ALU.is_equal, op1=ALU.mult)
                for j in range(1, 16):
                    vector.tensor_scalar(term[0:rw, 0:cols], fbuf[0:rw, 0:cols], float(j),
                                         lut_bc[0:rw, j:j + 1], op0=ALU.is_equal, op1=ALU.mult)
                    vector.tensor_add(a[0:rw, 0:cols], a[0:rw, 0:cols], term[0:rw, 0:cols])
                cv.inc(vector.engine_nop())

        @blk.gpsimd
        def _(gpsimd):
            for i, (src, dst, cols, r, rw) in enumerate(jobs):
                gpsimd.wait_ge(cv.h, cv0 + i + 1)
                st.dma(gpsimd.dma_start(out=dst[r * 128:r * 128 + rw, :],
                                        in_=ab[i % 2][0:rw, 0:cols]))
            gpsimd.wait_ge(st.h, st.n)
            for locT, fullT in gathers:
                cc.inc(gpsimd.collective_compute(
                    "AllGather", ALU.bypass, replica_groups=[list(range(8))],
                    ins=[locT.opt()], outs=[fullT.opt()]))
            gpsimd.wait_ge(cc.h, cc.n)
    sp.put(ld, pre, cv, st, cc)


def p_consts(nc, pes, sp, t, g):
    ld, cv, ev = sp.get(), sp.get(), sp.get()
    with nc.Block("pconst") as blk, \
         nc.psum_tensor("cobc", [128, 512], F32) as bc_ps:
        ld0, cv0, ev0 = ld.n, cv.n, ev.n

        @blk.sync
        def _(sync):
            for dst, src in [(g["ident"], t["ident"]), (g["tri"], t["tri"]),
                             (g["rcos"], t["ropecos"]), (g["rsin"], t["ropesin"]),
                             (g["bq"], t["bq0"]), (g["bk"], t["bk0"]),
                             (g["bv"], t["bv0"]), (g["bo_r"], t["bo0"]),
                             (g["b1_r"], t["b1"])]:
                ld.dma(sync.dma_start(out=dst[:, :], in_=src[:, :]))

        @blk.vector
        def _(vector):
            vector.memset(g["ones_f"][:, :], 1.0)
            vector.memset(g["ones_r"][0:1, :], 1.0)
            cv.inc(vector.engine_nop())

        @blk.tensor
        def _(tensor):
            tensor.wait_ge(ld.h, ld.n)
            tensor.wait_ge(cv.h, cv0 + 1)
            k = 0
            for row in (g["bo_r"], g["b1_r"]):
                for q in range(4):
                    if k >= 1:
                        tensor.wait_ge(ev.h, ev0 + k)
                    cv.inc(tensor.matmul(bc_ps[:, :], g["ones_r"][0:1, :],
                                         row[0:1, q * 512:(q + 1) * 512],
                                         start=True, stop=True))
                    k += 1

        @blk.scalar
        def _(scalar):
            k = 0
            for dstk in ("bo_bc", "b1_bc"):
                for q in range(4):
                    scalar.wait_ge(cv.h, cv0 + 2 + k)
                    ev.inc(scalar.copy(g[dstk][:, q * 512:(q + 1) * 512], bc_ps[:, :]))
                    k += 1
            scalar.wait_ge(ev.h, ev.n)
    sp.put(ld, cv, ev)


def p_layernorm(nc, pes, sp, g, name, x_src, xnT_dst):
    ld, sqs, sts, rsd, app, tp, ev, out, vch = [sp.get() for _ in range(9)]
    with nc.Block(name) as blk, \
         nc.sbuf_tensor(name + "x0", [128, D], F32) as x0, \
         nc.sbuf_tensor(name + "x1", [128, D], F32) as x1, \
         nc.sbuf_tensor(name + "sq", [128, D], F32) as sqb, \
         nc.sbuf_tensor(name + "xt", [128, D], F32) as xtmp, \
         nc.sbuf_tensor(name + "c", [128, 8], F32) as col, \
         nc.sbuf_tensor(name + "n0", [128, D], BF16) as n0, \
         nc.sbuf_tensor(name + "n1", [128, D], BF16) as n1, \
         nc.psum_tensor(name + "p0", [128, 128], BF16) as p0, \
         nc.psum_tensor(name + "p1", [128, 128], BF16) as p1, \
         nc.sbuf_tensor(name + "o0", [128, NCH, 128], BF16) as o0, \
         nc.sbuf_tensor(name + "o1", [128, NCH, 128], BF16) as o1:
        xb, xnb, tps, ob = [x0, x1], [n0, n1], [p0, p1], [o0, o1]
        c0 = {s: s.n for s in (ld, sqs, sts, rsd, app, tp, ev, out, vch)}
        vch.n += 6 * NT128

        @blk.sync
        def _(sync):
            for i in range(NT128):
                if i >= 2:
                    sync.wait_ge(app.h, c0[app] + i - 1)
                ld.dma(sync.dma_start(out=xb[i % 2][:, :],
                                      in_=x_src[i * 128:(i + 1) * 128, :]))

        @blk.scalar
        def _(scalar):
            for i in range(NT128):
                scalar.wait_ge(ld.h, c0[ld] + 16 * (i + 1))
                if i >= 1:
                    scalar.wait_ge(sts.h, c0[sts] + i)
                sqs.inc(scalar.activation(sqb[:, :], xb[i % 2][:, :], AF.Square))
                scalar.wait_ge(sts.h, c0[sts] + i + 1)
                rsd.inc(scalar.activation(col[:, 7:8], col[:, 6:7], AF.Sqrt))
                if i >= 1:
                    if i >= 3:
                        scalar.wait_ge(out.h, c0[out] + 16 * (i - 2))
                    for c in range(NCH):
                        k = (i - 1) * NCH + c
                        scalar.wait_ge(tp.h, c0[tp] + k + 1)
                        ev.inc(scalar.copy(ob[(i - 1) % 2][:, c, :], tps[c % 2][:, :]))
            scalar.wait_ge(out.h, c0[out] + 16 * (NT128 - 2))
            for c in range(NCH):
                k = (NT128 - 1) * NCH + c
                scalar.wait_ge(tp.h, c0[tp] + k + 1)
                ev.inc(scalar.copy(ob[(NT128 - 1) % 2][:, c, :], tps[c % 2][:, :]))

        @blk.vector
        def _(vector):
            vk = [c0[vch]]

            def step(instr):
                vk[0] += 1
                instr.then_inc(vch.h, 1)
                vector.wait_ge(vch.h, vk[0])
            for i in range(NT128):
                vector.wait_ge(sqs.h, c0[sqs] + i + 1)
                step(vector.tensor_reduce(col[:, 1:2], xb[i % 2][:, :], AX, ALU.add))
                step(vector.tensor_reduce(col[:, 0:1], sqb[:, :], AX, ALU.add))
                step(vector.tensor_scalar(col[:, 2:3], col[:, 1:2], -1.0 / D, None, op0=ALU.mult))
                step(vector.tensor_tensor(col[:, 3:4], col[:, 2:3], col[:, 2:3], ALU.mult))
                step(vector.tensor_scalar(col[:, 4:5], col[:, 0:1], 1.0 / D, 1e-5,
                                          op0=ALU.mult, op1=ALU.add))
                step(vector.tensor_tensor(col[:, 5:6], col[:, 4:5], col[:, 3:4], ALU.subtract))
                sts.inc(vector.reciprocal(col[:, 6:7], col[:, 5:6]))
                vector.wait_ge(rsd.h, c0[rsd] + i + 1)
                if i >= 2:
                    vector.wait_ge(tp.h, c0[tp] + NCH * (i - 1))
                vector.tensor_scalar(xtmp[:, :], xb[i % 2][:, :],
                                     col[:, 2:3], None, op0=ALU.add)
                app.inc(vector.tensor_scalar(xnb[i % 2][:, :], xtmp[:, :],
                                             col[:, 7:8], None, op0=ALU.mult))

        @blk.tensor
        def _(tensor):
            for i in range(NT128):
                tensor.wait_ge(app.h, c0[app] + i + 1)
                for c in range(NCH):
                    k = i * NCH + c
                    if k >= 2:
                        tensor.wait_ge(ev.h, c0[ev] + k - 1)
                    tp.inc(tensor.transpose(tps[c % 2][:, :],
                                            xnb[i % 2][:, c * 128:(c + 1) * 128],
                                            g["ident"][:, :]))

        @blk.gpsimd
        def _(gpsimd):
            for i in range(NT128):
                gpsimd.wait_ge(ev.h, c0[ev] + NCH * (i + 1))
                out.dma(gpsimd.dma_start(
                    out=xnT_dst.rearrange("(c p) t -> p c t", p=128)[:, :, i * 128:(i + 1) * 128],
                    in_=ob[i % 2][:, :, :]))
            gpsimd.wait_ge(out.h, out.n)
    sp.put(ld, sqs, sts, rsd, app, tp, ev, out, vch)


def p_proj(nc, pes, sp, g, name, w_bf, xnT_src, out_dst, o_chunks,
           transpose_w, bias_col=None, scale=None, norm_base=None,
           token_major=False):
    ocols = o_chunks * 128
    sems = [sp.get() for _ in range(9)]
    wl, xl, mm, evd, sq, nmm, nmax, tev, out = sems
    tpt = sp.get()
    tiles = [(tt, oc) for tt in range(NT512) for oc in range(o_chunks)]
    with nc.Block(name) as blk, \
         nc.sbuf_tensor(name + "w", [128, NCH, ocols], BF16) as w_sb, \
         nc.sbuf_tensor(name + "xa", [128, NCH, 512], BF16) as xa, \
         nc.sbuf_tensor(name + "xb", [128, NCH, 512], BF16) as xbt, \
         nc.psum_tensor(name + "p0", [128, 512], F32) as ps0, \
         nc.psum_tensor(name + "p1", [128, 512], F32) as ps1, \
         nc.sbuf_tensor(name + "e0", [128, 512], BF16) as e0, \
         nc.sbuf_tensor(name + "e1", [128, 512], BF16) as e1, \
         nc.sbuf_tensor(name + "nq", [128, 512], F32) as nsq, \
         nc.psum_tensor(name + "np", [1, 512], F32) as nps, \
         nc.psum_tensor(name + "t0", [128, 128], BF16) as tp0, \
         nc.psum_tensor(name + "t1", [128, 128], BF16) as tp1, \
         nc.sbuf_tensor(name + "b0", [128, 128], BF16) as tb0, \
         nc.sbuf_tensor(name + "b1", [128, 128], BF16) as tb1:
        xn_sb, ps, eb = [xa, xbt], [ps0, ps1], [e0, e1]
        tps, tb = [tp0, tp1], [tb0, tb1]
        c0 = {s: s.n for s in sems + [tpt]}
        bias = g[bias_col] if bias_col else None

        @blk.sync
        def _(sync):
            for c in range(NCH):
                if transpose_w:
                    wl.dma(sync.dma_start(out=w_sb[:, c, :],
                                          in_=w_bf[:, c * 128:(c + 1) * 128],
                                          transpose=True))
                else:
                    wl.dma(sync.dma_start(out=w_sb[:, c, :],
                                          in_=w_bf[c * 128:(c + 1) * 128, :]))
            for tt in range(NT512):
                if tt >= 2:
                    sync.wait_ge(mm.h, c0[mm] + (tt - 1) * o_chunks)
                xl.dma(sync.dma_start(
                    out=xn_sb[tt % 2][:, :, :],
                    in_=xnT_src.rearrange("(c p) t -> p c t", p=128)[:, :, tt * 512:(tt + 1) * 512]))

        @blk.tensor
        def _(tensor):
            tensor.wait_ge(wl.h, wl.n)
            for i, (tt, oc) in enumerate(tiles):
                tensor.wait_ge(xl.h, c0[xl] + 16 * (tt + 1))
                if i >= 2:
                    tensor.wait_ge(evd.h, c0[evd] + i - 1)
                for c in range(NCH):
                    m = tensor.matmul(ps[i % 2][:, :],
                                      w_sb[:, c, oc * 128:(oc + 1) * 128],
                                      xn_sb[tt % 2][:, c, :],
                                      start=(c == 0), stop=(c == NCH - 1))
                    if c == NCH - 1:
                        mm.inc(m)
                if norm_base is not None and i >= 1:
                    tensor.wait_ge(sq.h, c0[sq] + i)
                    if i >= 2:
                        tensor.wait_ge(nmax.h, c0[nmax] + i - 1)
                    nmm.inc(tensor.matmul(nps[0:1, :], g["ones_f"][:, 0:1], nsq[:, :],
                                          start=True, stop=True))
                if token_major and i >= 1:
                    tensor.wait_ge(evd.h, c0[evd] + i)
                    for q in range(4):
                        k = (i - 1) * 4 + q
                        if k >= 2:
                            tensor.wait_ge(tev.h, c0[tev] + k - 1)
                        tpt.inc(tensor.transpose(tps[q % 2][:, :],
                                                 eb[(i - 1) % 2][:, q * 128:(q + 1) * 128],
                                                 g["ident"][:, :]))
            if norm_base is not None:
                i = len(tiles)
                tensor.wait_ge(sq.h, c0[sq] + i)
                tensor.wait_ge(nmax.h, c0[nmax] + i - 1)
                nmm.inc(tensor.matmul(nps[0:1, :], g["ones_f"][:, 0:1], nsq[:, :],
                                      start=True, stop=True))
            if token_major:
                i = len(tiles)
                for q in range(4):
                    k = (i - 1) * 4 + q
                    tensor.wait_ge(tev.h, c0[tev] + k - 1)
                    tpt.inc(tensor.transpose(tps[q % 2][:, :],
                                             eb[(i - 1) % 2][:, q * 128:(q + 1) * 128],
                                             g["ident"][:, :]))

        @blk.vector
        def _(vector):
            for i, (tt, oc) in enumerate(tiles):
                vector.wait_ge(mm.h, c0[mm] + i + 1)
                if not token_major and i >= 2:
                    vector.wait_ge(out.h, c0[out] + 16 * (i - 1))
                if norm_base is not None and i >= 2:
                    vector.wait_ge(sq.h, c0[sq] + i - 1)
                if token_major and i >= 2:
                    vector.wait_ge(tpt.h, c0[tpt] + 4 * (i - 1))
                if bias is not None and scale is not None:
                    evd.inc(vector.tensor_scalar(eb[i % 2][:, :], ps[i % 2][:, :],
                                                 bias[:, oc:oc + 1], float(scale),
                                                 op0=ALU.add, op1=ALU.mult))
                elif bias is not None:
                    evd.inc(vector.tensor_scalar(eb[i % 2][:, :], ps[i % 2][:, :],
                                                 bias[:, oc:oc + 1], None, op0=ALU.add))
                else:
                    evd.inc(vector.tensor_copy(eb[i % 2][:, :], ps[i % 2][:, :]))
                if norm_base is not None and i >= 1:
                    vector.wait_ge(nmm.h, c0[nmm] + i)
                    slot = norm_base + i - 1
                    nmax.inc(vector.tensor_reduce(g["normcol"][0:1, slot:slot + 1],
                                                  nps[0:1, :], AX, ALU.max))
            if norm_base is not None:
                i = len(tiles)
                vector.wait_ge(nmm.h, c0[nmm] + i)
                slot = norm_base + i - 1
                nmax.inc(vector.tensor_reduce(g["normcol"][0:1, slot:slot + 1],
                                              nps[0:1, :], AX, ALU.max))

        @blk.scalar
        def _(scalar):
            for i, (tt, oc) in enumerate(tiles):
                if norm_base is not None:
                    scalar.wait_ge(evd.h, c0[evd] + i + 1)
                    if i >= 1:
                        scalar.wait_ge(nmm.h, c0[nmm] + i)
                    sq.inc(scalar.activation(nsq[:, :], eb[i % 2][:, :], AF.Square))
                if token_major and i >= 1:
                    for q in range(4):
                        k = (i - 1) * 4 + q
                        scalar.wait_ge(tpt.h, c0[tpt] + k + 1)
                        if k >= 2:
                            scalar.wait_ge(out.h, c0[out] + 16 * (k - 1))
                        tev.inc(scalar.copy(tb[q % 2][:, :], tps[q % 2][:, :]))
            if token_major:
                i = len(tiles)
                for q in range(4):
                    k = (i - 1) * 4 + q
                    scalar.wait_ge(tpt.h, c0[tpt] + k + 1)
                    scalar.wait_ge(out.h, c0[out] + 16 * (k - 1))
                    tev.inc(scalar.copy(tb[q % 2][:, :], tps[q % 2][:, :]))

        @blk.gpsimd
        def _(gpsimd):
            for i, (tt, oc) in enumerate(tiles):
                if token_major:
                    for q in range(4):
                        k = i * 4 + q
                        gpsimd.wait_ge(tev.h, c0[tev] + k + 1)
                        out.dma(gpsimd.dma_start(
                            out=out_dst[tt * 512 + q * 128:tt * 512 + (q + 1) * 128,
                                        oc * 128:(oc + 1) * 128],
                            in_=tb[q % 2][:, :]))
                else:
                    gpsimd.wait_ge(evd.h, c0[evd] + i + 1)
                    out.dma(gpsimd.dma_start(
                        out=out_dst[oc * 128:(oc + 1) * 128, tt * 512:(tt + 1) * 512],
                        in_=eb[i % 2][:, :]))
            gpsimd.wait_ge(out.h, out.n)
    sp.put(*sems, tpt)


def p_negm(nc, pes, sp, g):
    cv = sp.get()
    with nc.Block("pnegm") as blk, \
         nc.sbuf_tensor("ngc", [1, 8], F32) as c, \
         nc.psum_tensor("ngp", [128, 1], F32) as psb:
        cv0 = cv.n

        @blk.vector
        def _(vector):
            cv.inc(vector.tensor_reduce(c[0:1, 0:1], g["normcol"][0:1, 0:128], AX, ALU.max))
            vector.wait_ge(cv.h, cv0 + 1)
            cv.inc(vector.tensor_reduce(c[0:1, 1:2], g["normcol"][0:1, 128:256], AX, ALU.max))
            vector.wait_ge(cv.h, cv0 + 3)
            cv.inc(vector.tensor_tensor(c[0:1, 4:5], c[0:1, 2:3], c[0:1, 3:4], ALU.mult))
            vector.wait_ge(cv.h, cv0 + 4)
            cv.inc(vector.tensor_scalar(c[0:1, 5:6], c[0:1, 4:5], -1.0, -1.0,
                                        op0=ALU.mult, op1=ALU.add))
            vector.wait_ge(cv.h, cv0 + 6)
            cv.inc(vector.tensor_copy(g["negm"][:, 0:1], psb[:, 0:1]))

        @blk.scalar
        def _(scalar):
            scalar.wait_ge(cv.h, cv0 + 2)
            scalar.activation(c[0:1, 2:3], c[0:1, 0:1], AF.Sqrt)
            cv.inc(scalar.activation(c[0:1, 3:4], c[0:1, 1:2], AF.Sqrt))

        @blk.tensor
        def _(tensor):
            tensor.wait_ge(cv.h, cv0 + 5)
            cv.inc(tensor.matmul(psb[:, 0:1], g["ones_r"][0:1, :], c[0:1, 5:6],
                                 start=True, stop=True))
    sp.put(cv)


def p_attn(nc, pes, sp, g, name, qT, kT, vtok, attnT_dst, kv_heads, negm_imm):
    grp = H // kv_heads
    sems = [sp.get() for _ in range(10)]
    ld, smm, mskd, expd, pvd, nrm, tpt, tev, outst, vrdy = sems

    sched = []
    cs = mk = 0
    for h in range(H):
        for T in range(NT512):
            ch = [(c, max(0, c * 128 - T * 512)) for c in range(4 * T + 4)]
            nd = sum(1 for c, _ in ch if c * 128 >= T * 512)
            tile = h * NT512 + T
            sched.append(dict(h=h, T=T, ch=ch, cs0=cs, mk0=mk,
                              base=tile * 4, tile=tile))
            cs += len(ch)
            mk += nd

    from contextlib import ExitStack as _ES
    _pes = _ES()
    blk = _pes.enter_context(nc.Block(name))
    k0 = _pes.enter_context(nc.sbuf_tensor(name + "k0", [128, S], BF16))
    k1 = _pes.enter_context(nc.sbuf_tensor(name + "k1", [128, S], BF16))
    q0 = _pes.enter_context(nc.sbuf_tensor(name + "q0", [128, S], BF16))
    q1 = _pes.enter_context(nc.sbuf_tensor(name + "q1", [128, S], BF16))
    v0 = _pes.enter_context(nc.sbuf_tensor(name + "v0", [128, 32, 132], BF16))
    v1 = _pes.enter_context(nc.sbuf_tensor(name + "v1", [128, 32, 132], BF16))
    pt_sb = _pes.enter_context(nc.sbuf_tensor(name + "pt", [128, 32, 512], BF16))
    s0 = _pes.enter_context(nc.psum_tensor(name + "s0", [128, 512], F32))
    s1 = _pes.enter_context(nc.psum_tensor(name + "s1", [128, 512], F32))
    o20 = _pes.enter_context(nc.psum_tensor(name + "o0", [128, 132], F32))
    o21 = _pes.enter_context(nc.psum_tensor(name + "o1", [128, 132], F32))
    o22 = _pes.enter_context(nc.psum_tensor(name + "o2", [128, 132], F32))
    o23 = _pes.enter_context(nc.psum_tensor(name + "o3", [128, 132], F32))
    tq0 = _pes.enter_context(nc.psum_tensor(name + "tq0", [128, 128], BF16))
    tq1 = _pes.enter_context(nc.psum_tensor(name + "tq1", [128, 128], BF16))
    a0 = _pes.enter_context(nc.sbuf_tensor(name + "a0", [128, 128], BF16))
    a1 = _pes.enter_context(nc.sbuf_tensor(name + "a1", [128, 128], BF16))
    linv = _pes.enter_context(nc.sbuf_tensor(name + "li", [128, 4], F32))
    negm_t = _pes.enter_context(nc.sbuf_tensor(name + "nm", [128, 1], F32))
    at0 = _pes.enter_context(nc.sbuf_tensor(name + "at0", [128, 512], BF16))
    at1 = _pes.enter_context(nc.sbuf_tensor(name + "at1", [128, 512], BF16))
    if True:
        kb, qb, vb = [k0, k1], [q0, q1], [v0, v1]
        sps, o2 = [s0, s1], [o20, o21, o22, o23]
        tps, ao, at = [tq0, tq1], [a0, a1], [at0, at1]
        c0 = {s: s.n for s in sems}
        neg_ap = negm_t[:, 0:1] if negm_imm is not None else g["negm"][:, 0:1]

        @blk.sync
        def _(sync):
            for h in range(H):
                kv = h // grp
                if h >= 2:
                    sync.wait_ge(pvd.h, c0[pvd] + 32 * (h - 1))
                ld.dma(sync.dma_start(out=kb[h % 2][:, :],
                                      in_=kT[kv * 128:(kv + 1) * 128, :]))
                ld.dma(sync.dma_start(out=qb[h % 2][:, :],
                                      in_=qT[h * 128:(h + 1) * 128, :]))
                ld.dma(sync.dma_start(
                    out=vb[h % 2][:, :, 0:128],
                    in_=vtok.rearrange("(c p) o -> p c o", p=128)[:, :, kv * 128:(kv + 1) * 128]))

        @blk.tensor
        def _(tensor):
            tensor.wait_ge(vrdy.h, c0[vrdy] + 1)
            for ent in sched:
                h, T, ch = ent["h"], ent["T"], ent["ch"]
                nch = len(ch)
                if T == 0:
                    tensor.wait_ge(ld.h, c0[ld] + 48 * (h + 1))

                def emit_S(idx):
                    c, qlo = ch[idx]
                    cs_g = ent["cs0"] + idx
                    if cs_g >= 2:
                        tensor.wait_ge(expd.h, c0[expd] + cs_g - 1)
                    smm.inc(tensor.matmul(
                        sps[cs_g % 2][:, qlo:512],
                        kb[h % 2][:, c * 128:(c + 1) * 128],
                        qb[h % 2][:, T * 512 + qlo:(T + 1) * 512],
                        start=True, stop=True))

                def emit_PV(idx):
                    c, qlo = ch[idx]
                    cs_g = ent["cs0"] + idx
                    tensor.wait_ge(expd.h, c0[expd] + cs_g + 1)
                    if c == 0 and ent["base"] >= 4:
                        tensor.wait_ge(nrm.h, c0[nrm] + ent["base"])
                    for jj in range(max(0, c - 4 * T), 4):
                        last = (c == 4 * T + jj)
                        m = tensor.matmul(o2[jj][:, 0:129],
                                          pt_sb[:, c, jj * 128:(jj + 1) * 128],
                                          vb[h % 2][:, c, 0:129],
                                          start=(c == 0), stop=last)
                        if last:
                            pvd.inc(m)

                emit_S(0)
                for idx in range(nch):
                    if idx + 1 < nch:
                        emit_S(idx + 1)
                    emit_PV(idx)
                for jj in range(4):
                    pq = ent["base"] + jj
                    tensor.wait_ge(nrm.h, c0[nrm] + pq + 1)
                    if pq >= 2:
                        tensor.wait_ge(tev.h, c0[tev] + pq - 1)
                    tpt.inc(tensor.transpose(tps[pq % 2][:, :], ao[pq % 2][:, :],
                                             g["ident"][:, :]))

        @blk.vector
        def _(vector):
            vr = [0]
            vector.memset(v0[:, :, 128:129], 1.0)
            vector.memset(v1[:, :, 128:129], 1.0)
            if negm_imm is not None:
                vector.memset(negm_t[:, 0:1], float(negm_imm))
            vrdy.inc(vector.engine_nop())
            vrdy.n += 4 * H * NT512
            for ent in sched:
                h, T, ch = ent["h"], ent["T"], ent["ch"]
                mki = ent["mk0"]
                for idx, (c, qlo) in enumerate(ch):
                    if c * 128 >= T * 512:
                        cs_g = ent["cs0"] + idx
                        vector.wait_ge(smm.h, c0[smm] + cs_g + 1)
                        mskd.inc(vector.tensor_tensor(
                            sps[cs_g % 2][:, qlo:qlo + 128],
                            sps[cs_g % 2][:, qlo:qlo + 128],
                            g["tri"][:, :], ALU.add))
                        mki += 1
                for jj in range(4):
                    pq = ent["base"] + jj
                    vector.wait_ge(pvd.h, c0[pvd] + pq + 1)
                    if pq >= 2:
                        vector.wait_ge(tpt.h, c0[tpt] + pq - 1)
                    vector.reciprocal(linv[:, jj:jj + 1],
                                      o2[jj][:, 128:129]).then_inc(vrdy.h, 1)
                    vr[0] += 1
                    vector.wait_ge(vrdy.h, c0[vrdy] + 1 + vr[0])
                    nrm.inc(vector.tensor_scalar(ao[pq % 2][:, :], o2[jj][:, 0:128],
                                                 linv[:, jj:jj + 1], None,
                                                 op0=ALU.mult))

        @blk.scalar
        def _(scalar):
            scalar.wait_ge(vrdy.h, c0[vrdy] + 1)
            for ent in sched:
                h, T, ch = ent["h"], ent["T"], ent["ch"]
                mki = ent["mk0"]
                if ent["base"] >= 4:
                    scalar.wait_ge(pvd.h, c0[pvd] + ent["base"])
                for idx, (c, qlo) in enumerate(ch):
                    cs_g = ent["cs0"] + idx
                    scalar.wait_ge(smm.h, c0[smm] + cs_g + 1)
                    if c * 128 >= T * 512:
                        scalar.wait_ge(mskd.h, c0[mskd] + mki + 1)
                        mki += 1
                    expd.inc(scalar.activation(
                        pt_sb[:, c, qlo:512], sps[cs_g % 2][:, qlo:512],
                        AF.Exp, bias=neg_ap))
                tile = ent["tile"]
                for jj in range(4):
                    pq = ent["base"] + jj
                    scalar.wait_ge(tpt.h, c0[tpt] + pq + 1)
                    if tile >= 2 and jj == 0:
                        scalar.wait_ge(outst.h, c0[outst] + 16 * (tile - 1))
                    tev.inc(scalar.copy(at[tile % 2][:, jj * 128:(jj + 1) * 128],
                                        tps[pq % 2][:, :]))

        @blk.gpsimd
        def _(gpsimd):
            for ent in sched:
                h, T, tile = ent["h"], ent["T"], ent["tile"]
                gpsimd.wait_ge(tev.h, c0[tev] + 4 * (tile + 1))
                outst.dma(gpsimd.dma_start(
                    out=attnT_dst[h * 128:(h + 1) * 128, T * 512:(T + 1) * 512],
                    in_=at[tile % 2][:, :]))
            gpsimd.wait_ge(outst.h, outst.n)
    _pes.close()
    sp.put(*sems)


def p_ffn1(nc, pes, sp, g, t):
    iblocks = []
    c = 0
    while c < ICH:
        iblocks.append((c, min(8, ICH - c)))
        c += 8
    sems = [sp.get() for _ in range(7)]
    wl, xl, gmm, umm, sil, ev, out = sems
    with nc.Block("pffn1") as blk, \
         nc.sbuf_tensor("f1wg", [128, NCH, 1024], BF16) as wg_sb, \
         nc.sbuf_tensor("f1wu", [128, NCH, 1024], BF16) as wu_sb, \
         nc.sbuf_tensor("f1x0", [128, NCH, 512], BF16) as xa, \
         nc.sbuf_tensor("f1x1", [128, NCH, 512], BF16) as xb, \
         nc.psum_tensor("f1g0", [128, 512], F32) as g0, \
         nc.psum_tensor("f1g1", [128, 512], F32) as g1, \
         nc.psum_tensor("f1u0", [128, 512], F32) as u0, \
         nc.psum_tensor("f1u1", [128, 512], F32) as u1, \
         nc.sbuf_tensor("f1s", [128, 512], F32) as gs, \
         nc.sbuf_tensor("f1h0", [128, 512], BF16) as h0, \
         nc.sbuf_tensor("f1h1", [128, 512], BF16) as h1:
        xn_sb, gps, ups, hb = [xa, xb], [g0, g1], [u0, u1], [h0, h1]
        c0 = {s: s.n for s in sems}
        jobs = []
        for bi, (ic0, bn) in enumerate(iblocks):
            for tt in range(NT512):
                for icr in range(bn):
                    jobs.append((bi, ic0, bn, tt, icr))

        @blk.sync
        def _(sync):
            ji = 0
            for bi, (ic0, bn) in enumerate(iblocks):
                if bi >= 1:
                    prev_jobs = sum(b[1] for b in iblocks[:bi]) * NT512
                    sync.wait_ge(umm.h, c0[umm] + prev_jobs)
                for c in range(NCH):
                    wl.dma(sync.dma_start(
                        out=wg_sb[:, c, 0:bn * 128],
                        in_=t["wg_bf"][c * 128:(c + 1) * 128, ic0 * 128:(ic0 + bn) * 128]))
                    wl.dma(sync.dma_start(
                        out=wu_sb[:, c, 0:bn * 128],
                        in_=t["wu_bf"][c * 128:(c + 1) * 128, ic0 * 128:(ic0 + bn) * 128]))
                for tt in range(NT512):
                    gtile = bi * NT512 + tt
                    if gtile >= 2:
                        done_jobs = [j for j in range(len(jobs))
                                     if jobs[j][0] * NT512 + jobs[j][3] <= gtile - 2]
                        sync.wait_ge(umm.h, c0[umm] + len(done_jobs))
                    xl.dma(sync.dma_start(
                        out=xn_sb[gtile % 2][:, :, :],
                        in_=t["xnT"].rearrange("(c p) t -> p c t", p=128)[:, :, tt * 512:(tt + 1) * 512]))

        @blk.tensor
        def _(tensor):
            for i, (bi, ic0, bn, tt, icr) in enumerate(jobs):
                gtile = bi * NT512 + tt
                if icr == 0:
                    tensor.wait_ge(xl.h, c0[xl] + 16 * (gtile + 1))
                    tensor.wait_ge(wl.h, c0[wl] + 32 * (bi + 1))
                if i >= 2:
                    tensor.wait_ge(ev.h, c0[ev] + i - 1)
                for c in range(NCH):
                    m = tensor.matmul(gps[i % 2][:, :],
                                      wg_sb[:, c, icr * 128:(icr + 1) * 128],
                                      xn_sb[gtile % 2][:, c, :],
                                      start=(c == 0), stop=(c == NCH - 1))
                    if c == NCH - 1:
                        gmm.inc(m)
                for c in range(NCH):
                    m = tensor.matmul(ups[i % 2][:, :],
                                      wu_sb[:, c, icr * 128:(icr + 1) * 128],
                                      xn_sb[gtile % 2][:, c, :],
                                      start=(c == 0), stop=(c == NCH - 1))
                    if c == NCH - 1:
                        umm.inc(m)

        @blk.scalar
        def _(scalar):
            for i, job in enumerate(jobs):
                scalar.wait_ge(gmm.h, c0[gmm] + i + 1)
                if i >= 1:
                    scalar.wait_ge(ev.h, c0[ev] + i)
                sil.inc(scalar.activation(gs[:, :], gps[i % 2][:, :], AF.Silu))

        @blk.vector
        def _(vector):
            for i, job in enumerate(jobs):
                vector.wait_ge(sil.h, c0[sil] + i + 1)
                vector.wait_ge(umm.h, c0[umm] + i + 1)
                if i >= 2:
                    vector.wait_ge(out.h, c0[out] + 16 * (i - 1))
                ev.inc(vector.tensor_tensor(hb[i % 2][:, :], gs[:, :],
                                            ups[i % 2][:, :], ALU.mult))

        @blk.gpsimd
        def _(gpsimd):
            for i, (bi, ic0, bn, tt, icr) in enumerate(jobs):
                gpsimd.wait_ge(ev.h, c0[ev] + i + 1)
                ic = ic0 + icr
                out.dma(gpsimd.dma_start(
                    out=t["hT"][ic * 128:(ic + 1) * 128, tt * 512:(tt + 1) * 512],
                    in_=hb[i % 2][:, :]))
            gpsimd.wait_ge(out.h, out.n)
    sp.put(*sems)


def p_combine(nc, pes, sp, g, name, parts, resid_src, bias_bc_key, dst):
    sems = [sp.get() for _ in range(5)]
    wl, al, mm, ev, out = sems
    with nc.Block(name) as blk, \
         nc.sbuf_tensor(name + "w0", [128, parts[0][2], 512], BF16) as w0, \
         nc.sbuf_tensor(name + "w1", [128, parts[1][2], 512], BF16) as w1, \
         nc.sbuf_tensor(name + "a0", [128, parts[0][2], 128], BF16) as a00, \
         nc.sbuf_tensor(name + "a1", [128, parts[0][2], 128], BF16) as a01, \
         nc.sbuf_tensor(name + "b0", [128, parts[1][2], 128], BF16) as a10, \
         nc.sbuf_tensor(name + "b1", [128, parts[1][2], 128], BF16) as a11, \
         nc.sbuf_tensor(name + "r0", [128, 512], F32) as r0, \
         nc.sbuf_tensor(name + "r1", [128, 512], F32) as r1, \
         nc.psum_tensor(name + "p0", [128, 512], F32) as p0, \
         nc.psum_tensor(name + "p1", [128, 512], F32) as p1, \
         nc.sbuf_tensor(name + "o0", [128, 512], F32) as ob0, \
         nc.sbuf_tensor(name + "o1", [128, 512], F32) as ob1:
        w_sb = [w0, w1]
        act_sb = [[a00, a01], [a10, a11]]
        rb, ps, ob = [r0, r1], [p0, p1], [ob0, ob1]
        c0 = {s: s.n for s in sems}
        bias = g[bias_bc_key]
        n_ch_tot = sum(p[2] for p in parts)

        @blk.sync
        def _(sync):
            for oh in range(4):
                if oh >= 1:
                    sync.wait_ge(mm.h, c0[mm] + oh * NT128)
                for pi, (actT, w_bf, nch, trw) in enumerate(parts):
                    for c in range(nch):
                        if trw:
                            wl.dma(sync.dma_start(
                                out=w_sb[pi][:, c, :],
                                in_=w_bf[oh * 512:(oh + 1) * 512, c * 128:(c + 1) * 128],
                                transpose=True))
                        else:
                            wl.dma(sync.dma_start(
                                out=w_sb[pi][:, c, :],
                                in_=w_bf[c * 128:(c + 1) * 128, oh * 512:(oh + 1) * 512]))
                for tt in range(NT128):
                    i = oh * NT128 + tt
                    if i >= 2:
                        sync.wait_ge(mm.h, c0[mm] + i - 1)
                    for pi, (actT, w_bf, nch, trw) in enumerate(parts):
                        al.dma(sync.dma_start(
                            out=act_sb[pi][i % 2][:, :, :],
                            in_=actT.rearrange("(c p) t -> p c t", p=128)[:, :, tt * 128:(tt + 1) * 128]))
                    al.dma(sync.dma_start(
                        out=rb[i % 2][:, :],
                        in_=resid_src[tt * 128:(tt + 1) * 128, oh * 512:(oh + 1) * 512]))

        @blk.tensor
        def _(tensor):
            for oh in range(4):
                tensor.wait_ge(wl.h, c0[wl] + 16 * n_ch_tot * (oh + 1))
                for tt in range(NT128):
                    i = oh * NT128 + tt
                    tensor.wait_ge(al.h, c0[al] + 48 * (i + 1))
                    if i >= 2:
                        tensor.wait_ge(ev.h, c0[ev] + i - 1)
                    first = True
                    for pi, (actT, w_bf, nch, trw) in enumerate(parts):
                        for c in range(nch):
                            last = (pi == len(parts) - 1 and c == nch - 1)
                            m = tensor.matmul(ps[i % 2][:, :],
                                              act_sb[pi][i % 2][:, c, :],
                                              w_sb[pi][:, c, :],
                                              start=first, stop=last)
                            first = False
                            if last:
                                mm.inc(m)

        @blk.vector
        def _(vector):
            for oh in range(4):
                for tt in range(NT128):
                    i = oh * NT128 + tt
                    vector.wait_ge(mm.h, c0[mm] + i + 1)
                    if i >= 2:
                        vector.wait_ge(out.h, c0[out] + 16 * (i - 1))
                    vector.scalar_tensor_tensor(
                        ob[i % 2][:, :], ps[i % 2][:, :], 1.0, rb[i % 2][:, :],
                        op0=ALU.mult, op1=ALU.add)
                    ev.inc(vector.tensor_tensor(
                        ob[i % 2][:, :], ob[i % 2][:, :],
                        bias[:, oh * 512:(oh + 1) * 512], ALU.add))

        @blk.gpsimd
        def _(gpsimd):
            for oh in range(4):
                for tt in range(NT128):
                    i = oh * NT128 + tt
                    gpsimd.wait_ge(ev.h, c0[ev] + i + 1)
                    out.dma(gpsimd.dma_start(
                        out=dst[tt * 128:(tt + 1) * 128, oh * 512:(oh + 1) * 512],
                        in_=ob[i % 2][:, :]))
            gpsimd.wait_ge(out.h, out.n)
    sp.put(*sems)


def p_proj_l1qk(nc, pes, sp, g, name, w_bf, xnT_src, out_dst, o_chunks, sqrt_scale):
    ocols = o_chunks * 128
    sems = [sp.get() for _ in range(10)]
    wl, xl, mm, rpd, sqd, summ, rcd, rsd, bcm, apd = sems
    out = sp.get()
    tiles = [(tt, oc) for tt in range(NT512) for oc in range(o_chunks)]
    n = len(tiles)
    with nc.Block(name) as blk, \
         nc.sbuf_tensor(name + "w", [128, NCH, ocols], BF16) as w_sb, \
         nc.sbuf_tensor(name + "xa", [128, NCH, 512], BF16) as xa, \
         nc.sbuf_tensor(name + "xb", [128, NCH, 512], BF16) as xbt, \
         nc.psum_tensor(name + "p0", [128, 512], F32) as ps0, \
         nc.psum_tensor(name + "p1", [128, 512], F32) as ps1, \
         nc.sbuf_tensor(name + "r0", [128, 512], F32) as ro0, \
         nc.sbuf_tensor(name + "r1", [128, 512], F32) as ro1, \
         nc.sbuf_tensor(name + "rt", [128, 512], F32) as rtmp, \
         nc.sbuf_tensor(name + "rq", [128, 512], F32) as rsq, \
         nc.psum_tensor(name + "rp", [1, 512], F32) as rps, \
         nc.sbuf_tensor(name + "ra", [1, 512], F32) as rrow_a, \
         nc.sbuf_tensor(name + "rb", [1, 512], F32) as rrow_b, \
         nc.sbuf_tensor(name + "rc", [1, 512], F32) as rrow_c, \
         nc.psum_tensor(name + "rbc", [128, 512], F32) as rbc, \
         nc.sbuf_tensor(name + "e0", [128, 512], BF16) as e0, \
         nc.sbuf_tensor(name + "e1", [128, 512], BF16) as e1:
        xn_sb, ps, ro, eb = [xa, xbt], [ps0, ps1], [ro0, ro1], [e0, e1]
        c0 = {s: s.n for s in sems + [out]}

        @blk.sync
        def _(sync):
            for c in range(NCH):
                wl.dma(sync.dma_start(out=w_sb[:, c, :],
                                      in_=w_bf[c * 128:(c + 1) * 128, :]))
            for tt in range(NT512):
                if tt >= 2:
                    sync.wait_ge(mm.h, c0[mm] + (tt - 1) * o_chunks)
                xl.dma(sync.dma_start(
                    out=xn_sb[tt % 2][:, :, :],
                    in_=xnT_src.rearrange("(c p) t -> p c t", p=128)[:, :, tt * 512:(tt + 1) * 512]))

        @blk.tensor
        def _(tensor):
            tensor.wait_ge(wl.h, wl.n)

            def aux(i):
                tensor.wait_ge(sqd.h, c0[sqd] + i + 1)
                if i >= 1:
                    tensor.wait_ge(rcd.h, c0[rcd] + i)
                summ.inc(tensor.matmul(rps[0:1, :], g["ones_f"][:, 0:1], rsq[:, :],
                                       start=True, stop=True))
                tensor.wait_ge(rsd.h, c0[rsd] + i + 1)
                if i >= 1:
                    tensor.wait_ge(apd.h, c0[apd] + i)
                bcm.inc(tensor.matmul(rbc[:, :], g["ones_r"][0:1, :], rrow_c[0:1, :],
                                      start=True, stop=True))

            for i, (tt, oc) in enumerate(tiles):
                tensor.wait_ge(xl.h, c0[xl] + 16 * (tt + 1))
                if i >= 2:
                    tensor.wait_ge(rpd.h, c0[rpd] + i - 1)
                for c in range(NCH):
                    m = tensor.matmul(ps[i % 2][:, :],
                                      w_sb[:, c, oc * 128:(oc + 1) * 128],
                                      xn_sb[tt % 2][:, c, :],
                                      start=(c == 0), stop=(c == NCH - 1))
                    if c == NCH - 1:
                        mm.inc(m)
                if i >= 1:
                    aux(i - 1)
            aux(n - 1)

        @blk.vector
        def _(vector):
            def rope(i, tt):
                vector.wait_ge(mm.h, c0[mm] + i + 1)
                if i >= 1:
                    vector.wait_ge(apd.h, c0[apd] + i - 1)
                p, r = ps[i % 2], ro[i % 2]
                c_ap = g["rcos"][:, tt * 512:(tt + 1) * 512]
                s_ap = g["rsin"][:, tt * 512:(tt + 1) * 512]
                vector.tensor_tensor(rtmp[0:64, :], p[64:128, :], s_ap, ALU.mult)
                vector.tensor_tensor(r[0:64, :], p[0:64, :], c_ap, ALU.mult)
                vector.tensor_add(r[0:64, :], r[0:64, :], rtmp[0:64, :])
                vector.tensor_tensor(rtmp[64:128, :], p[0:64, :], s_ap, ALU.mult)
                vector.tensor_tensor(r[64:128, :], p[64:128, :], c_ap, ALU.mult)
                rpd.inc(vector.tensor_sub(r[64:128, :], r[64:128, :], rtmp[64:128, :]))

            def rcalc(i):
                vector.wait_ge(summ.h, c0[summ] + i + 1)
                vector.tensor_scalar(rrow_a[0:1, :], rps[0:1, :], 1.0 / DH, 1e-6,
                                     op0=ALU.mult, op1=ALU.add)
                rcd.inc(vector.reciprocal(rrow_b[0:1, :], rrow_a[0:1, :]))

            def apply(i):
                vector.wait_ge(bcm.h, c0[bcm] + i + 1)
                if i >= 2:
                    vector.wait_ge(out.h, c0[out] + 16 * (i - 1))
                apd.inc(vector.tensor_tensor(eb[i % 2][:, :], ro[i % 2][:, :],
                                             rbc[:, :], ALU.mult))

            for i, (tt, oc) in enumerate(tiles):
                rope(i, tt)
                if i >= 1:
                    rcalc(i - 1)
                    apply(i - 1)
            rcalc(n - 1)
            apply(n - 1)

        @blk.scalar
        def _(scalar):
            def sq(i):
                scalar.wait_ge(rpd.h, c0[rpd] + i + 1)
                if i >= 1:
                    scalar.wait_ge(summ.h, c0[summ] + i)
                sqd.inc(scalar.activation(rsq[:, :], ro[i % 2][:, :], AF.Square))

            def sqrt(i):
                scalar.wait_ge(rcd.h, c0[rcd] + i + 1)
                if i >= 1:
                    scalar.wait_ge(bcm.h, c0[bcm] + i)
                rsd.inc(scalar.activation(rrow_c[0:1, :], rrow_b[0:1, :], AF.Sqrt,
                                          scale=float(sqrt_scale)))

            for i in range(n):
                sq(i)
                if i >= 1:
                    sqrt(i - 1)
            sqrt(n - 1)

        @blk.gpsimd
        def _(gpsimd):
            for i, (tt, oc) in enumerate(tiles):
                gpsimd.wait_ge(apd.h, c0[apd] + i + 1)
                out.dma(gpsimd.dma_start(
                    out=out_dst[oc * 128:(oc + 1) * 128, tt * 512:(tt + 1) * 512],
                    in_=eb[i % 2][:, :]))
            gpsimd.wait_ge(out.h, out.n)
    sp.put(*sems, out)


PHASES = ["consts", "cast", "wgen", "ln0", "q0", "k0", "negm", "v0", "attn0",
          "ffn1", "comb0", "ln1", "q1", "k1", "v1", "attn1", "comb1"]


def build_nc(upto=None, dbg=()):
    nc = bass.Bass("TRN2", target_bir_lowering=False)
    from contextlib import ExitStack
    es = ExitStack()

    t = {}
    t["x"] = nc.dram_tensor("x", [S, D], F32, kind="ExternalInput").ap()
    for k, cols in [("wq0", D), ("wk0", D), ("wv0", D), ("wo0", D),
                    ("wq1", D), ("wk1", 512), ("wv1", 512),
                    ("wo1", D), ("w1ffn", D)]:
        t[k + "s"] = nc.dram_tensor(k + "s", [256, cols], F32, kind="ExternalInput").ap()
    for k, shp in [("bq0", [128, 16]), ("bk0", [128, 16]), ("bv0", [128, 16]),
                   ("bo0", [1, D]), ("b1", [1, D]), ("lut", [1, 16]),
                   ("tri", [128, 128]), ("ropecos", [64, S]), ("ropesin", [64, S])]:
        t[k] = nc.dram_tensor(k, shp, F32, kind="ExternalInput").ap()
    t["ident"] = nc.dram_tensor("ident", [128, 128], BF16, kind="ExternalInput").ap()
    t["gidxs"] = nc.dram_tensor("gidxs", [256, IFF], I32, kind="ExternalInput").ap()
    t["uidxs"] = nc.dram_tensor("uidxs", [256, IFF], I32, kind="ExternalInput").ap()
    t["didxs"] = nc.dram_tensor("didxs", [688, D], I32, kind="ExternalInput").ap()
    t["y"] = nc.dram_tensor("y", [S, D], F32, kind="ExternalOutput").ap()

    scr = [("wq0_bf", [D, D], BF16), ("wk0_bf", [D, D], BF16), ("wv0_bf", [D, D], BF16),
           ("wo0_bf", [D, D], BF16), ("wq1_bf", [D, D], BF16), ("wk1_bf", [D, 512], BF16),
           ("wv1_bf", [D, 512], BF16), ("wo1_bf", [D, D], BF16), ("bw1_bf", [D, D], BF16),
           ("wg_bf", [D, IFF], BF16), ("wu_bf", [D, IFF], BF16), ("wd_bf", [IFF, D], BF16),
           ("xnT", [D, S], BF16), ("xnT2", [D, S], BF16),
           ("qTs", [D, S], BF16), ("kTs", [D, S], BF16),
           ("qTs2", [D, S], BF16), ("kTs2", [512, S], BF16),
           ("vtok", [S, D], BF16), ("vtok2", [S, 512], BF16),
           ("attnT", [D, S], BF16), ("attnT2", [D, S], BF16),
           ("hT", [IFF, S], BF16), ("x1", [S, D], F32)]
    for k in CAST_KEYS:
        cols = 512 if k in ("wk1", "wv1") else D
        scr.append((k + "_loc", [256, cols], BF16))
    scr += [("bw1_loc", [256, D], BF16), ("wg_loc", [256, IFF], BF16),
            ("wu_loc", [256, IFF], BF16), ("wd_loc", [688, D], BF16)]
    for k, shp, dt in scr:
        t[k] = nc.dram_tensor(k, shp, dt, kind="Internal").ap()

    g = {}
    g["ident"] = es.enter_context(nc.sbuf_tensor("g_id", [128, 128], BF16))
    g["tri"] = es.enter_context(nc.sbuf_tensor("g_tri", [128, 128], F32))
    g["rcos"] = es.enter_context(nc.sbuf_tensor("g_rc", [64, S], F32))
    g["rsin"] = es.enter_context(nc.sbuf_tensor("g_rs", [64, S], F32))
    g["bq"] = es.enter_context(nc.sbuf_tensor("g_bq", [128, 16], F32))
    g["bk"] = es.enter_context(nc.sbuf_tensor("g_bk", [128, 16], F32))
    g["bv"] = es.enter_context(nc.sbuf_tensor("g_bv", [128, 16], F32))
    g["bo_r"] = es.enter_context(nc.sbuf_tensor("g_bor", [1, D], F32))
    g["b1_r"] = es.enter_context(nc.sbuf_tensor("g_b1r", [1, D], F32))
    g["bo_bc"] = es.enter_context(nc.sbuf_tensor("g_bobc", [128, D], F32))
    g["b1_bc"] = es.enter_context(nc.sbuf_tensor("g_b1bc", [128, D], F32))
    g["ones_f"] = es.enter_context(nc.sbuf_tensor("g_onf", [128, 1], F32))
    g["ones_r"] = es.enter_context(nc.sbuf_tensor("g_onr", [1, 128], F32))
    g["normcol"] = es.enter_context(nc.sbuf_tensor("g_nc", [1, 256], F32))
    g["negm"] = es.enter_context(nc.sbuf_tensor("g_ngm", [128, 1], F32))

    sp = SemPool(es, nc)
    stop = PHASES.index(upto) if upto else len(PHASES) - 1

    def run(label, fn):
        if PHASES.index(label) <= stop:
            fn()

    run("consts", lambda: p_consts(nc, None, sp, t, g))
    run("cast", lambda: p_cast(nc, None, sp, t))
    run("wgen", lambda: p_wgen(nc, None, sp, t))
    run("ln0", lambda: p_layernorm(nc, None, sp, g, "ln0", t["x"], t["xnT"]))
    run("q0", lambda: p_proj(nc, None, sp, g, "q0", t["wq0_bf"], t["xnT"], t["qTs"],
                             16, True, bias_col="bq", scale=L0_SCALE, norm_base=0))
    run("k0", lambda: p_proj(nc, None, sp, g, "k0", t["wk0_bf"], t["xnT"], t["kTs"],
                             16, True, bias_col="bk", scale=L0_SCALE, norm_base=128))
    run("negm", lambda: p_negm(nc, None, sp, g))
    run("v0", lambda: p_proj(nc, None, sp, g, "v0", t["wv0_bf"], t["xnT"], t["vtok"],
                             16, True, bias_col="bv", token_major=True))
    run("attn0", lambda: p_attn(nc, None, sp, g, "at0", t["qTs"], t["kTs"], t["vtok"],
                                t["attnT"], H, None))
    run("ffn1", lambda: p_ffn1(nc, None, sp, g, t))
    run("comb0", lambda: p_combine(nc, None, sp, g, "cb0",
                                   [(t["attnT"], t["wo0_bf"], NCH, True),
                                    (t["hT"], t["wd_bf"], ICH, False)],
                                   t["x"], "bo_bc", t["x1"]))
    run("ln1", lambda: p_layernorm(nc, None, sp, g, "ln1", t["x1"], t["xnT2"]))
    run("q1", lambda: p_proj_l1qk(nc, None, sp, g, "q1", t["wq1_bf"], t["xnT2"],
                                  t["qTs2"], 16, 1.0))
    run("k1", lambda: p_proj_l1qk(nc, None, sp, g, "k1", t["wk1_bf"], t["xnT2"],
                                  t["kTs2"], 4, L1_KSCALE2))
    run("v1", lambda: p_proj(nc, None, sp, g, "v1", t["wv1_bf"], t["xnT2"], t["vtok2"],
                             4, False, token_major=True))
    run("attn1", lambda: p_attn(nc, None, sp, g, "at1", t["qTs2"], t["kTs2"], t["vtok2"],
                                t["attnT2"], 4, L1_NEG_BOUND))
    run("comb1", lambda: p_combine(nc, None, sp, g, "cb1",
                                   [(t["attnT2"], t["wo1_bf"], NCH, False),
                                    (t["xnT2"], t["bw1_bf"], NCH, True)],
                                   t["x1"], "b1_bc", t["y"]))

    if dbg:
        dsem = sp.get()
        with nc.Block("pdbg") as blk:
            @blk.gpsimd
            def _(gpsimd):
                for nm in dbg:
                    src = t[nm]
                    dt = {"x1": F32}.get(nm, BF16)
                    od = nc.dram_tensor("dbg_" + nm, list(src.shape), dt,
                                        kind="ExternalOutput").ap()
                    dsem.dma(gpsimd.dma_start(out=od[:, :], in_=src[:, :]))
                gpsimd.wait_ge(dsem.h, dsem.n)
        sp.put(dsem)

    es.close()
    return nc


def _host_consts():
    c = {}
    c["ident"] = np.eye(128, dtype=ml_dtypes.bfloat16)
    kk = np.arange(128)[:, None]
    qq = np.arange(128)[None, :]
    c["tri"] = np.where(qq >= kk, np.float32(0.0), np.float32(MASK)).astype(np.float32)
    half = 64
    inv = (np.float32(1.0) / (np.float32(100000.0)
           ** (np.arange(half, dtype=np.float32) / np.float32(half)))).astype(np.float32)
    freqs = (np.arange(S, dtype=np.float32)[:, None] * inv[None, :]).astype(np.float32)
    c["ropecos"] = np.ascontiguousarray(np.cos(freqs).astype(np.float32).T)
    c["ropesin"] = np.ascontiguousarray(np.sin(freqs).astype(np.float32).T)
    return c


def host_in_maps(inputs):
    ii = {k: np.asarray(v) for k, v in inputs.items()}
    X = ii["L0_attn_X"].astype(np.float32, copy=False)

    common = dict(_host_consts())
    for k, src in [("bq0", "L0_attn_bq"), ("bk0", "L0_attn_bk"), ("bv0", "L0_attn_bv")]:
        common[k] = np.ascontiguousarray(ii[src].astype(np.float32).reshape(16, 128).T)
    common["bo0"] = ii["L0_attn_bo"].astype(np.float32).reshape(1, D)
    common["lut"] = ii["L0_ffn_lut"].astype(np.float32).reshape(1, 16)
    common["b1"] = ii["L1_ffn_bias"].astype(np.float32).reshape(1, D)
    for k in list(common):
        common[k] = np.ascontiguousarray(np.asarray(common[k]))

    full = {"wq0": ii["L0_attn_Wq"], "wk0": ii["L0_attn_Wk"],
            "wv0": ii["L0_attn_Wv"], "wo0": ii["L0_attn_Wo"],
            "wq1": ii["L1_attn_wq"], "wk1": ii["L1_attn_wk"],
            "wv1": ii["L1_attn_wv"], "wo1": ii["L1_attn_wo"],
            "w1ffn": ii["L1_ffn_weight"]}
    full = {k: np.asarray(v, np.float32) for k, v in full.items()}
    gT = np.ascontiguousarray(ii["L0_ffn_gate_idx"].astype(np.int32).T)
    uT = np.ascontiguousarray(ii["L0_ffn_up_idx"].astype(np.int32).T)
    dT = np.ascontiguousarray(ii["L0_ffn_down_idx"].astype(np.int32).T)

    in_maps = []
    for b in range(8):
        m = dict(common, x=np.ascontiguousarray(X[b]))
        for k, v in full.items():
            m[k + "s"] = np.ascontiguousarray(v[b * 256:(b + 1) * 256])
        m["gidxs"] = np.ascontiguousarray(gT[b * 256:(b + 1) * 256])
        m["uidxs"] = np.ascontiguousarray(uT[b * 256:(b + 1) * 256])
        m["didxs"] = np.ascontiguousarray(dT[b * 688:(b + 1) * 688])
        in_maps.append(m)
    return in_maps


def kernel(**inputs):
    in_maps = host_in_maps(inputs)
    nc = build_nc()
    res = run_bass_kernel_spmd(nc, in_maps, core_ids=list(range(8)))
    return np.stack([np.asarray(res.results[b]["y"], dtype=np.float32)
                     for b in range(8)], axis=0)


# version 03
# speedup vs baseline: 1.0245x; 1.0245x over previous
import numpy as np
import ml_dtypes
import concourse.bass as bass
import concourse.mybir as mybir
from concourse.bass_utils import run_bass_kernel_spmd

F32 = mybir.dt.float32
BF16 = mybir.dt.bfloat16
I32 = mybir.dt.int32
AF = mybir.ActivationFunctionType
ALU = mybir.AluOpType
AX = mybir.AxisListType.X

S = 4096
D = 2048
H = 16
DH = 128
IFF = 5504
NCH = D // 128
ICH = IFF // 128
NT128 = S // 128
NT512 = S // 512
L0_SCALE = float(np.float32(1.0) / np.sqrt(np.sqrt(np.float32(DH), dtype=np.float32)))
L1_KSCALE2 = 1.0 / DH
MASK = -10000.0
L1_NEG_BOUND = -12.0


class Sem:
    def __init__(self, es, nc, name):
        self.h = es.enter_context(nc.semaphore(name))
        self.n = 0

    def dma(self, instr):
        self.n += 16
        instr.then_inc(self.h, 16)
        return self.n

    def inc(self, instr):
        self.n += 1
        instr.then_inc(self.h, 1)
        return self.n


class SemPool:
    def __init__(self, es, nc):
        self.es, self.nc = es, nc
        self.free = []
        self.k = 0

    def get(self):
        if self.free:
            return self.free.pop(0)
        s = Sem(self.es, self.nc, f"sp{self.k}")
        self.k += 1
        return s

    def put(self, *sems):
        self.free.extend(sems)


CAST_KEYS = ("wq0", "wk0", "wv0", "wo0", "wq1", "wk1", "wv1", "wo1")


def p_cast(nc, pes, sp, t):
    jobs = []
    off = 0
    for key in CAST_KEYS + ("w1ffn",):
        srcT = t[(key + "s") if key != "w1ffn" else "w1ffns"]
        cols = srcT.shape[1]
        dst = t["cast_loc"][:, off:off + cols]
        for r in range(2):
            jobs.append((srcT, dst, cols, r, 128, key == "w1ffn"))
        off += cols
    gathers = [(t["cast_loc"], t["cast_full"])]

    ld, cv, st = sp.get(), sp.get(), sp.get()
    with nc.Block("pcast") as blk, \
         nc.sbuf_tensor("caf0", [128, 2048], F32) as f0, \
         nc.sbuf_tensor("caf1", [128, 2048], F32) as f1, \
         nc.sbuf_tensor("cab0", [128, 2048], BF16) as b0, \
         nc.sbuf_tensor("cab1", [128, 2048], BF16) as b1, \
         nc.sbuf_tensor("cacol", [128, 4], F32) as col, \
         nc.sbuf_tensor("casgn", [128, 2048], F32) as sgn:
        fb, ob = [f0, f1], [b0, b1]
        ld0, cv0, st0 = ld.n, cv.n, st.n
        cc = sp.get()
        cc0 = cc.n
        done_at = []
        acc = cv0
        for (_s, _d, _c, _r, _rw, _b) in jobs:
            acc += 3 if _b else 1
            done_at.append(acc)

        @blk.sync
        def _(sync):
            for i, (src, dst, cols, r, rw, _) in enumerate(jobs):
                if i >= 2:
                    sync.wait_ge(cv.h, done_at[i - 2])
                ld.dma(sync.dma_start(out=fb[i % 2][0:rw, 0:cols],
                                      in_=src[r * 128:r * 128 + rw, :]))

        @blk.vector
        def _(vector):
            for i, (src, dst, cols, r, rw, binz) in enumerate(jobs):
                vector.wait_ge(ld.h, ld0 + 16 * (i + 1))
                if i >= 2:
                    vector.wait_ge(st.h, st0 + 16 * (i - 1))
                if binz:
                    r_ = vector.tensor_reduce(col[:, 0:1], fb[i % 2][:, 0:cols], AX,
                                              ALU.add, apply_absolute_value=True)
                    cv.inc(r_)
                    vector.wait_ge(cv.h, cv.n)
                    t_ = vector.tensor_scalar(col[:, 2:3], col[:, 0:1], 1.0 / cols, None,
                                              op0=ALU.mult)
                    cv.inc(t_)
                    vector.wait_ge(cv.h, cv.n)
                    vector.tensor_scalar(sgn[:, 0:cols], fb[i % 2][:, 0:cols], 0.0, None,
                                         op0=ALU.is_ge)
                    vector.tensor_scalar(sgn[:, 0:cols], sgn[:, 0:cols], 2.0, -1.0,
                                         op0=ALU.mult, op1=ALU.add)
                    vector.tensor_scalar(ob[i % 2][:, 0:cols], sgn[:, 0:cols],
                                         col[:, 2:3], None, op0=ALU.mult)
                else:
                    vector.tensor_copy(ob[i % 2][:, 0:cols], fb[i % 2][:, 0:cols])
                cv.inc(vector.engine_nop())
                assert cv.n == done_at[i]

        @blk.gpsimd
        def _(gpsimd):
            for i, (src, dst, cols, r, rw, _) in enumerate(jobs):
                gpsimd.wait_ge(cv.h, done_at[i])
                st.dma(gpsimd.dma_start(out=dst[r * 128:r * 128 + rw, :],
                                        in_=ob[i % 2][0:rw, 0:cols]))
            gpsimd.wait_ge(st.h, st.n)
            for locT, fullT in gathers:
                cc.inc(gpsimd.collective_compute(
                    "AllGather", ALU.bypass, replica_groups=[list(range(8))],
                    ins=[locT.opt()], outs=[fullT.opt()]))
            gpsimd.wait_ge(cc.h, cc.n)
    sp.put(ld, cv, st, cc)


def p_wgen(nc, pes, sp, t):
    jobs = ([(t["gidxs"], t["wgu_loc"][:, 0:IFF], IFF, r, 128) for r in range(2)]
            + [(t["uidxs"], t["wgu_loc"][:, IFF:2 * IFF], IFF, r, 128) for r in range(2)]
            + [(t["didxs"], t["wd_loc"], 2048, r, (128 if r < 5 else 48))
               for r in range(6)])
    gathers = [(t["wgu_loc"], t["wgu_full"]), (t["wd_loc"], t["wd_bf"])]

    ld, pre, cv, st = sp.get(), sp.get(), sp.get(), sp.get()
    with nc.Block("pwgen") as blk, \
         nc.sbuf_tensor("wgi0", [128, IFF], I32) as i0, \
         nc.sbuf_tensor("wgi1", [128, IFF], I32) as i1, \
         nc.sbuf_tensor("wgf", [128, IFF], BF16) as fbuf, \
         nc.sbuf_tensor("wgt", [128, IFF], BF16) as term, \
         nc.sbuf_tensor("wga0", [128, IFF], BF16) as a0, \
         nc.sbuf_tensor("wga1", [128, IFF], BF16) as a1, \
         nc.sbuf_tensor("wgl", [1, 16], F32) as lut_s, \
         nc.sbuf_tensor("wgon", [1, 128], F32) as ones_r, \
         nc.psum_tensor("wglp", [128, 16], F32) as lut_ps, \
         nc.sbuf_tensor("wglb", [128, 16], F32) as lut_bc:
        ib, ab = [i0, i1], [a0, a1]
        cc = sp.get()
        ld0, pre0, cv0, st0, cc0 = ld.n, pre.n, cv.n, st.n, cc.n

        @blk.sync
        def _(sync):
            pre.dma(sync.dma_start(out=lut_s[0:1, :], in_=t["lut"][:, :]))
            for i, (src, dst, cols, r, rw) in enumerate(jobs):
                if i >= 2:
                    sync.wait_ge(cv.h, cv0 + i - 1)
                ld.dma(sync.dma_start(out=ib[i % 2][0:rw, 0:cols],
                                      in_=src[r * 128:r * 128 + rw, :]))

        @blk.tensor
        def _(tensor):
            tensor.wait_ge(pre.h, pre0 + 17)
            pre.n = pre0 + 17
            pre.inc(tensor.matmul(lut_ps[:, :], ones_r[0:1, :], lut_s[0:1, :],
                                  start=True, stop=True))
            pre.n += 1

        @blk.vector
        def _(vector):
            vector.memset(ones_r[0:1, :], 1.0)
            vector.engine_nop().then_inc(pre.h, 1)
            vector.wait_ge(pre.h, pre0 + 18)
            vector.tensor_copy(lut_bc[:, :], lut_ps[:, :]).then_inc(pre.h, 1)
            vector.wait_ge(pre.h, pre0 + 19)
            for i, (src, dst, cols, r, rw) in enumerate(jobs):
                vector.wait_ge(ld.h, ld0 + 16 * (i + 1))
                if i >= 2:
                    vector.wait_ge(st.h, st0 + 16 * (i - 1))
                a = ab[i % 2]
                vector.tensor_copy(fbuf[0:rw, 0:cols], ib[i % 2][0:rw, 0:cols])
                vector.tensor_scalar(a[0:rw, 0:cols], fbuf[0:rw, 0:cols], 0.0,
                                     lut_bc[0:rw, 0:1], op0=ALU.is_equal, op1=ALU.mult)
                for j in range(1, 16):
                    vector.tensor_scalar(term[0:rw, 0:cols], fbuf[0:rw, 0:cols], float(j),
                                         lut_bc[0:rw, j:j + 1], op0=ALU.is_equal, op1=ALU.mult)
                    vector.tensor_add(a[0:rw, 0:cols], a[0:rw, 0:cols], term[0:rw, 0:cols])
                cv.inc(vector.engine_nop())

        @blk.gpsimd
        def _(gpsimd):
            for i, (src, dst, cols, r, rw) in enumerate(jobs):
                gpsimd.wait_ge(cv.h, cv0 + i + 1)
                st.dma(gpsimd.dma_start(out=dst[r * 128:r * 128 + rw, :],
                                        in_=ab[i % 2][0:rw, 0:cols]))
            gpsimd.wait_ge(st.h, st.n)
            for locT, fullT in gathers:
                cc.inc(gpsimd.collective_compute(
                    "AllGather", ALU.bypass, replica_groups=[list(range(8))],
                    ins=[locT.opt()], outs=[fullT.opt()]))
            gpsimd.wait_ge(cc.h, cc.n)
    sp.put(ld, pre, cv, st, cc)


def p_consts(nc, pes, sp, t, g):
    ld, cv, ev = sp.get(), sp.get(), sp.get()
    with nc.Block("pconst") as blk, \
         nc.psum_tensor("cobc", [128, 512], F32) as bc_ps:
        ld0, cv0, ev0 = ld.n, cv.n, ev.n

        @blk.sync
        def _(sync):
            for dst, src in [(g["ident"], t["ident"]), (g["tri"], t["tri"]),
                             (g["bq"], t["bq0"]), (g["bk"], t["bk0"]),
                             (g["bv"], t["bv0"]), (g["bo_r"], t["bo0"]),
                             (g["b1_r"], t["b1"])]:
                ld.dma(sync.dma_start(out=dst[:, :], in_=src[:, :]))

        @blk.vector
        def _(vector):
            vector.memset(g["ones_f"][:, :], 1.0)
            vector.memset(g["ones_r"][0:1, :], 1.0)
            cv.inc(vector.engine_nop())

        @blk.tensor
        def _(tensor):
            tensor.wait_ge(ld.h, ld.n)
            tensor.wait_ge(cv.h, cv0 + 1)
            k = 0
            for row in (g["bo_r"], g["b1_r"]):
                for q in range(4):
                    if k >= 1:
                        tensor.wait_ge(ev.h, ev0 + k)
                    cv.inc(tensor.matmul(bc_ps[:, :], g["ones_r"][0:1, :],
                                         row[0:1, q * 512:(q + 1) * 512],
                                         start=True, stop=True))
                    k += 1

        @blk.scalar
        def _(scalar):
            k = 0
            for dstk in ("bo_bc", "b1_bc"):
                for q in range(4):
                    scalar.wait_ge(cv.h, cv0 + 2 + k)
                    ev.inc(scalar.copy(g[dstk][:, q * 512:(q + 1) * 512], bc_ps[:, :]))
                    k += 1
            scalar.wait_ge(ev.h, ev.n)
    sp.put(ld, cv, ev)


def p_layernorm(nc, pes, sp, g, name, x_src, xnT_dst):
    ld, sqs, sts, rsd, app, tp, ev, out, vch = [sp.get() for _ in range(9)]
    with nc.Block(name) as blk, \
         nc.sbuf_tensor(name + "x0", [128, D], F32) as x0, \
         nc.sbuf_tensor(name + "x1", [128, D], F32) as x1, \
         nc.sbuf_tensor(name + "sq", [128, D], F32) as sqb, \
         nc.sbuf_tensor(name + "xt", [128, D], F32) as xtmp, \
         nc.sbuf_tensor(name + "c", [128, 8], F32) as col, \
         nc.sbuf_tensor(name + "n0", [128, D], BF16) as n0, \
         nc.sbuf_tensor(name + "n1", [128, D], BF16) as n1, \
         nc.psum_tensor(name + "p0", [128, 128], BF16) as p0, \
         nc.psum_tensor(name + "p1", [128, 128], BF16) as p1, \
         nc.sbuf_tensor(name + "o0", [128, NCH, 128], BF16) as o0, \
         nc.sbuf_tensor(name + "o1", [128, NCH, 128], BF16) as o1:
        xb, xnb, tps, ob = [x0, x1], [n0, n1], [p0, p1], [o0, o1]
        c0 = {s: s.n for s in (ld, sqs, sts, rsd, app, tp, ev, out, vch)}
        vch.n += 6 * NT128

        @blk.sync
        def _(sync):
            for i in range(NT128):
                if i >= 2:
                    sync.wait_ge(app.h, c0[app] + i - 1)
                ld.dma(sync.dma_start(out=xb[i % 2][:, :],
                                      in_=x_src[i * 128:(i + 1) * 128, :]))

        @blk.scalar
        def _(scalar):
            for i in range(NT128):
                scalar.wait_ge(ld.h, c0[ld] + 16 * (i + 1))
                if i >= 1:
                    scalar.wait_ge(sts.h, c0[sts] + i)
                sqs.inc(scalar.activation(sqb[:, :], xb[i % 2][:, :], AF.Square))
                scalar.wait_ge(sts.h, c0[sts] + i + 1)
                rsd.inc(scalar.activation(col[:, 7:8], col[:, 6:7], AF.Sqrt))
                if i >= 1:
                    if i >= 3:
                        scalar.wait_ge(out.h, c0[out] + 16 * (i - 2))
                    for c in range(NCH):
                        k = (i - 1) * NCH + c
                        scalar.wait_ge(tp.h, c0[tp] + k + 1)
                        ev.inc(scalar.copy(ob[(i - 1) % 2][:, c, :], tps[c % 2][:, :]))
            scalar.wait_ge(out.h, c0[out] + 16 * (NT128 - 2))
            for c in range(NCH):
                k = (NT128 - 1) * NCH + c
                scalar.wait_ge(tp.h, c0[tp] + k + 1)
                ev.inc(scalar.copy(ob[(NT128 - 1) % 2][:, c, :], tps[c % 2][:, :]))

        @blk.vector
        def _(vector):
            vk = [c0[vch]]

            def step(instr):
                vk[0] += 1
                instr.then_inc(vch.h, 1)
                vector.wait_ge(vch.h, vk[0])
            for i in range(NT128):
                vector.wait_ge(sqs.h, c0[sqs] + i + 1)
                step(vector.tensor_reduce(col[:, 1:2], xb[i % 2][:, :], AX, ALU.add))
                step(vector.tensor_reduce(col[:, 0:1], sqb[:, :], AX, ALU.add))
                step(vector.tensor_scalar(col[:, 2:3], col[:, 1:2], -1.0 / D, None, op0=ALU.mult))
                step(vector.tensor_tensor(col[:, 3:4], col[:, 2:3], col[:, 2:3], ALU.mult))
                step(vector.tensor_scalar(col[:, 4:5], col[:, 0:1], 1.0 / D, 1e-5,
                                          op0=ALU.mult, op1=ALU.add))
                step(vector.tensor_tensor(col[:, 5:6], col[:, 4:5], col[:, 3:4], ALU.subtract))
                sts.inc(vector.reciprocal(col[:, 6:7], col[:, 5:6]))
                vector.wait_ge(rsd.h, c0[rsd] + i + 1)
                if i >= 2:
                    vector.wait_ge(tp.h, c0[tp] + NCH * (i - 1))
                vector.tensor_scalar(xtmp[:, :], xb[i % 2][:, :],
                                     col[:, 2:3], None, op0=ALU.add)
                app.inc(vector.tensor_scalar(xnb[i % 2][:, :], xtmp[:, :],
                                             col[:, 7:8], None, op0=ALU.mult))

        @blk.tensor
        def _(tensor):
            for i in range(NT128):
                tensor.wait_ge(app.h, c0[app] + i + 1)
                for c in range(NCH):
                    k = i * NCH + c
                    if k >= 2:
                        tensor.wait_ge(ev.h, c0[ev] + k - 1)
                    tp.inc(tensor.transpose(tps[c % 2][:, :],
                                            xnb[i % 2][:, c * 128:(c + 1) * 128],
                                            g["ident"][:, :]))

        @blk.gpsimd
        def _(gpsimd):
            for i in range(NT128):
                gpsimd.wait_ge(ev.h, c0[ev] + NCH * (i + 1))
                out.dma(gpsimd.dma_start(
                    out=xnT_dst.rearrange("(c p) t -> p c t", p=128)[:, :, i * 128:(i + 1) * 128],
                    in_=ob[i % 2][:, :, :]))
            gpsimd.wait_ge(out.h, out.n)
    sp.put(ld, sqs, sts, rsd, app, tp, ev, out, vch)


def p_proj(nc, pes, sp, g, name, w_bf, xnT_src, out_dst, o_chunks,
           transpose_w, bias_col=None, scale=None, norm_base=None,
           token_major=False):
    ocols = o_chunks * 128
    sems = [sp.get() for _ in range(9)]
    wl, xl, mm, evd, sq, nmm, nmax, tev, out = sems
    tpt = sp.get()
    tiles = [(tt, oc) for tt in range(NT512) for oc in range(o_chunks)]
    with nc.Block(name) as blk, \
         nc.sbuf_tensor(name + "w", [128, NCH, ocols], BF16) as w_sb, \
         nc.sbuf_tensor(name + "xa", [128, NCH, 512], BF16) as xa, \
         nc.sbuf_tensor(name + "xb", [128, NCH, 512], BF16) as xbt, \
         nc.psum_tensor(name + "p0", [128, 512], F32) as ps0, \
         nc.psum_tensor(name + "p1", [128, 512], F32) as ps1, \
         nc.sbuf_tensor(name + "e0", [128, 512], BF16) as e0, \
         nc.sbuf_tensor(name + "e1", [128, 512], BF16) as e1, \
         nc.sbuf_tensor(name + "nq", [128, 512], F32) as nsq, \
         nc.psum_tensor(name + "np", [1, 512], F32) as nps, \
         nc.psum_tensor(name + "t0", [128, 128], BF16) as tp0, \
         nc.psum_tensor(name + "t1", [128, 128], BF16) as tp1, \
         nc.sbuf_tensor(name + "b0", [128, 128], BF16) as tb0, \
         nc.sbuf_tensor(name + "b1", [128, 128], BF16) as tb1:
        xn_sb, ps, eb = [xa, xbt], [ps0, ps1], [e0, e1]
        tps, tb = [tp0, tp1], [tb0, tb1]
        c0 = {s: s.n for s in sems + [tpt]}
        bias = g[bias_col] if bias_col else None

        @blk.sync
        def _(sync):
            for c in range(NCH):
                if transpose_w:
                    wl.dma(sync.dma_start(out=w_sb[:, c, :],
                                          in_=w_bf[:, c * 128:(c + 1) * 128],
                                          transpose=True))
                else:
                    wl.dma(sync.dma_start(out=w_sb[:, c, :],
                                          in_=w_bf[c * 128:(c + 1) * 128, :]))
            for tt in range(NT512):
                if tt >= 2:
                    sync.wait_ge(mm.h, c0[mm] + (tt - 1) * o_chunks)
                xl.dma(sync.dma_start(
                    out=xn_sb[tt % 2][:, :, :],
                    in_=xnT_src.rearrange("(c p) t -> p c t", p=128)[:, :, tt * 512:(tt + 1) * 512]))

        @blk.tensor
        def _(tensor):
            tensor.wait_ge(wl.h, wl.n)
            for i, (tt, oc) in enumerate(tiles):
                tensor.wait_ge(xl.h, c0[xl] + 16 * (tt + 1))
                if i >= 2:
                    tensor.wait_ge(evd.h, c0[evd] + i - 1)
                for c in range(NCH):
                    m = tensor.matmul(ps[i % 2][:, :],
                                      w_sb[:, c, oc * 128:(oc + 1) * 128],
                                      xn_sb[tt % 2][:, c, :],
                                      start=(c == 0), stop=(c == NCH - 1))
                    if c == NCH - 1:
                        mm.inc(m)
                if norm_base is not None and i >= 1:
                    tensor.wait_ge(sq.h, c0[sq] + i)
                    if i >= 2:
                        tensor.wait_ge(nmax.h, c0[nmax] + i - 1)
                    nmm.inc(tensor.matmul(nps[0:1, :], g["ones_f"][:, 0:1], nsq[:, :],
                                          start=True, stop=True))
                if token_major and i >= 1:
                    tensor.wait_ge(evd.h, c0[evd] + i)
                    for q in range(4):
                        k = (i - 1) * 4 + q
                        if k >= 2:
                            tensor.wait_ge(tev.h, c0[tev] + k - 1)
                        tpt.inc(tensor.transpose(tps[q % 2][:, :],
                                                 eb[(i - 1) % 2][:, q * 128:(q + 1) * 128],
                                                 g["ident"][:, :]))
            if norm_base is not None:
                i = len(tiles)
                tensor.wait_ge(sq.h, c0[sq] + i)
                tensor.wait_ge(nmax.h, c0[nmax] + i - 1)
                nmm.inc(tensor.matmul(nps[0:1, :], g["ones_f"][:, 0:1], nsq[:, :],
                                      start=True, stop=True))
            if token_major:
                i = len(tiles)
                for q in range(4):
                    k = (i - 1) * 4 + q
                    tensor.wait_ge(tev.h, c0[tev] + k - 1)
                    tpt.inc(tensor.transpose(tps[q % 2][:, :],
                                             eb[(i - 1) % 2][:, q * 128:(q + 1) * 128],
                                             g["ident"][:, :]))

        @blk.vector
        def _(vector):
            for i, (tt, oc) in enumerate(tiles):
                vector.wait_ge(mm.h, c0[mm] + i + 1)
                if not token_major and i >= 2:
                    vector.wait_ge(out.h, c0[out] + 16 * (i - 1))
                if norm_base is not None and i >= 2:
                    vector.wait_ge(sq.h, c0[sq] + i - 1)
                if token_major and i >= 2:
                    vector.wait_ge(tpt.h, c0[tpt] + 4 * (i - 1))
                if bias is not None and scale is not None:
                    evd.inc(vector.tensor_scalar(eb[i % 2][:, :], ps[i % 2][:, :],
                                                 bias[:, oc:oc + 1], float(scale),
                                                 op0=ALU.add, op1=ALU.mult))
                elif bias is not None:
                    evd.inc(vector.tensor_scalar(eb[i % 2][:, :], ps[i % 2][:, :],
                                                 bias[:, oc:oc + 1], None, op0=ALU.add))
                else:
                    evd.inc(vector.tensor_copy(eb[i % 2][:, :], ps[i % 2][:, :]))
                if norm_base is not None and i >= 1:
                    vector.wait_ge(nmm.h, c0[nmm] + i)
                    slot = norm_base + i - 1
                    nmax.inc(vector.tensor_reduce(g["normcol"][0:1, slot:slot + 1],
                                                  nps[0:1, :], AX, ALU.max))
            if norm_base is not None:
                i = len(tiles)
                vector.wait_ge(nmm.h, c0[nmm] + i)
                slot = norm_base + i - 1
                nmax.inc(vector.tensor_reduce(g["normcol"][0:1, slot:slot + 1],
                                              nps[0:1, :], AX, ALU.max))

        @blk.scalar
        def _(scalar):
            for i, (tt, oc) in enumerate(tiles):
                if norm_base is not None:
                    scalar.wait_ge(evd.h, c0[evd] + i + 1)
                    if i >= 1:
                        scalar.wait_ge(nmm.h, c0[nmm] + i)
                    sq.inc(scalar.activation(nsq[:, :], eb[i % 2][:, :], AF.Square))
                if token_major and i >= 1:
                    for q in range(4):
                        k = (i - 1) * 4 + q
                        scalar.wait_ge(tpt.h, c0[tpt] + k + 1)
                        if k >= 2:
                            scalar.wait_ge(out.h, c0[out] + 16 * (k - 1))
                        tev.inc(scalar.copy(tb[q % 2][:, :], tps[q % 2][:, :]))
            if token_major:
                i = len(tiles)
                for q in range(4):
                    k = (i - 1) * 4 + q
                    scalar.wait_ge(tpt.h, c0[tpt] + k + 1)
                    scalar.wait_ge(out.h, c0[out] + 16 * (k - 1))
                    tev.inc(scalar.copy(tb[q % 2][:, :], tps[q % 2][:, :]))

        @blk.gpsimd
        def _(gpsimd):
            for i, (tt, oc) in enumerate(tiles):
                if token_major:
                    for q in range(4):
                        k = i * 4 + q
                        gpsimd.wait_ge(tev.h, c0[tev] + k + 1)
                        out.dma(gpsimd.dma_start(
                            out=out_dst[tt * 512 + q * 128:tt * 512 + (q + 1) * 128,
                                        oc * 128:(oc + 1) * 128],
                            in_=tb[q % 2][:, :]))
                else:
                    gpsimd.wait_ge(evd.h, c0[evd] + i + 1)
                    out.dma(gpsimd.dma_start(
                        out=out_dst[oc * 128:(oc + 1) * 128, tt * 512:(tt + 1) * 512],
                        in_=eb[i % 2][:, :]))
            gpsimd.wait_ge(out.h, out.n)
    sp.put(*sems, tpt)


def p_negm(nc, pes, sp, g):
    cv = sp.get()
    with nc.Block("pnegm") as blk, \
         nc.sbuf_tensor("ngc", [1, 8], F32) as c, \
         nc.psum_tensor("ngp", [128, 1], F32) as psb:
        cv0 = cv.n

        @blk.vector
        def _(vector):
            cv.inc(vector.tensor_reduce(c[0:1, 0:1], g["normcol"][0:1, 0:128], AX, ALU.max))
            vector.wait_ge(cv.h, cv0 + 1)
            cv.inc(vector.tensor_reduce(c[0:1, 1:2], g["normcol"][0:1, 128:256], AX, ALU.max))
            vector.wait_ge(cv.h, cv0 + 3)
            cv.inc(vector.tensor_tensor(c[0:1, 4:5], c[0:1, 2:3], c[0:1, 3:4], ALU.mult))
            vector.wait_ge(cv.h, cv0 + 4)
            cv.inc(vector.tensor_scalar(c[0:1, 5:6], c[0:1, 4:5], -1.0, -1.0,
                                        op0=ALU.mult, op1=ALU.add))
            vector.wait_ge(cv.h, cv0 + 6)
            cv.inc(vector.tensor_copy(g["negm"][:, 0:1], psb[:, 0:1]))

        @blk.scalar
        def _(scalar):
            scalar.wait_ge(cv.h, cv0 + 2)
            scalar.activation(c[0:1, 2:3], c[0:1, 0:1], AF.Sqrt)
            cv.inc(scalar.activation(c[0:1, 3:4], c[0:1, 1:2], AF.Sqrt))

        @blk.tensor
        def _(tensor):
            tensor.wait_ge(cv.h, cv0 + 5)
            cv.inc(tensor.matmul(psb[:, 0:1], g["ones_r"][0:1, :], c[0:1, 5:6],
                                 start=True, stop=True))
    sp.put(cv)


def p_attn(nc, pes, sp, g, name, qT, kT, vtok, attnT_dst, kv_heads, negm_imm):
    grp = H // kv_heads
    sems = [sp.get() for _ in range(10)]
    ld, smm, mskd, expd, pvd, nrm, tpt, tev, outst, vrdy = sems

    sched = []
    cs = mk = 0
    for h in range(H):
        for T in range(NT512):
            ch = [(c, max(0, c * 128 - T * 512)) for c in range(4 * T + 4)]
            tile = h * NT512 + T
            pairs = []
            for pi in range(len(ch) // 2):
                c0_, c1_ = ch[2 * pi], ch[2 * pi + 1]
                diag = c0_[0] * 128 >= T * 512 or c1_[0] * 128 >= T * 512
                pairs.append((pi, c0_, c1_, diag))
            nd = sum(1 for c, _ in ch if c * 128 >= T * 512)
            sched.append(dict(h=h, T=T, ch=ch, pairs=pairs, cs0=cs, mk0=mk,
                              base=tile * 4, tile=tile))
            cs += len(ch)
            mk += nd
    pair_last_cs = []
    for e in sched:
        for pi in range(len(e["pairs"])):
            pair_last_cs.append(e["cs0"] + 2 * pi + 1)

    from contextlib import ExitStack as _ES
    _pes = _ES()
    blk = _pes.enter_context(nc.Block(name))
    k0 = _pes.enter_context(nc.sbuf_tensor(name + "k0", [128, S], BF16))
    k1 = _pes.enter_context(nc.sbuf_tensor(name + "k1", [128, S], BF16))
    q0 = _pes.enter_context(nc.sbuf_tensor(name + "q0", [128, S], BF16))
    q1 = _pes.enter_context(nc.sbuf_tensor(name + "q1", [128, S], BF16))
    v0 = _pes.enter_context(nc.sbuf_tensor(name + "v0", [128, 32, 132], BF16))
    v1 = _pes.enter_context(nc.sbuf_tensor(name + "v1", [128, 32, 132], BF16))
    pt0 = _pes.enter_context(nc.sbuf_tensor(name + "pt0", [128, 32, 512], BF16))
    pt1 = _pes.enter_context(nc.sbuf_tensor(name + "pt1", [128, 32, 512], BF16))
    s0 = _pes.enter_context(nc.psum_tensor(name + "s0", [128, 1024], F32))
    s1 = _pes.enter_context(nc.psum_tensor(name + "s1", [128, 1024], F32))
    o20 = _pes.enter_context(nc.psum_tensor(name + "o20", [128, 2, 132], F32))
    o21 = _pes.enter_context(nc.psum_tensor(name + "o21", [128, 2, 132], F32))
    tq0 = _pes.enter_context(nc.psum_tensor(name + "tq0", [128, 128], BF16))
    tq1 = _pes.enter_context(nc.psum_tensor(name + "tq1", [128, 128], BF16))
    a0 = _pes.enter_context(nc.sbuf_tensor(name + "a0", [128, 128], BF16))
    a1 = _pes.enter_context(nc.sbuf_tensor(name + "a1", [128, 128], BF16))
    linv = _pes.enter_context(nc.sbuf_tensor(name + "li", [128, 4], F32))
    negm_t = _pes.enter_context(nc.sbuf_tensor(name + "nm", [128, 1], F32))
    at0 = _pes.enter_context(nc.sbuf_tensor(name + "at0", [128, 512], BF16))
    at1 = _pes.enter_context(nc.sbuf_tensor(name + "at1", [128, 512], BF16))
    if True:
        kb, qb, vb = [k0, k1], [q0, q1], [v0, v1]
        sps, o2t = [s0, s1], [o20, o21]
        pt_sb = [pt0, pt1]
        tps, ao, at = [tq0, tq1], [a0, a1], [at0, at1]
        c0 = {s: s.n for s in sems}
        neg_ap = negm_t[:, 0:1] if negm_imm is not None else g["negm"][:, 0:1]

        def o2ap(jj):
            return o2t[jj // 2][:, jj % 2, 0:129]

        @blk.sync
        def _(sync):
            for h in range(H):
                kv = h // grp
                if h >= 2:
                    sync.wait_ge(pvd.h, c0[pvd] + 32 * (h - 1))
                ld.dma(sync.dma_start(out=kb[h % 2][:, :],
                                      in_=kT[kv * 128:(kv + 1) * 128, :]))
                ld.dma(sync.dma_start(out=qb[h % 2][:, :],
                                      in_=qT[h * 128:(h + 1) * 128, :]))
                ld.dma(sync.dma_start(
                    out=vb[h % 2][:, :, 0:128],
                    in_=vtok.rearrange("(c p) o -> p c o", p=128)[:, :, kv * 128:(kv + 1) * 128]))

        @blk.tensor
        def _(tensor):
            tensor.wait_ge(vrdy.h, c0[vrdy] + 1)
            pair_g = [0]
            for ent in sched:
                h, T, ch, pairs = ent["h"], ent["T"], ent["ch"], ent["pairs"]
                tb = ent["tile"] % 2
                if T == 0:
                    tensor.wait_ge(ld.h, c0[ld] + 48 * (h + 1))

                def emit_S(pi):
                    pg = pair_g[0] + pi
                    if pg >= 2:
                        prev_last_cs = pair_last_cs[pg - 2]
                        tensor.wait_ge(expd.h, c0[expd] + prev_last_cs + 1)
                    for half in range(2):
                        c, qlo = ch[2 * pi + half]
                        smm.inc(tensor.matmul(
                            sps[pg % 2][:, half * 512 + qlo:half * 512 + 512],
                            kb[h % 2][:, c * 128:(c + 1) * 128],
                            qb[h % 2][:, T * 512 + qlo:(T + 1) * 512],
                            start=True, stop=True))

                def emit_PV(pi):
                    _, _, c1_, _ = pairs[pi]
                    last_cs = ent["cs0"] + 2 * pi + 1
                    tensor.wait_ge(expd.h, c0[expd] + last_cs + 1)
                    for half in range(2):
                        c, qlo = ch[2 * pi + half]
                        if c == 0 and ent["base"] >= 4:
                            tensor.wait_ge(nrm.h, c0[nrm] + ent["base"])
                        for jj in range(max(0, c - 4 * T), 4):
                            last = (c == 4 * T + jj)
                            m = tensor.matmul(o2ap(jj),
                                              pt_sb[tb][:, c, jj * 128:(jj + 1) * 128],
                                              vb[h % 2][:, c, 0:129],
                                              start=(c == 0 and jj % 2 == 0),
                                              stop=last)
                            if last:
                                pvd.inc(m)

                npair = len(pairs)
                emit_S(0)
                for pi in range(npair):
                    if pi + 1 < npair:
                        emit_S(pi + 1)
                    emit_PV(pi)
                pair_g[0] += npair
                for jj in range(4):
                    pq = ent["base"] + jj
                    tensor.wait_ge(nrm.h, c0[nrm] + pq + 1)
                    if pq >= 2:
                        tensor.wait_ge(tev.h, c0[tev] + pq - 1)
                    tpt.inc(tensor.transpose(tps[pq % 2][:, :], ao[pq % 2][:, :],
                                             g["ident"][:, :]))

        @blk.vector
        def _(vector):
            vr = [0]
            vector.memset(v0[:, :, 128:129], 1.0)
            vector.memset(v1[:, :, 128:129], 1.0)
            if negm_imm is not None:
                vector.memset(negm_t[:, 0:1], float(negm_imm))
            vrdy.inc(vector.engine_nop())
            vrdy.n += 4 * H * NT512
            pair_g = [0]
            for ent in sched:
                h, T, ch, pairs = ent["h"], ent["T"], ent["ch"], ent["pairs"]
                mki = ent["mk0"]
                for idx, (c, qlo) in enumerate(ch):
                    if c * 128 >= T * 512:
                        cs_g = ent["cs0"] + idx
                        pg = pair_g[0] + idx // 2
                        half = idx % 2
                        vector.wait_ge(smm.h, c0[smm] + cs_g + 1)
                        mskd.inc(vector.tensor_tensor(
                            sps[pg % 2][:, half * 512 + qlo:half * 512 + qlo + 128],
                            sps[pg % 2][:, half * 512 + qlo:half * 512 + qlo + 128],
                            g["tri"][:, :], ALU.add))
                        mki += 1
                pair_g[0] += len(pairs)
                for jj in range(4):
                    pq = ent["base"] + jj
                    vector.wait_ge(pvd.h, c0[pvd] + ent["base"] + 4)
                    if pq >= 2:
                        vector.wait_ge(tpt.h, c0[tpt] + pq - 1)
                    vector.reciprocal(linv[:, jj:jj + 1],
                                      o2t[jj // 2][:, jj % 2, 128:129]).then_inc(vrdy.h, 1)
                    vr[0] += 1
                    vector.wait_ge(vrdy.h, c0[vrdy] + 1 + vr[0])
                    nrm.inc(vector.tensor_scalar(ao[pq % 2][:, :],
                                                 o2t[jj // 2][:, jj % 2, 0:128],
                                                 linv[:, jj:jj + 1], None,
                                                 op0=ALU.mult))

        @blk.scalar
        def _(scalar):
            scalar.wait_ge(vrdy.h, c0[vrdy] + 1)
            pair_g = [0]
            for ent in sched:
                h, T, ch, pairs = ent["h"], ent["T"], ent["ch"], ent["pairs"]
                tb = ent["tile"] % 2
                mki = ent["mk0"]
                if ent["tile"] >= 2:
                    scalar.wait_ge(pvd.h, c0[pvd] + 4 * (ent["tile"] - 1))
                for (pi, (ca, qla), (cb, qlb), diag) in pairs:
                    pg = pair_g[0] + pi
                    last_cs = ent["cs0"] + 2 * pi + 1
                    scalar.wait_ge(smm.h, c0[smm] + last_cs + 1)
                    if diag:
                        if ca * 128 >= T * 512:
                            mki += 1
                        if cb * 128 >= T * 512:
                            mki += 1
                        scalar.wait_ge(mskd.h, c0[mskd] + mki)
                        scalar.activation(pt_sb[tb][:, ca, qla:512],
                                          sps[pg % 2][:, qla:512],
                                          AF.Exp, bias=neg_ap).then_inc(expd.h, 1)
                        expd.n += 1
                        scalar.activation(pt_sb[tb][:, cb, qlb:512],
                                          sps[pg % 2][:, 512 + qlb:1024],
                                          AF.Exp, bias=neg_ap).then_inc(expd.h, 1)
                        expd.n += 1
                    else:
                        expd.n += 2
                        scalar.activation(pt_sb[tb][:, 2 * pi:2 * pi + 2, 0:512],
                                          sps[pg % 2][:, 0:1024],
                                          AF.Exp, bias=neg_ap).then_inc(expd.h, 2)
                pair_g[0] += len(pairs)
                tile = ent["tile"]
                for jj in range(4):
                    pq = ent["base"] + jj
                    scalar.wait_ge(tpt.h, c0[tpt] + pq + 1)
                    if tile >= 2 and jj == 0:
                        scalar.wait_ge(outst.h, c0[outst] + 16 * (tile - 1))
                    tev.inc(scalar.copy(at[tile % 2][:, jj * 128:(jj + 1) * 128],
                                        tps[pq % 2][:, :]))

        @blk.gpsimd
        def _(gpsimd):
            for ent in sched:
                h, T, tile = ent["h"], ent["T"], ent["tile"]
                gpsimd.wait_ge(tev.h, c0[tev] + 4 * (tile + 1))
                outst.dma(gpsimd.dma_start(
                    out=attnT_dst[h * 128:(h + 1) * 128, T * 512:(T + 1) * 512],
                    in_=at[tile % 2][:, :]))
            gpsimd.wait_ge(outst.h, outst.n)
    _pes.close()
    sp.put(*sems)


def p_ffn1(nc, pes, sp, g, t):
    iblocks = []
    c = 0
    while c < ICH:
        iblocks.append((c, min(8, ICH - c)))
        c += 8
    sems = [sp.get() for _ in range(7)]
    wl, xl, gmm, umm, sil, ev, out = sems
    with nc.Block("pffn1") as blk, \
         nc.sbuf_tensor("f1wg", [128, NCH, 1024], BF16) as wg_sb, \
         nc.sbuf_tensor("f1wu", [128, NCH, 1024], BF16) as wu_sb, \
         nc.sbuf_tensor("f1x0", [128, NCH, 512], BF16) as xa, \
         nc.sbuf_tensor("f1x1", [128, NCH, 512], BF16) as xb, \
         nc.psum_tensor("f1g0", [128, 512], F32) as g0, \
         nc.psum_tensor("f1g1", [128, 512], F32) as g1, \
         nc.psum_tensor("f1u0", [128, 512], F32) as u0, \
         nc.psum_tensor("f1u1", [128, 512], F32) as u1, \
         nc.sbuf_tensor("f1s", [128, 512], F32) as gs, \
         nc.sbuf_tensor("f1h0", [128, 512], BF16) as h0, \
         nc.sbuf_tensor("f1h1", [128, 512], BF16) as h1:
        xn_sb, gps, ups, hb = [xa, xb], [g0, g1], [u0, u1], [h0, h1]
        c0 = {s: s.n for s in sems}
        jobs = []
        for bi, (ic0, bn) in enumerate(iblocks):
            for tt in range(NT512):
                for icr in range(bn):
                    jobs.append((bi, ic0, bn, tt, icr))

        @blk.sync
        def _(sync):
            ji = 0
            for bi, (ic0, bn) in enumerate(iblocks):
                if bi >= 1:
                    prev_jobs = sum(b[1] for b in iblocks[:bi]) * NT512
                    sync.wait_ge(umm.h, c0[umm] + prev_jobs)
                for c in range(NCH):
                    wl.dma(sync.dma_start(
                        out=wg_sb[:, c, 0:bn * 128],
                        in_=t["wg_bf"][c * 128:(c + 1) * 128, ic0 * 128:(ic0 + bn) * 128]))
                    wl.dma(sync.dma_start(
                        out=wu_sb[:, c, 0:bn * 128],
                        in_=t["wu_bf"][c * 128:(c + 1) * 128, ic0 * 128:(ic0 + bn) * 128]))
                for tt in range(NT512):
                    gtile = bi * NT512 + tt
                    if gtile >= 2:
                        done_jobs = [j for j in range(len(jobs))
                                     if jobs[j][0] * NT512 + jobs[j][3] <= gtile - 2]
                        sync.wait_ge(umm.h, c0[umm] + len(done_jobs))
                    xl.dma(sync.dma_start(
                        out=xn_sb[gtile % 2][:, :, :],
                        in_=t["xnT"].rearrange("(c p) t -> p c t", p=128)[:, :, tt * 512:(tt + 1) * 512]))

        @blk.tensor
        def _(tensor):
            for i, (bi, ic0, bn, tt, icr) in enumerate(jobs):
                gtile = bi * NT512 + tt
                if icr == 0:
                    tensor.wait_ge(xl.h, c0[xl] + 16 * (gtile + 1))
                    tensor.wait_ge(wl.h, c0[wl] + 32 * (bi + 1))
                if i >= 2:
                    tensor.wait_ge(ev.h, c0[ev] + i - 1)
                for c in range(NCH):
                    m = tensor.matmul(gps[i % 2][:, :],
                                      wg_sb[:, c, icr * 128:(icr + 1) * 128],
                                      xn_sb[gtile % 2][:, c, :],
                                      start=(c == 0), stop=(c == NCH - 1))
                    if c == NCH - 1:
                        gmm.inc(m)
                for c in range(NCH):
                    m = tensor.matmul(ups[i % 2][:, :],
                                      wu_sb[:, c, icr * 128:(icr + 1) * 128],
                                      xn_sb[gtile % 2][:, c, :],
                                      start=(c == 0), stop=(c == NCH - 1))
                    if c == NCH - 1:
                        umm.inc(m)

        @blk.scalar
        def _(scalar):
            for i, job in enumerate(jobs):
                scalar.wait_ge(gmm.h, c0[gmm] + i + 1)
                if i >= 1:
                    scalar.wait_ge(ev.h, c0[ev] + i)
                sil.inc(scalar.activation(gs[:, :], gps[i % 2][:, :], AF.Silu))

        @blk.vector
        def _(vector):
            for i, job in enumerate(jobs):
                vector.wait_ge(sil.h, c0[sil] + i + 1)
                vector.wait_ge(umm.h, c0[umm] + i + 1)
                if i >= 2:
                    vector.wait_ge(out.h, c0[out] + 16 * (i - 1))
                ev.inc(vector.tensor_tensor(hb[i % 2][:, :], gs[:, :],
                                            ups[i % 2][:, :], ALU.mult))

        @blk.gpsimd
        def _(gpsimd):
            for i, (bi, ic0, bn, tt, icr) in enumerate(jobs):
                gpsimd.wait_ge(ev.h, c0[ev] + i + 1)
                ic = ic0 + icr
                out.dma(gpsimd.dma_start(
                    out=t["hT"][ic * 128:(ic + 1) * 128, tt * 512:(tt + 1) * 512],
                    in_=hb[i % 2][:, :]))
            gpsimd.wait_ge(out.h, out.n)
    sp.put(*sems)


def p_combine(nc, pes, sp, g, name, parts, resid_src, bias_bc_key, dst):
    sems = [sp.get() for _ in range(5)]
    wl, al, mm, ev, out = sems
    with nc.Block(name) as blk, \
         nc.sbuf_tensor(name + "w0", [128, parts[0][2], 512], BF16) as w0, \
         nc.sbuf_tensor(name + "w1", [128, parts[1][2], 512], BF16) as w1, \
         nc.sbuf_tensor(name + "a0", [128, parts[0][2], 128], BF16) as a00, \
         nc.sbuf_tensor(name + "a1", [128, parts[0][2], 128], BF16) as a01, \
         nc.sbuf_tensor(name + "b0", [128, parts[1][2], 128], BF16) as a10, \
         nc.sbuf_tensor(name + "b1", [128, parts[1][2], 128], BF16) as a11, \
         nc.sbuf_tensor(name + "r0", [128, 512], F32) as r0, \
         nc.sbuf_tensor(name + "r1", [128, 512], F32) as r1, \
         nc.psum_tensor(name + "p0", [128, 512], F32) as p0, \
         nc.psum_tensor(name + "p1", [128, 512], F32) as p1, \
         nc.sbuf_tensor(name + "o0", [128, 512], F32) as ob0, \
         nc.sbuf_tensor(name + "o1", [128, 512], F32) as ob1:
        w_sb = [w0, w1]
        act_sb = [[a00, a01], [a10, a11]]
        rb, ps, ob = [r0, r1], [p0, p1], [ob0, ob1]
        c0 = {s: s.n for s in sems}
        bias = g[bias_bc_key]
        n_ch_tot = sum(p[2] for p in parts)

        @blk.sync
        def _(sync):
            for oh in range(4):
                if oh >= 1:
                    sync.wait_ge(mm.h, c0[mm] + oh * NT128)
                for pi, (actT, w_bf, nch, trw) in enumerate(parts):
                    for c in range(nch):
                        if trw:
                            wl.dma(sync.dma_start(
                                out=w_sb[pi][:, c, :],
                                in_=w_bf[oh * 512:(oh + 1) * 512, c * 128:(c + 1) * 128],
                                transpose=True))
                        else:
                            wl.dma(sync.dma_start(
                                out=w_sb[pi][:, c, :],
                                in_=w_bf[c * 128:(c + 1) * 128, oh * 512:(oh + 1) * 512]))
                for tt in range(NT128):
                    i = oh * NT128 + tt
                    if i >= 2:
                        sync.wait_ge(mm.h, c0[mm] + i - 1)
                    for pi, (actT, w_bf, nch, trw) in enumerate(parts):
                        al.dma(sync.dma_start(
                            out=act_sb[pi][i % 2][:, :, :],
                            in_=actT.rearrange("(c p) t -> p c t", p=128)[:, :, tt * 128:(tt + 1) * 128]))
                    al.dma(sync.dma_start(
                        out=rb[i % 2][:, :],
                        in_=resid_src[tt * 128:(tt + 1) * 128, oh * 512:(oh + 1) * 512]))

        @blk.tensor
        def _(tensor):
            for oh in range(4):
                tensor.wait_ge(wl.h, c0[wl] + 16 * n_ch_tot * (oh + 1))
                for tt in range(NT128):
                    i = oh * NT128 + tt
                    tensor.wait_ge(al.h, c0[al] + 48 * (i + 1))
                    if i >= 2:
                        tensor.wait_ge(ev.h, c0[ev] + i - 1)
                    first = True
                    for pi, (actT, w_bf, nch, trw) in enumerate(parts):
                        for c in range(nch):
                            last = (pi == len(parts) - 1 and c == nch - 1)
                            m = tensor.matmul(ps[i % 2][:, :],
                                              act_sb[pi][i % 2][:, c, :],
                                              w_sb[pi][:, c, :],
                                              start=first, stop=last)
                            first = False
                            if last:
                                mm.inc(m)

        @blk.vector
        def _(vector):
            for oh in range(4):
                for tt in range(NT128):
                    i = oh * NT128 + tt
                    vector.wait_ge(mm.h, c0[mm] + i + 1)
                    if i >= 2:
                        vector.wait_ge(out.h, c0[out] + 16 * (i - 1))
                    vector.scalar_tensor_tensor(
                        ob[i % 2][:, :], ps[i % 2][:, :], 1.0, rb[i % 2][:, :],
                        op0=ALU.mult, op1=ALU.add)
                    ev.inc(vector.tensor_tensor(
                        ob[i % 2][:, :], ob[i % 2][:, :],
                        bias[:, oh * 512:(oh + 1) * 512], ALU.add))

        @blk.gpsimd
        def _(gpsimd):
            for oh in range(4):
                for tt in range(NT128):
                    i = oh * NT128 + tt
                    gpsimd.wait_ge(ev.h, c0[ev] + i + 1)
                    out.dma(gpsimd.dma_start(
                        out=dst[tt * 128:(tt + 1) * 128, oh * 512:(oh + 1) * 512],
                        in_=ob[i % 2][:, :]))
            gpsimd.wait_ge(out.h, out.n)
    sp.put(*sems)


def p_proj_l1qk(nc, pes, sp, g, name, w_bf, xnT_src, out_dst, o_chunks, sqrt_scale, rope_src):
    ocols = o_chunks * 128
    sems = [sp.get() for _ in range(11)]
    wl, xl, mm, rpd, sqd, summ, rcd, rsd, bcm, apd, out = sems
    tiles = [(tt, oc) for tt in range(NT512) for oc in range(o_chunks)]
    n = len(tiles)
    from contextlib import ExitStack as _ES
    _pes = _ES()
    blk = _pes.enter_context(nc.Block(name))
    w_sb = _pes.enter_context(nc.sbuf_tensor(name + "w", [128, NCH, ocols], BF16))
    xn_sb = [_pes.enter_context(nc.sbuf_tensor(f"{name}x{i}", [128, NCH, 512], BF16))
             for i in range(2)]
    ps = [_pes.enter_context(nc.psum_tensor(f"{name}p{i}", [128, 512], F32))
          for i in range(3)]
    ro = [_pes.enter_context(nc.sbuf_tensor(f"{name}r{i}", [128, 512], F32))
          for i in range(3)]
    rc2 = _pes.enter_context(nc.sbuf_tensor(name + "rc", [128, S], F32))
    rs2 = _pes.enter_context(nc.sbuf_tensor(name + "rs", [128, S], F32))
    tmpc = _pes.enter_context(nc.sbuf_tensor(name + "tc", [128, 512], F32))
    tmps = _pes.enter_context(nc.sbuf_tensor(name + "ts", [128, 512], F32))
    rsq = _pes.enter_context(nc.sbuf_tensor(name + "rq", [128, 512], F32))
    rps = _pes.enter_context(nc.psum_tensor(name + "rp", [1, 512], F32))
    rrow_a = _pes.enter_context(nc.sbuf_tensor(name + "ra", [1, 512], F32))
    rrow_b = _pes.enter_context(nc.sbuf_tensor(name + "rb", [1, 512], F32))
    rrow_c = _pes.enter_context(nc.sbuf_tensor(name + "rx", [1, 512], F32))
    rbc = _pes.enter_context(nc.psum_tensor(name + "rh", [128, 512], F32))
    eb = [_pes.enter_context(nc.sbuf_tensor(f"{name}e{i}", [128, 512], BF16))
          for i in range(2)]
    if True:
        c0 = {s: s.n for s in sems}

        @blk.sync
        def _(sync):
            wl.dma(sync.dma_start(out=rc2[0:64, :], in_=rope_src[0][:, :]))
            wl.dma(sync.dma_start(out=rc2[64:128, :], in_=rope_src[0][:, :]))
            wl.dma(sync.dma_start(out=rs2[0:64, :], in_=rope_src[1][:, :]))
            wl.dma(sync.dma_start(out=rs2[64:128, :], in_=rope_src[1][:, :]))
            for c in range(NCH):
                wl.dma(sync.dma_start(out=w_sb[:, c, :],
                                      in_=w_bf[c * 128:(c + 1) * 128, :]))
            for tt in range(NT512):
                if tt >= 2:
                    sync.wait_ge(mm.h, c0[mm] + (tt - 1) * o_chunks)
                xl.dma(sync.dma_start(
                    out=xn_sb[tt % 2][:, :, :],
                    in_=xnT_src.rearrange("(c p) t -> p c t", p=128)[:, :, tt * 512:(tt + 1) * 512]))

        @blk.tensor
        def _(tensor):
            tensor.wait_ge(wl.h, wl.n)

            def aux(i):
                tensor.wait_ge(sqd.h, c0[sqd] + i + 1)
                if i >= 1:
                    tensor.wait_ge(rcd.h, c0[rcd] + i)
                summ.inc(tensor.matmul(rps[0:1, :], g["ones_f"][:, 0:1], rsq[:, :],
                                       start=True, stop=True))
                tensor.wait_ge(rsd.h, c0[rsd] + i + 1)
                if i >= 1:
                    tensor.wait_ge(apd.h, c0[apd] + i)
                bcm.inc(tensor.matmul(rbc[:, :], g["ones_r"][0:1, :], rrow_c[0:1, :],
                                      start=True, stop=True))

            for i, (tt, oc) in enumerate(tiles):
                tensor.wait_ge(xl.h, c0[xl] + 16 * (tt + 1))
                if i >= 3:
                    tensor.wait_ge(rpd.h, c0[rpd] + i - 2)
                for c in range(NCH):
                    m = tensor.matmul(ps[i % 3][:, :],
                                      w_sb[:, c, oc * 128:(oc + 1) * 128],
                                      xn_sb[tt % 2][:, c, :],
                                      start=(c == 0), stop=(c == NCH - 1))
                    if c == NCH - 1:
                        mm.inc(m)
                if i >= 2:
                    aux(i - 2)
            aux(n - 2)
            aux(n - 1)

        @blk.vector
        def _(vector):
            def rope(i, tt):
                vector.wait_ge(mm.h, c0[mm] + i + 1)
                if i >= 3:
                    vector.wait_ge(apd.h, c0[apd] + i - 2)
                p, r = ps[i % 3], ro[i % 3]
                c_ap = rc2[:, tt * 512:(tt + 1) * 512]
                s_ap = rs2[:, tt * 512:(tt + 1) * 512]
                vector.tensor_tensor(tmpc[:, :], p[:, :], c_ap, ALU.mult)
                vector.tensor_tensor(tmps[0:64, :], p[64:128, :],
                                     rs2[0:64, tt * 512:(tt + 1) * 512], ALU.mult)
                vector.tensor_tensor(tmps[64:128, :], p[0:64, :],
                                     rs2[64:128, tt * 512:(tt + 1) * 512], ALU.mult)
                vector.tensor_add(r[0:64, :], tmpc[0:64, :], tmps[0:64, :])
                rpd.inc(vector.tensor_sub(r[64:128, :], tmpc[64:128, :], tmps[64:128, :]))

            def rcalc(i):
                vector.wait_ge(summ.h, c0[summ] + i + 1)
                if i >= 1:
                    vector.wait_ge(rsd.h, c0[rsd] + i)
                vector.tensor_scalar(rrow_a[0:1, :], rps[0:1, :], 1.0 / DH, 1e-6,
                                     op0=ALU.mult, op1=ALU.add)
                rcd.inc(vector.reciprocal(rrow_b[0:1, :], rrow_a[0:1, :]))

            def apply(i):
                vector.wait_ge(bcm.h, c0[bcm] + i + 1)
                if i >= 2:
                    vector.wait_ge(out.h, c0[out] + 16 * (i - 1))
                apd.inc(vector.tensor_tensor(eb[i % 2][:, :], ro[i % 3][:, :],
                                             rbc[:, :], ALU.mult))

            for i, (tt, oc) in enumerate(tiles):
                rope(i, tt)
                if i >= 2:
                    rcalc(i - 2)
                    apply(i - 2)
            for i in (n - 2, n - 1):
                rcalc(i)
                apply(i)

        @blk.scalar
        def _(scalar):
            def sq(i):
                scalar.wait_ge(rpd.h, c0[rpd] + i + 1)
                if i >= 1:
                    scalar.wait_ge(summ.h, c0[summ] + i)
                sqd.inc(scalar.activation(rsq[:, :], ro[i % 3][:, :], AF.Square))

            def sqrt(i):
                scalar.wait_ge(rcd.h, c0[rcd] + i + 1)
                if i >= 1:
                    scalar.wait_ge(bcm.h, c0[bcm] + i)
                rsd.inc(scalar.activation(rrow_c[0:1, :], rrow_b[0:1, :], AF.Sqrt,
                                          scale=float(sqrt_scale)))

            for i in range(n):
                sq(i)
                if i >= 1:
                    sqrt(i - 1)
            sqrt(n - 1)

        @blk.gpsimd
        def _(gpsimd):
            for i, (tt, oc) in enumerate(tiles):
                gpsimd.wait_ge(apd.h, c0[apd] + i + 1)
                out.dma(gpsimd.dma_start(
                    out=out_dst[oc * 128:(oc + 1) * 128, tt * 512:(tt + 1) * 512],
                    in_=eb[i % 2][:, :]))
            gpsimd.wait_ge(out.h, out.n)
    _pes.close()
    sp.put(*sems)


PHASES = ["consts", "cast", "wgen", "ln0", "q0", "k0", "negm", "v0", "attn0",
          "ffn1", "comb0", "ln1", "q1", "k1", "v1", "attn1", "comb1"]


def build_nc(upto=None, dbg=()):
    nc = bass.Bass("TRN2", target_bir_lowering=False)
    from contextlib import ExitStack
    es = ExitStack()

    t = {}
    t["x"] = nc.dram_tensor("x", [S, D], F32, kind="ExternalInput").ap()
    for k, cols in [("wq0", D), ("wk0", D), ("wv0", D), ("wo0", D),
                    ("wq1", D), ("wk1", 512), ("wv1", 512),
                    ("wo1", D), ("w1ffn", D)]:
        t[k + "s"] = nc.dram_tensor(k + "s", [256, cols], F32, kind="ExternalInput").ap()
    for k, shp in [("bq0", [128, 16]), ("bk0", [128, 16]), ("bv0", [128, 16]),
                   ("bo0", [1, D]), ("b1", [1, D]), ("lut", [1, 16]),
                   ("tri", [128, 128]), ("ropecos", [64, S]), ("ropesin", [64, S])]:
        t[k] = nc.dram_tensor(k, shp, F32, kind="ExternalInput").ap()
    t["ident"] = nc.dram_tensor("ident", [128, 128], BF16, kind="ExternalInput").ap()
    t["gidxs"] = nc.dram_tensor("gidxs", [256, IFF], I32, kind="ExternalInput").ap()
    t["uidxs"] = nc.dram_tensor("uidxs", [256, IFF], I32, kind="ExternalInput").ap()
    t["didxs"] = nc.dram_tensor("didxs", [688, D], I32, kind="ExternalInput").ap()
    t["y"] = nc.dram_tensor("y", [S, D], F32, kind="ExternalOutput").ap()

    scr = [("cast_loc", [256, 15360], BF16), ("cast_full", [D, 15360], BF16),
           ("wgu_loc", [256, 2 * IFF], BF16), ("wgu_full", [D, 2 * IFF], BF16),
           ("wd_loc", [688, D], BF16), ("wd_bf", [IFF, D], BF16),
           ("xnT", [D, S], BF16), ("xnT2", [D, S], BF16),
           ("qTs", [D, S], BF16), ("kTs", [D, S], BF16),
           ("qTs2", [D, S], BF16), ("kTs2", [512, S], BF16),
           ("vtok", [S, D], BF16), ("vtok2", [S, 512], BF16),
           ("attnT", [D, S], BF16), ("attnT2", [D, S], BF16),
           ("hT", [IFF, S], BF16), ("x1", [S, D], F32)]
    for k, shp, dt in scr:
        t[k] = nc.dram_tensor(k, shp, dt, kind="Internal").ap()
    off = 0
    for key, cols in [("wq0", D), ("wk0", D), ("wv0", D), ("wo0", D), ("wq1", D),
                      ("wk1", 512), ("wv1", 512), ("wo1", D), ("bw1", D)]:
        t[key + "_bf"] = t["cast_full"][:, off:off + cols]
        off += cols
    t["wg_bf"] = t["wgu_full"][:, 0:IFF]
    t["wu_bf"] = t["wgu_full"][:, IFF:2 * IFF]

    g = {}
    g["ident"] = es.enter_context(nc.sbuf_tensor("g_id", [128, 128], BF16))
    g["tri"] = es.enter_context(nc.sbuf_tensor("g_tri", [128, 128], F32))
    g["bq"] = es.enter_context(nc.sbuf_tensor("g_bq", [128, 16], F32))
    g["bk"] = es.enter_context(nc.sbuf_tensor("g_bk", [128, 16], F32))
    g["bv"] = es.enter_context(nc.sbuf_tensor("g_bv", [128, 16], F32))
    g["bo_r"] = es.enter_context(nc.sbuf_tensor("g_bor", [1, D], F32))
    g["b1_r"] = es.enter_context(nc.sbuf_tensor("g_b1r", [1, D], F32))
    g["bo_bc"] = es.enter_context(nc.sbuf_tensor("g_bobc", [128, D], F32))
    g["b1_bc"] = es.enter_context(nc.sbuf_tensor("g_b1bc", [128, D], F32))
    g["ones_f"] = es.enter_context(nc.sbuf_tensor("g_onf", [128, 1], F32))
    g["ones_r"] = es.enter_context(nc.sbuf_tensor("g_onr", [1, 128], F32))
    g["normcol"] = es.enter_context(nc.sbuf_tensor("g_nc", [1, 256], F32))
    g["negm"] = es.enter_context(nc.sbuf_tensor("g_ngm", [128, 1], F32))

    sp = SemPool(es, nc)
    stop = PHASES.index(upto) if upto else len(PHASES) - 1

    def run(label, fn):
        if PHASES.index(label) <= stop:
            fn()

    run("consts", lambda: p_consts(nc, None, sp, t, g))
    run("cast", lambda: p_cast(nc, None, sp, t))
    run("wgen", lambda: p_wgen(nc, None, sp, t))
    run("ln0", lambda: p_layernorm(nc, None, sp, g, "ln0", t["x"], t["xnT"]))
    run("q0", lambda: p_proj(nc, None, sp, g, "q0", t["wq0_bf"], t["xnT"], t["qTs"],
                             16, True, bias_col="bq", scale=L0_SCALE, norm_base=0))
    run("k0", lambda: p_proj(nc, None, sp, g, "k0", t["wk0_bf"], t["xnT"], t["kTs"],
                             16, True, bias_col="bk", scale=L0_SCALE, norm_base=128))
    run("negm", lambda: p_negm(nc, None, sp, g))
    run("v0", lambda: p_proj(nc, None, sp, g, "v0", t["wv0_bf"], t["xnT"], t["vtok"],
                             16, True, bias_col="bv", token_major=True))
    run("attn0", lambda: p_attn(nc, None, sp, g, "at0", t["qTs"], t["kTs"], t["vtok"],
                                t["attnT"], H, None))
    run("ffn1", lambda: p_ffn1(nc, None, sp, g, t))
    run("comb0", lambda: p_combine(nc, None, sp, g, "cb0",
                                   [(t["attnT"], t["wo0_bf"], NCH, True),
                                    (t["hT"], t["wd_bf"], ICH, False)],
                                   t["x"], "bo_bc", t["x1"]))
    run("ln1", lambda: p_layernorm(nc, None, sp, g, "ln1", t["x1"], t["xnT2"]))
    run("q1", lambda: p_proj_l1qk(nc, None, sp, g, "q1", t["wq1_bf"], t["xnT2"],
                                  t["qTs2"], 16, 1.0,
                                  (t["ropecos"], t["ropesin"])))
    run("k1", lambda: p_proj_l1qk(nc, None, sp, g, "k1", t["wk1_bf"], t["xnT2"],
                                  t["kTs2"], 4, L1_KSCALE2,
                                  (t["ropecos"], t["ropesin"])))
    run("v1", lambda: p_proj(nc, None, sp, g, "v1", t["wv1_bf"], t["xnT2"], t["vtok2"],
                             4, False, token_major=True))
    run("attn1", lambda: p_attn(nc, None, sp, g, "at1", t["qTs2"], t["kTs2"], t["vtok2"],
                                t["attnT2"], 4, L1_NEG_BOUND))
    run("comb1", lambda: p_combine(nc, None, sp, g, "cb1",
                                   [(t["attnT2"], t["wo1_bf"], NCH, False),
                                    (t["xnT2"], t["bw1_bf"], NCH, True)],
                                   t["x1"], "b1_bc", t["y"]))

    if dbg:
        dsem = sp.get()
        with nc.Block("pdbg") as blk:
            @blk.gpsimd
            def _(gpsimd):
                for nm in dbg:
                    src = t[nm]
                    dt = {"x1": F32}.get(nm, BF16)
                    od = nc.dram_tensor("dbg_" + nm, list(src.shape), dt,
                                        kind="ExternalOutput").ap()
                    dsem.dma(gpsimd.dma_start(out=od[:, :], in_=src[:, :]))
                gpsimd.wait_ge(dsem.h, dsem.n)
        sp.put(dsem)

    es.close()
    return nc


def _host_consts():
    c = {}
    c["ident"] = np.eye(128, dtype=ml_dtypes.bfloat16)
    kk = np.arange(128)[:, None]
    qq = np.arange(128)[None, :]
    c["tri"] = np.where(qq >= kk, np.float32(0.0), np.float32(MASK)).astype(np.float32)
    half = 64
    inv = (np.float32(1.0) / (np.float32(100000.0)
           ** (np.arange(half, dtype=np.float32) / np.float32(half)))).astype(np.float32)
    freqs = (np.arange(S, dtype=np.float32)[:, None] * inv[None, :]).astype(np.float32)
    c["ropecos"] = np.ascontiguousarray(np.cos(freqs).astype(np.float32).T)
    c["ropesin"] = np.ascontiguousarray(np.sin(freqs).astype(np.float32).T)
    return c


def host_in_maps(inputs):
    ii = {k: np.asarray(v) for k, v in inputs.items()}
    X = ii["L0_attn_X"].astype(np.float32, copy=False)

    common = dict(_host_consts())
    for k, src in [("bq0", "L0_attn_bq"), ("bk0", "L0_attn_bk"), ("bv0", "L0_attn_bv")]:
        common[k] = np.ascontiguousarray(ii[src].astype(np.float32).reshape(16, 128).T)
    common["bo0"] = ii["L0_attn_bo"].astype(np.float32).reshape(1, D)
    common["lut"] = ii["L0_ffn_lut"].astype(np.float32).reshape(1, 16)
    common["b1"] = ii["L1_ffn_bias"].astype(np.float32).reshape(1, D)
    for k in list(common):
        common[k] = np.ascontiguousarray(np.asarray(common[k]))

    full = {"wq0": ii["L0_attn_Wq"], "wk0": ii["L0_attn_Wk"],
            "wv0": ii["L0_attn_Wv"], "wo0": ii["L0_attn_Wo"],
            "wq1": ii["L1_attn_wq"], "wk1": ii["L1_attn_wk"],
            "wv1": ii["L1_attn_wv"], "wo1": ii["L1_attn_wo"],
            "w1ffn": ii["L1_ffn_weight"]}
    full = {k: np.asarray(v, np.float32) for k, v in full.items()}
    gT = np.ascontiguousarray(ii["L0_ffn_gate_idx"].astype(np.int32).T)
    uT = np.ascontiguousarray(ii["L0_ffn_up_idx"].astype(np.int32).T)
    dT = np.ascontiguousarray(ii["L0_ffn_down_idx"].astype(np.int32).T)

    in_maps = []
    for b in range(8):
        m = dict(common, x=np.ascontiguousarray(X[b]))
        for k, v in full.items():
            m[k + "s"] = np.ascontiguousarray(v[b * 256:(b + 1) * 256])
        m["gidxs"] = np.ascontiguousarray(gT[b * 256:(b + 1) * 256])
        m["uidxs"] = np.ascontiguousarray(uT[b * 256:(b + 1) * 256])
        m["didxs"] = np.ascontiguousarray(dT[b * 688:(b + 1) * 688])
        in_maps.append(m)
    return in_maps


def kernel(**inputs):
    in_maps = host_in_maps(inputs)
    nc = build_nc()
    res = run_bass_kernel_spmd(nc, in_maps, core_ids=list(range(8)))
    return np.stack([np.asarray(res.results[b]["y"], dtype=np.float32)
                     for b in range(8)], axis=0)


# version 05
# speedup vs baseline: 1.0983x; 1.0720x over previous
import numpy as np
import ml_dtypes
import concourse.bass as bass
import concourse.mybir as mybir
from concourse.bass_utils import run_bass_kernel_spmd

F32 = mybir.dt.float32
BF16 = mybir.dt.bfloat16
I32 = mybir.dt.int32
AF = mybir.ActivationFunctionType
ALU = mybir.AluOpType
AX = mybir.AxisListType.X

S = 4096
D = 2048
H = 16
DH = 128
IFF = 5504
NCH = D // 128
ICH = IFF // 128
NT128 = S // 128
NT512 = S // 512
L0_SCALE = float(np.float32(1.0) / np.sqrt(np.sqrt(np.float32(DH), dtype=np.float32)))
L1_KSCALE2 = 1.0 / DH
MASK = -10000.0
L1_NEG_BOUND = -12.0


class Sem:
    def __init__(self, es, nc, name):
        self.h = es.enter_context(nc.semaphore(name))
        self.n = 0

    def dma(self, instr):
        self.n += 16
        instr.then_inc(self.h, 16)
        return self.n

    def inc(self, instr):
        self.n += 1
        instr.then_inc(self.h, 1)
        return self.n


class SemPool:
    def __init__(self, es, nc):
        self.es, self.nc = es, nc
        self.free = []
        self.k = 0

    def get(self):
        if self.free:
            return self.free.pop(0)
        s = Sem(self.es, self.nc, f"sp{self.k}")
        self.k += 1
        return s

    def put(self, *sems):
        self.free.extend(sems)


CAST_KEYS = ("wq0", "wk0", "wv0", "wo0", "wq1", "wk1", "wv1", "wo1")


def p_cast(nc, pes, sp, t, blk, cc):
    jobs = []
    off = 0
    for key in CAST_KEYS + ("w1ffn",):
        srcT = t[(key + "s") if key != "w1ffn" else "w1ffns"]
        cols = srcT.shape[1]
        dst = t["cast_loc"][:, off:off + cols]
        for r in range(2):
            jobs.append((srcT, dst, cols, r, 128, key == "w1ffn"))
        off += cols
    gathers = [(t["cast_loc"], t["cast_full"])]

    ld, cv, st = sp.get(), sp.get(), sp.get()
    f0 = pes.enter_context(nc.sbuf_tensor("caf0", [128, 2048], F32))
    f1 = pes.enter_context(nc.sbuf_tensor("caf1", [128, 2048], F32))
    b0 = pes.enter_context(nc.sbuf_tensor("cab0", [128, 2048], BF16))
    b1 = pes.enter_context(nc.sbuf_tensor("cab1", [128, 2048], BF16))
    col = pes.enter_context(nc.sbuf_tensor("cacol", [128, 4], F32))
    sgn = pes.enter_context(nc.sbuf_tensor("casgn", [128, 2048], F32))
    if True:
        fb, ob = [f0, f1], [b0, b1]
        ld0, cv0, st0 = ld.n, cv.n, st.n
        done_at = []
        acc = cv0
        for (_s, _d, _c, _r, _rw, _b) in jobs:
            acc += 3 if _b else 1
            done_at.append(acc)

        @blk.sync
        def _(sync):
            for i, (src, dst, cols, r, rw, _) in enumerate(jobs):
                if i >= 2:
                    sync.wait_ge(cv.h, done_at[i - 2])
                ld.dma(sync.dma_start(out=fb[i % 2][0:rw, 0:cols],
                                      in_=src[r * 128:r * 128 + rw, :]))

        @blk.vector
        def _(vector):
            for i, (src, dst, cols, r, rw, binz) in enumerate(jobs):
                vector.wait_ge(ld.h, ld0 + 16 * (i + 1))
                if i >= 2:
                    vector.wait_ge(st.h, st0 + 16 * (i - 1))
                if binz:
                    r_ = vector.tensor_reduce(col[:, 0:1], fb[i % 2][:, 0:cols], AX,
                                              ALU.add, apply_absolute_value=True)
                    cv.inc(r_)
                    vector.wait_ge(cv.h, cv.n)
                    t_ = vector.tensor_scalar(col[:, 2:3], col[:, 0:1], 1.0 / cols, None,
                                              op0=ALU.mult)
                    cv.inc(t_)
                    vector.wait_ge(cv.h, cv.n)
                    vector.tensor_scalar(sgn[:, 0:cols], fb[i % 2][:, 0:cols], 0.0, None,
                                         op0=ALU.is_ge)
                    vector.tensor_scalar(sgn[:, 0:cols], sgn[:, 0:cols], 2.0, -1.0,
                                         op0=ALU.mult, op1=ALU.add)
                    vector.tensor_scalar(ob[i % 2][:, 0:cols], sgn[:, 0:cols],
                                         col[:, 2:3], None, op0=ALU.mult)
                else:
                    vector.tensor_copy(ob[i % 2][:, 0:cols], fb[i % 2][:, 0:cols])
                cv.inc(vector.engine_nop())
                assert cv.n == done_at[i]

        @blk.gpsimd
        def _(gpsimd):
            for i, (src, dst, cols, r, rw, _) in enumerate(jobs):
                gpsimd.wait_ge(cv.h, done_at[i])
                st.dma(gpsimd.dma_start(out=dst[r * 128:r * 128 + rw, :],
                                        in_=ob[i % 2][0:rw, 0:cols]))
            gpsimd.wait_ge(st.h, st.n)
            for locT, fullT in gathers:
                cc.inc(gpsimd.collective_compute(
                    "AllGather", ALU.bypass, replica_groups=[list(range(8))],
                    ins=[locT.opt()], outs=[fullT.opt()]))
    sp.put(ld, cv, st)


def p_wgen(nc, pes, sp, t, blk, cc):
    jobs = ([(t["gidxs"], t["wgu_loc"][:, 0:IFF], IFF, r, 128) for r in range(2)]
            + [(t["uidxs"], t["wgu_loc"][:, IFF:2 * IFF], IFF, r, 128) for r in range(2)]
            + [(t["didxs"], t["wd_loc"], 2048, r, (128 if r < 5 else 48))
               for r in range(6)])
    gathers = [(t["wgu_loc"], t["wgu_full"]), (t["wd_loc"], t["wd_bf"])]

    ld, pre, cv, st = sp.get(), sp.get(), sp.get(), sp.get()
    i0 = pes.enter_context(nc.sbuf_tensor("wgi0", [128, IFF], I32))
    i1 = pes.enter_context(nc.sbuf_tensor("wgi1", [128, IFF], I32))
    fbuf = pes.enter_context(nc.sbuf_tensor("wgf", [128, IFF], BF16))
    term = pes.enter_context(nc.sbuf_tensor("wgt", [128, IFF], BF16))
    a0 = pes.enter_context(nc.sbuf_tensor("wga0", [128, IFF], BF16))
    a1 = pes.enter_context(nc.sbuf_tensor("wga1", [128, IFF], BF16))
    lut_s = pes.enter_context(nc.sbuf_tensor("wgl", [1, 16], F32))
    ones_r = pes.enter_context(nc.sbuf_tensor("wgon", [1, 128], F32))
    lut_ps = pes.enter_context(nc.psum_tensor("wglp", [128, 16], F32))
    lut_bc = pes.enter_context(nc.sbuf_tensor("wglb", [128, 16], F32))
    if True:
        ib, ab = [i0, i1], [a0, a1]
        ld0, pre0, cv0, st0 = ld.n, pre.n, cv.n, st.n

        @blk.sync
        def _(sync):
            pre.dma(sync.dma_start(out=lut_s[0:1, :], in_=t["lut"][:, :]))
            for i, (src, dst, cols, r, rw) in enumerate(jobs):
                if i >= 2:
                    sync.wait_ge(cv.h, cv0 + i - 1)
                ld.dma(sync.dma_start(out=ib[i % 2][0:rw, 0:cols],
                                      in_=src[r * 128:r * 128 + rw, :]))

        @blk.tensor
        def _(tensor):
            tensor.wait_ge(pre.h, pre0 + 17)
            pre.n = pre0 + 17
            pre.inc(tensor.matmul(lut_ps[:, :], ones_r[0:1, :], lut_s[0:1, :],
                                  start=True, stop=True))
            pre.n += 1

        @blk.vector
        def _(vector):
            vector.memset(ones_r[0:1, :], 1.0)
            vector.engine_nop().then_inc(pre.h, 1)
            vector.wait_ge(pre.h, pre0 + 18)
            vector.tensor_copy(lut_bc[:, :], lut_ps[:, :]).then_inc(pre.h, 1)
            vector.wait_ge(pre.h, pre0 + 19)
            for i, (src, dst, cols, r, rw) in enumerate(jobs):
                vector.wait_ge(ld.h, ld0 + 16 * (i + 1))
                if i >= 2:
                    vector.wait_ge(st.h, st0 + 16 * (i - 1))
                a = ab[i % 2]
                vector.tensor_copy(fbuf[0:rw, 0:cols], ib[i % 2][0:rw, 0:cols])
                vector.tensor_scalar(a[0:rw, 0:cols], fbuf[0:rw, 0:cols], 0.0,
                                     lut_bc[0:rw, 0:1], op0=ALU.is_equal, op1=ALU.mult)
                for j in range(1, 16):
                    vector.tensor_scalar(term[0:rw, 0:cols], fbuf[0:rw, 0:cols], float(j),
                                         lut_bc[0:rw, j:j + 1], op0=ALU.is_equal, op1=ALU.mult)
                    vector.tensor_add(a[0:rw, 0:cols], a[0:rw, 0:cols], term[0:rw, 0:cols])
                cv.inc(vector.engine_nop())

        @blk.gpsimd
        def _(gpsimd):
            for i, (src, dst, cols, r, rw) in enumerate(jobs):
                gpsimd.wait_ge(cv.h, cv0 + i + 1)
                st.dma(gpsimd.dma_start(out=dst[r * 128:r * 128 + rw, :],
                                        in_=ab[i % 2][0:rw, 0:cols]))
            gpsimd.wait_ge(st.h, st.n)
            for locT, fullT in gathers:
                cc.inc(gpsimd.collective_compute(
                    "AllGather", ALU.bypass, replica_groups=[list(range(8))],
                    ins=[locT.opt()], outs=[fullT.opt()]))
    sp.put(ld, pre, cv, st)


def p_consts(nc, pes, sp, t, g, blk):
    ld, cv, ev = sp.get(), sp.get(), sp.get()
    if True:
        ld0, cv0, ev0 = ld.n, cv.n, ev.n

        @blk.sync
        def _(sync):
            for dst, src in [(g["ident"], t["ident"]), (g["tri"], t["tri"]),
                             (g["bq"], t["bq0"]), (g["bk"], t["bk0"]),
                             (g["bv"], t["bv0"]), (g["bo_r"], t["bo0"]),
                             (g["b1_r"], t["b1"])]:
                ld.dma(sync.dma_start(out=dst[:, :], in_=src[:, :]))

        @blk.vector
        def _(vector):
            vector.memset(g["ones_f"][:, :], 1.0)
            vector.memset(g["ones_r"][0:1, :], 1.0)
            cv.inc(vector.engine_nop())

        @blk.tensor
        def _(tensor):
            tensor.wait_ge(ld.h, ld.n)
            tensor.wait_ge(cv.h, cv0 + 1)
    sp.put(ld, cv, ev)


def p_layernorm(nc, pes, sp, g, name, x_src, xnT_dst, blk=None):
    ld, sqs, sts, rsd, app, tp, ev, out, vch = [sp.get() for _ in range(9)]
    from contextlib import ExitStack as _ES
    _own = blk is None
    _pes = pes if pes is not None else _ES()
    if _own:
        blk = _pes.enter_context(nc.Block(name))
    x0 = _pes.enter_context(nc.sbuf_tensor(name + "x0", [128, D], F32))
    x1 = _pes.enter_context(nc.sbuf_tensor(name + "x1", [128, D], F32))
    x2 = _pes.enter_context(nc.sbuf_tensor(name + "x2", [128, D], F32))
    x3 = _pes.enter_context(nc.sbuf_tensor(name + "x3", [128, D], F32))
    sqb = _pes.enter_context(nc.sbuf_tensor(name + "sq", [128, D], F32))
    xtmp = _pes.enter_context(nc.sbuf_tensor(name + "xt", [128, D], F32))
    col = _pes.enter_context(nc.sbuf_tensor(name + "c", [128, 8], F32))
    n0 = _pes.enter_context(nc.sbuf_tensor(name + "n0", [128, D], BF16))
    n1 = _pes.enter_context(nc.sbuf_tensor(name + "n1", [128, D], BF16))
    p0 = _pes.enter_context(nc.psum_tensor(name + "p0", [128, 128], BF16))
    p1 = _pes.enter_context(nc.psum_tensor(name + "p1", [128, 128], BF16))
    o0 = _pes.enter_context(nc.sbuf_tensor(name + "o0", [128, NCH, 128], BF16))
    o1 = _pes.enter_context(nc.sbuf_tensor(name + "o1", [128, NCH, 128], BF16))
    if True:
        xb, xnb, tps, ob = [x0, x1, x2, x3], [n0, n1], [p0, p1], [o0, o1]
        c0 = {s: s.n for s in (ld, sqs, sts, rsd, app, tp, ev, out, vch)}
        vch.n += 6 * NT128

        @blk.sync
        def _(sync):
            for i in range(NT128):
                if i >= 4:
                    sync.wait_ge(app.h, c0[app] + i - 3)
                ld.dma(sync.dma_start(out=xb[i % 4][:, :],
                                      in_=x_src[i * 128:(i + 1) * 128, :]))

        @blk.scalar
        def _(scalar):
            for i in range(NT128):
                scalar.wait_ge(ld.h, c0[ld] + 16 * (i + 1))
                if i >= 1:
                    scalar.wait_ge(sts.h, c0[sts] + i)
                sqs.inc(scalar.activation(sqb[:, :], xb[i % 4][:, :], AF.Square))
                scalar.wait_ge(sts.h, c0[sts] + i + 1)
                rsd.inc(scalar.activation(col[:, 7:8], col[:, 6:7], AF.Sqrt))
                if i >= 1:
                    if i >= 3:
                        scalar.wait_ge(out.h, c0[out] + 16 * (i - 2))
                    for c in range(NCH):
                        k = (i - 1) * NCH + c
                        scalar.wait_ge(tp.h, c0[tp] + k + 1)
                        ev.inc(scalar.copy(ob[(i - 1) % 2][:, c, :], tps[c % 2][:, :]))
            scalar.wait_ge(out.h, c0[out] + 16 * (NT128 - 2))
            for c in range(NCH):
                k = (NT128 - 1) * NCH + c
                scalar.wait_ge(tp.h, c0[tp] + k + 1)
                ev.inc(scalar.copy(ob[(NT128 - 1) % 2][:, c, :], tps[c % 2][:, :]))

        @blk.vector
        def _(vector):
            vk = [c0[vch]]

            def step(instr):
                vk[0] += 1
                instr.then_inc(vch.h, 1)
                vector.wait_ge(vch.h, vk[0])
            for i in range(NT128):
                vector.wait_ge(sqs.h, c0[sqs] + i + 1)
                step(vector.tensor_reduce(col[:, 1:2], xb[i % 4][:, :], AX, ALU.add))
                step(vector.tensor_reduce(col[:, 0:1], sqb[:, :], AX, ALU.add))
                step(vector.tensor_scalar(col[:, 2:3], col[:, 1:2], -1.0 / D, None, op0=ALU.mult))
                step(vector.tensor_tensor(col[:, 3:4], col[:, 2:3], col[:, 2:3], ALU.mult))
                step(vector.tensor_scalar(col[:, 4:5], col[:, 0:1], 1.0 / D, 1e-5,
                                          op0=ALU.mult, op1=ALU.add))
                step(vector.tensor_tensor(col[:, 5:6], col[:, 4:5], col[:, 3:4], ALU.subtract))
                sts.inc(vector.reciprocal(col[:, 6:7], col[:, 5:6]))
                vector.wait_ge(rsd.h, c0[rsd] + i + 1)
                if i >= 2:
                    vector.wait_ge(tp.h, c0[tp] + NCH * (i - 1))
                vector.tensor_scalar(xtmp[:, :], xb[i % 4][:, :],
                                     col[:, 2:3], None, op0=ALU.add)
                app.inc(vector.tensor_scalar(xnb[i % 2][:, :], xtmp[:, :],
                                             col[:, 7:8], None, op0=ALU.mult))

        @blk.tensor
        def _(tensor):
            for i in range(NT128):
                tensor.wait_ge(app.h, c0[app] + i + 1)
                for c in range(NCH):
                    k = i * NCH + c
                    if k >= 2:
                        tensor.wait_ge(ev.h, c0[ev] + k - 1)
                    tp.inc(tensor.transpose(tps[c % 2][:, :],
                                            xnb[i % 2][:, c * 128:(c + 1) * 128],
                                            g["ident"][:, :]))

        @blk.gpsimd
        def _(gpsimd):
            for i in range(NT128):
                gpsimd.wait_ge(ev.h, c0[ev] + NCH * (i + 1))
                out.dma(gpsimd.dma_start(
                    out=xnT_dst.rearrange("(c p) t -> p c t", p=128)[:, :, i * 128:(i + 1) * 128],
                    in_=ob[i % 2][:, :, :]))
            gpsimd.wait_ge(out.h, out.n)
    if _own and pes is None:
        _pes.close()
    sp.put(ld, sqs, sts, rsd, app, tp, ev, out, vch)


def p_proj(nc, pes, sp, g, name, w_bf, xnT_src, out_dst, o_chunks,
           transpose_w, bias_col=None, scale=None, norm_base=None,
           token_major=False, ccw=None):
    ocols = o_chunks * 128
    sems = [sp.get() for _ in range(9)]
    wl, xl, mm, evd, sq, nmm, nmax, tev, out = sems
    tpt = sp.get()
    tiles = [(tt, oc) for tt in range(NT512) for oc in range(o_chunks)]
    with nc.Block(name) as blk, \
         nc.sbuf_tensor(name + "w", [128, NCH, ocols], BF16) as w_sb, \
         nc.sbuf_tensor(name + "xa", [128, NCH, 512], BF16) as xa, \
         nc.sbuf_tensor(name + "xb", [128, NCH, 512], BF16) as xbt, \
         nc.psum_tensor(name + "p0", [128, 512], F32) as ps0, \
         nc.psum_tensor(name + "p1", [128, 512], F32) as ps1, \
         nc.sbuf_tensor(name + "e0", [128, 512], BF16) as e0, \
         nc.sbuf_tensor(name + "e1", [128, 512], BF16) as e1, \
         nc.sbuf_tensor(name + "nq", [128, 512], F32) as nsq, \
         nc.psum_tensor(name + "np", [1, 512], F32) as nps, \
         nc.psum_tensor(name + "t0", [128, 128], BF16) as tp0, \
         nc.psum_tensor(name + "t1", [128, 128], BF16) as tp1, \
         nc.sbuf_tensor(name + "b0", [128, 128], BF16) as tb0, \
         nc.sbuf_tensor(name + "b1", [128, 128], BF16) as tb1:
        xn_sb, ps, eb = [xa, xbt], [ps0, ps1], [e0, e1]
        tps, tb = [tp0, tp1], [tb0, tb1]
        c0 = {s: s.n for s in sems + [tpt]}
        bias = g[bias_col] if bias_col else None

        @blk.sync
        def _(sync):
            for s_ in (ccw or ()):
                sync.wait_ge(s_.h, s_.n)
            for c in range(NCH):
                if transpose_w:
                    wl.dma(sync.dma_start(out=w_sb[:, c, :],
                                          in_=w_bf[:, c * 128:(c + 1) * 128],
                                          transpose=True))
                else:
                    wl.dma(sync.dma_start(out=w_sb[:, c, :],
                                          in_=w_bf[c * 128:(c + 1) * 128, :]))
            for tt in range(NT512):
                if tt >= 2:
                    sync.wait_ge(mm.h, c0[mm] + (tt - 1) * o_chunks)
                xl.dma(sync.dma_start(
                    out=xn_sb[tt % 2][:, :, :],
                    in_=xnT_src.rearrange("(c p) t -> p c t", p=128)[:, :, tt * 512:(tt + 1) * 512]))

        @blk.tensor
        def _(tensor):
            tensor.wait_ge(wl.h, wl.n)
            for i, (tt, oc) in enumerate(tiles):
                tensor.wait_ge(xl.h, c0[xl] + 16 * (tt + 1))
                if i >= 2:
                    tensor.wait_ge(evd.h, c0[evd] + i - 1)
                for c in range(NCH):
                    m = tensor.matmul(ps[i % 2][:, :],
                                      w_sb[:, c, oc * 128:(oc + 1) * 128],
                                      xn_sb[tt % 2][:, c, :],
                                      start=(c == 0), stop=(c == NCH - 1))
                    if c == NCH - 1:
                        mm.inc(m)
                if norm_base is not None and i >= 1:
                    tensor.wait_ge(sq.h, c0[sq] + i)
                    if i >= 2:
                        tensor.wait_ge(nmax.h, c0[nmax] + i - 1)
                    nmm.inc(tensor.matmul(nps[0:1, :], g["ones_f"][:, 0:1], nsq[:, :],
                                          start=True, stop=True))
                if token_major and i >= 1:
                    tensor.wait_ge(evd.h, c0[evd] + i)
                    for q in range(4):
                        k = (i - 1) * 4 + q
                        if k >= 2:
                            tensor.wait_ge(tev.h, c0[tev] + k - 1)
                        tpt.inc(tensor.transpose(tps[q % 2][:, :],
                                                 eb[(i - 1) % 2][:, q * 128:(q + 1) * 128],
                                                 g["ident"][:, :]))
            if norm_base is not None:
                i = len(tiles)
                tensor.wait_ge(sq.h, c0[sq] + i)
                tensor.wait_ge(nmax.h, c0[nmax] + i - 1)
                nmm.inc(tensor.matmul(nps[0:1, :], g["ones_f"][:, 0:1], nsq[:, :],
                                      start=True, stop=True))
            if token_major:
                i = len(tiles)
                for q in range(4):
                    k = (i - 1) * 4 + q
                    tensor.wait_ge(tev.h, c0[tev] + k - 1)
                    tpt.inc(tensor.transpose(tps[q % 2][:, :],
                                             eb[(i - 1) % 2][:, q * 128:(q + 1) * 128],
                                             g["ident"][:, :]))

        @blk.vector
        def _(vector):
            for i, (tt, oc) in enumerate(tiles):
                vector.wait_ge(mm.h, c0[mm] + i + 1)
                if not token_major and i >= 2:
                    vector.wait_ge(out.h, c0[out] + 16 * (i - 1))
                if norm_base is not None and i >= 2:
                    vector.wait_ge(sq.h, c0[sq] + i - 1)
                if token_major and i >= 2:
                    vector.wait_ge(tpt.h, c0[tpt] + 4 * (i - 1))
                if bias is not None and scale is not None:
                    evd.inc(vector.tensor_scalar(eb[i % 2][:, :], ps[i % 2][:, :],
                                                 bias[:, oc:oc + 1], float(scale),
                                                 op0=ALU.add, op1=ALU.mult))
                elif bias is not None:
                    evd.inc(vector.tensor_scalar(eb[i % 2][:, :], ps[i % 2][:, :],
                                                 bias[:, oc:oc + 1], None, op0=ALU.add))
                else:
                    evd.inc(vector.tensor_copy(eb[i % 2][:, :], ps[i % 2][:, :]))
                if norm_base is not None and i >= 1:
                    vector.wait_ge(nmm.h, c0[nmm] + i)
                    slot = norm_base + i - 1
                    nmax.inc(vector.tensor_reduce(g["normcol"][0:1, slot:slot + 1],
                                                  nps[0:1, :], AX, ALU.max))
            if norm_base is not None:
                i = len(tiles)
                vector.wait_ge(nmm.h, c0[nmm] + i)
                slot = norm_base + i - 1
                nmax.inc(vector.tensor_reduce(g["normcol"][0:1, slot:slot + 1],
                                              nps[0:1, :], AX, ALU.max))

        @blk.scalar
        def _(scalar):
            for i, (tt, oc) in enumerate(tiles):
                if norm_base is not None:
                    scalar.wait_ge(evd.h, c0[evd] + i + 1)
                    if i >= 1:
                        scalar.wait_ge(nmm.h, c0[nmm] + i)
                    sq.inc(scalar.activation(nsq[:, :], eb[i % 2][:, :], AF.Square))
                if token_major and i >= 1:
                    for q in range(4):
                        k = (i - 1) * 4 + q
                        scalar.wait_ge(tpt.h, c0[tpt] + k + 1)
                        if k >= 2:
                            scalar.wait_ge(out.h, c0[out] + 16 * (k - 1))
                        tev.inc(scalar.copy(tb[q % 2][:, :], tps[q % 2][:, :]))
            if token_major:
                i = len(tiles)
                for q in range(4):
                    k = (i - 1) * 4 + q
                    scalar.wait_ge(tpt.h, c0[tpt] + k + 1)
                    scalar.wait_ge(out.h, c0[out] + 16 * (k - 1))
                    tev.inc(scalar.copy(tb[q % 2][:, :], tps[q % 2][:, :]))

        @blk.gpsimd
        def _(gpsimd):
            for i, (tt, oc) in enumerate(tiles):
                if token_major:
                    for q in range(4):
                        k = i * 4 + q
                        gpsimd.wait_ge(tev.h, c0[tev] + k + 1)
                        out.dma(gpsimd.dma_start(
                            out=out_dst[tt * 512 + q * 128:tt * 512 + (q + 1) * 128,
                                        oc * 128:(oc + 1) * 128],
                            in_=tb[q % 2][:, :]))
                else:
                    gpsimd.wait_ge(evd.h, c0[evd] + i + 1)
                    out.dma(gpsimd.dma_start(
                        out=out_dst[oc * 128:(oc + 1) * 128, tt * 512:(tt + 1) * 512],
                        in_=eb[i % 2][:, :]))
            gpsimd.wait_ge(out.h, out.n)
    sp.put(*sems, tpt)


def p_negm(nc, pes, sp, g):
    cv = sp.get()
    with nc.Block("pnegm") as blk, \
         nc.sbuf_tensor("ngc", [1, 8], F32) as c, \
         nc.psum_tensor("ngp", [128, 1], F32) as psb:
        cv0 = cv.n

        @blk.vector
        def _(vector):
            cv.inc(vector.tensor_reduce(c[0:1, 0:1], g["normcol"][0:1, 0:128], AX, ALU.max))
            vector.wait_ge(cv.h, cv0 + 1)
            cv.inc(vector.tensor_reduce(c[0:1, 1:2], g["normcol"][0:1, 128:256], AX, ALU.max))
            vector.wait_ge(cv.h, cv0 + 3)
            cv.inc(vector.tensor_tensor(c[0:1, 4:5], c[0:1, 2:3], c[0:1, 3:4], ALU.mult))
            vector.wait_ge(cv.h, cv0 + 4)
            cv.inc(vector.tensor_scalar(c[0:1, 5:6], c[0:1, 4:5], -1.0, -1.0,
                                        op0=ALU.mult, op1=ALU.add))
            vector.wait_ge(cv.h, cv0 + 6)
            cv.inc(vector.tensor_copy(g["negm"][:, 0:1], psb[:, 0:1]))

        @blk.scalar
        def _(scalar):
            scalar.wait_ge(cv.h, cv0 + 2)
            scalar.activation(c[0:1, 2:3], c[0:1, 0:1], AF.Sqrt)
            cv.inc(scalar.activation(c[0:1, 3:4], c[0:1, 1:2], AF.Sqrt))

        @blk.tensor
        def _(tensor):
            tensor.wait_ge(cv.h, cv0 + 5)
            cv.inc(tensor.matmul(psb[:, 0:1], g["ones_r"][0:1, :], c[0:1, 5:6],
                                 start=True, stop=True))
    sp.put(cv)


def p_attn(nc, pes, sp, g, name, qT, kT, vtok, attnT_dst, kv_heads, negm_imm):
    grp = H // kv_heads
    sems = [sp.get() for _ in range(10)]
    ld, smm, mskd, expd, pvd, nrm, tpt, tev, outst, vrdy = sems

    sched = []
    cs = mk = 0
    for h in range(H):
        for T in range(NT512):
            ch = [(c, max(0, c * 128 - T * 512)) for c in range(4 * T + 4)]
            tile = h * NT512 + T
            pairs = []
            for pi in range(len(ch) // 2):
                c0_, c1_ = ch[2 * pi], ch[2 * pi + 1]
                diag = c0_[0] * 128 >= T * 512 or c1_[0] * 128 >= T * 512
                pairs.append((pi, c0_, c1_, diag))
            nd = sum(1 for c, _ in ch if c * 128 >= T * 512)
            sched.append(dict(h=h, T=T, ch=ch, pairs=pairs, cs0=cs, mk0=mk,
                              base=tile * 4, tile=tile))
            cs += len(ch)
            mk += nd
    pair_last_cs = []
    for e in sched:
        for pi in range(len(e["pairs"])):
            pair_last_cs.append(e["cs0"] + 2 * pi + 1)

    from contextlib import ExitStack as _ES
    _pes = _ES()
    blk = _pes.enter_context(nc.Block(name))
    k0 = _pes.enter_context(nc.sbuf_tensor(name + "k0", [128, S], BF16))
    k1 = _pes.enter_context(nc.sbuf_tensor(name + "k1", [128, S], BF16))
    q0 = _pes.enter_context(nc.sbuf_tensor(name + "q0", [128, S], BF16))
    q1 = _pes.enter_context(nc.sbuf_tensor(name + "q1", [128, S], BF16))
    v0 = _pes.enter_context(nc.sbuf_tensor(name + "v0", [128, 32, 132], BF16))
    v1 = _pes.enter_context(nc.sbuf_tensor(name + "v1", [128, 32, 132], BF16))
    pt0 = _pes.enter_context(nc.sbuf_tensor(name + "pt0", [128, 32, 512], BF16))
    pt1 = _pes.enter_context(nc.sbuf_tensor(name + "pt1", [128, 32, 512], BF16))
    s0 = _pes.enter_context(nc.psum_tensor(name + "s0", [128, 1024], F32))
    s1 = _pes.enter_context(nc.psum_tensor(name + "s1", [128, 1024], F32))
    s2 = _pes.enter_context(nc.psum_tensor(name + "s2", [128, 1024], F32))
    o20 = _pes.enter_context(nc.psum_tensor(name + "o20", [128, 2, 132], F32))
    o21 = _pes.enter_context(nc.psum_tensor(name + "o21", [128, 2, 132], F32))
    aot = _pes.enter_context(nc.sbuf_tensor(name + "ao", [128, 8, 128], BF16))
    linv = _pes.enter_context(nc.sbuf_tensor(name + "li", [128, 4], F32))
    negm_t = _pes.enter_context(nc.sbuf_tensor(name + "nm", [128, 1], F32))
    if True:
        kb, qb, vb = [k0, k1], [q0, q1], [v0, v1]
        sps, o2t = [s0, s1, s2], [o20, o21]
        pt_sb = [pt0, pt1]
        c0 = {s: s.n for s in sems}
        neg_ap = negm_t[:, 0:1] if negm_imm is not None else g["negm"][:, 0:1]

        def o2ap(jj):
            return o2t[jj // 2][:, jj % 2, 0:129]

        @blk.sync
        def _(sync):
            for h in range(H):
                kv = h // grp
                if h >= 2:
                    sync.wait_ge(pvd.h, c0[pvd] + 32 * (h - 1))
                ld.dma(sync.dma_start(out=kb[h % 2][:, :],
                                      in_=kT[kv * 128:(kv + 1) * 128, :]))
                ld.dma(sync.dma_start(out=qb[h % 2][:, :],
                                      in_=qT[h * 128:(h + 1) * 128, :]))
                ld.dma(sync.dma_start(
                    out=vb[h % 2][:, :, 0:128],
                    in_=vtok.rearrange("(c p) o -> p c o", p=128)[:, :, kv * 128:(kv + 1) * 128]))

        @blk.tensor
        def _(tensor):
            tensor.wait_ge(vrdy.h, c0[vrdy] + 1)
            pair_g = [0]
            for ent in sched:
                h, T, ch, pairs = ent["h"], ent["T"], ent["ch"], ent["pairs"]
                tb = ent["tile"] % 2
                if T == 0:
                    tensor.wait_ge(ld.h, c0[ld] + 48 * (h + 1))

                def emit_S(pi):
                    pg = pair_g[0] + pi
                    if pg >= 3:
                        prev_last_cs = pair_last_cs[pg - 3]
                        tensor.wait_ge(expd.h, c0[expd] + prev_last_cs + 1)
                    for half in range(2):
                        c, qlo = ch[2 * pi + half]
                        smm.inc(tensor.matmul(
                            sps[pg % 3][:, half * 512 + qlo:half * 512 + 512],
                            kb[h % 2][:, c * 128:(c + 1) * 128],
                            qb[h % 2][:, T * 512 + qlo:(T + 1) * 512],
                            start=True, stop=True))

                def emit_PV(pi):
                    _, _, c1_, _ = pairs[pi]
                    last_cs = ent["cs0"] + 2 * pi + 1
                    tensor.wait_ge(expd.h, c0[expd] + last_cs + 1)
                    for half in range(2):
                        c, qlo = ch[2 * pi + half]
                        if c == 0 and ent["base"] >= 4:
                            tensor.wait_ge(nrm.h, c0[nrm] + ent["base"])
                        for jj in range(max(0, c - 4 * T), 4):
                            last = (c == 4 * T + jj)
                            m = tensor.matmul(o2ap(jj),
                                              pt_sb[tb][:, c, jj * 128:(jj + 1) * 128],
                                              vb[h % 2][:, c, 0:129],
                                              start=(c == 0 and jj % 2 == 0),
                                              stop=last)
                            if last:
                                pvd.inc(m)

                npair = len(pairs)

                emit_S(0)
                for pi in range(npair):
                    if pi + 1 < npair:
                        emit_S(pi + 1)
                    emit_PV(pi)
                pair_g[0] += npair

        @blk.vector
        def _(vector):
            vr = [0]
            vector.memset(v0[:, :, 128:129], 1.0)
            vector.memset(v1[:, :, 128:129], 1.0)
            if negm_imm is not None:
                vector.memset(negm_t[:, 0:1], float(negm_imm))
            vrdy.inc(vector.engine_nop())
            vrdy.n += 4 * H * NT512
            pair_g = [0]
            for ent in sched:
                h, T, ch, pairs = ent["h"], ent["T"], ent["ch"], ent["pairs"]
                mki = ent["mk0"]
                for idx, (c, qlo) in enumerate(ch):
                    if c * 128 >= T * 512:
                        cs_g = ent["cs0"] + idx
                        pg = pair_g[0] + idx // 2
                        half = idx % 2
                        vector.wait_ge(smm.h, c0[smm] + cs_g + 1)
                        mskd.inc(vector.tensor_tensor(
                            sps[pg % 3][:, half * 512 + qlo:half * 512 + qlo + 128],
                            sps[pg % 3][:, half * 512 + qlo:half * 512 + qlo + 128],
                            g["tri"][:, :], ALU.add))
                        mki += 1
                pair_g[0] += len(pairs)
                for jj in range(4):
                    pq = ent["base"] + jj
                    vector.wait_ge(pvd.h, c0[pvd] + ent["base"] + 4)
                    if pq >= 8:
                        vector.wait_ge(outst.h, c0[outst] + 16 * (pq - 7))
                    vector.reciprocal(linv[:, jj:jj + 1],
                                      o2t[jj // 2][:, jj % 2, 128:129]).then_inc(vrdy.h, 1)
                    vr[0] += 1
                    vector.wait_ge(vrdy.h, c0[vrdy] + 1 + vr[0])
                    nrm.inc(vector.tensor_scalar(aot[:, pq % 8, :],
                                                 o2t[jj // 2][:, jj % 2, 0:128],
                                                 linv[:, jj:jj + 1], None,
                                                 op0=ALU.mult))

        @blk.scalar
        def _(scalar):
            scalar.wait_ge(vrdy.h, c0[vrdy] + 1)
            pair_g = [0]
            for ent in sched:
                h, T, ch, pairs = ent["h"], ent["T"], ent["ch"], ent["pairs"]
                tb = ent["tile"] % 2
                mki = ent["mk0"]
                if ent["tile"] >= 2:
                    scalar.wait_ge(pvd.h, c0[pvd] + 4 * (ent["tile"] - 1))
                for (pi, (ca, qla), (cb, qlb), diag) in pairs:
                    pg = pair_g[0] + pi
                    last_cs = ent["cs0"] + 2 * pi + 1
                    scalar.wait_ge(smm.h, c0[smm] + last_cs + 1)
                    if diag:
                        if ca * 128 >= T * 512:
                            mki += 1
                        if cb * 128 >= T * 512:
                            mki += 1
                        scalar.wait_ge(mskd.h, c0[mskd] + mki)
                        scalar.activation(pt_sb[tb][:, ca, qla:512],
                                          sps[pg % 3][:, qla:512],
                                          AF.Exp, bias=neg_ap).then_inc(expd.h, 1)
                        expd.n += 1
                        scalar.activation(pt_sb[tb][:, cb, qlb:512],
                                          sps[pg % 3][:, 512 + qlb:1024],
                                          AF.Exp, bias=neg_ap).then_inc(expd.h, 1)
                        expd.n += 1
                    else:
                        expd.n += 2
                        scalar.activation(pt_sb[tb][:, 2 * pi:2 * pi + 2, 0:512],
                                          sps[pg % 3][:, 0:1024],
                                          AF.Exp, bias=neg_ap).then_inc(expd.h, 2)
                pair_g[0] += len(pairs)

        @blk.gpsimd
        def _(gpsimd):
            for ent in sched:
                h, T, tile = ent["h"], ent["T"], ent["tile"]
                for jj in range(4):
                    pq = ent["base"] + jj
                    gpsimd.wait_ge(nrm.h, c0[nrm] + pq + 1)
                    outst.dma(gpsimd.dma_start(
                        out=attnT_dst[T * 512 + jj * 128:T * 512 + (jj + 1) * 128,
                                      h * 128:(h + 1) * 128],
                        in_=aot[:, pq % 8, :]))
            gpsimd.wait_ge(outst.h, outst.n)
    _pes.close()
    sp.put(*sems)


def p_ffn1(nc, pes, sp, g, t, ccw=None):
    iblocks = []
    c = 0
    while c < ICH:
        iblocks.append((c, min(8, ICH - c)))
        c += 8
    sems = [sp.get() for _ in range(7)]
    wl, xl, gmm, umm, sil, ev, out = sems
    with nc.Block("pffn1") as blk, \
         nc.sbuf_tensor("f1wg", [128, NCH, 1024], BF16) as wg_sb, \
         nc.sbuf_tensor("f1wu", [128, NCH, 1024], BF16) as wu_sb, \
         nc.sbuf_tensor("f1x0", [128, NCH, 512], BF16) as xa, \
         nc.sbuf_tensor("f1x1", [128, NCH, 512], BF16) as xb, \
         nc.psum_tensor("f1g0", [128, 512], F32) as g0, \
         nc.psum_tensor("f1g1", [128, 512], F32) as g1, \
         nc.psum_tensor("f1u0", [128, 512], F32) as u0, \
         nc.psum_tensor("f1u1", [128, 512], F32) as u1, \
         nc.sbuf_tensor("f1s", [128, 512], F32) as gs, \
         nc.sbuf_tensor("f1h0", [128, 512], BF16) as h0, \
         nc.sbuf_tensor("f1h1", [128, 512], BF16) as h1:
        xn_sb, gps, ups, hb = [xa, xb], [g0, g1], [u0, u1], [h0, h1]
        c0 = {s: s.n for s in sems}
        jobs = []
        for bi, (ic0, bn) in enumerate(iblocks):
            for tt in range(NT512):
                for icr in range(bn):
                    jobs.append((bi, ic0, bn, tt, icr))

        @blk.sync
        def _(sync):
            for s_ in (ccw or ()):
                sync.wait_ge(s_.h, s_.n)
            ji = 0
            for bi, (ic0, bn) in enumerate(iblocks):
                if bi >= 1:
                    prev_jobs = sum(b[1] for b in iblocks[:bi]) * NT512
                    sync.wait_ge(umm.h, c0[umm] + prev_jobs)
                for c in range(NCH):
                    wl.dma(sync.dma_start(
                        out=wg_sb[:, c, 0:bn * 128],
                        in_=t["wg_bf"][c * 128:(c + 1) * 128, ic0 * 128:(ic0 + bn) * 128]))
                    wl.dma(sync.dma_start(
                        out=wu_sb[:, c, 0:bn * 128],
                        in_=t["wu_bf"][c * 128:(c + 1) * 128, ic0 * 128:(ic0 + bn) * 128]))
                for tt in range(NT512):
                    gtile = bi * NT512 + tt
                    if gtile >= 2:
                        done_jobs = [j for j in range(len(jobs))
                                     if jobs[j][0] * NT512 + jobs[j][3] <= gtile - 2]
                        sync.wait_ge(umm.h, c0[umm] + len(done_jobs))
                    xl.dma(sync.dma_start(
                        out=xn_sb[gtile % 2][:, :, :],
                        in_=t["xnT"].rearrange("(c p) t -> p c t", p=128)[:, :, tt * 512:(tt + 1) * 512]))

        @blk.tensor
        def _(tensor):
            for i, (bi, ic0, bn, tt, icr) in enumerate(jobs):
                gtile = bi * NT512 + tt
                if icr == 0:
                    tensor.wait_ge(xl.h, c0[xl] + 16 * (gtile + 1))
                    tensor.wait_ge(wl.h, c0[wl] + 32 * (bi + 1))
                if i >= 2:
                    tensor.wait_ge(ev.h, c0[ev] + i - 1)
                for c in range(NCH):
                    m = tensor.matmul(gps[i % 2][:, :],
                                      wg_sb[:, c, icr * 128:(icr + 1) * 128],
                                      xn_sb[gtile % 2][:, c, :],
                                      start=(c == 0), stop=(c == NCH - 1))
                    if c == NCH - 1:
                        gmm.inc(m)
                for c in range(NCH):
                    m = tensor.matmul(ups[i % 2][:, :],
                                      wu_sb[:, c, icr * 128:(icr + 1) * 128],
                                      xn_sb[gtile % 2][:, c, :],
                                      start=(c == 0), stop=(c == NCH - 1))
                    if c == NCH - 1:
                        umm.inc(m)

        @blk.scalar
        def _(scalar):
            for i, job in enumerate(jobs):
                scalar.wait_ge(gmm.h, c0[gmm] + i + 1)
                if i >= 1:
                    scalar.wait_ge(ev.h, c0[ev] + i)
                sil.inc(scalar.activation(gs[:, :], gps[i % 2][:, :], AF.Silu))

        @blk.vector
        def _(vector):
            for i, job in enumerate(jobs):
                vector.wait_ge(sil.h, c0[sil] + i + 1)
                vector.wait_ge(umm.h, c0[umm] + i + 1)
                if i >= 2:
                    vector.wait_ge(out.h, c0[out] + 16 * (i - 1))
                ev.inc(vector.tensor_tensor(hb[i % 2][:, :], gs[:, :],
                                            ups[i % 2][:, :], ALU.mult))

        @blk.gpsimd
        def _(gpsimd):
            for i, (bi, ic0, bn, tt, icr) in enumerate(jobs):
                gpsimd.wait_ge(ev.h, c0[ev] + i + 1)
                ic = ic0 + icr
                out.dma(gpsimd.dma_start(
                    out=t["hT"][ic * 128:(ic + 1) * 128, tt * 512:(tt + 1) * 512],
                    in_=hb[i % 2][:, :]))
            gpsimd.wait_ge(out.h, out.n)
    sp.put(*sems)


def p_combine(nc, pes, sp, g, name, parts, resid_src, bias_bc_key, dst, ccw=()):
    sems = [sp.get() for _ in range(6)]
    wl, al, mm, ev, out, afl = sems
    with nc.Block(name) as blk, \
         nc.sbuf_tensor(name + "w0", [128, parts[0][2], 512], BF16) as w0, \
         nc.sbuf_tensor(name + "w1", [128, parts[1][2], 512], BF16) as w1, \
         nc.sbuf_tensor(name + "af0", [128, parts[0][2], 512], BF16) as af0, \
         nc.sbuf_tensor(name + "af1", [128, parts[0][2], 512], BF16) as af1, \
         nc.sbuf_tensor(name + "b0", [128, parts[1][2], 128], BF16) as a10, \
         nc.sbuf_tensor(name + "b1", [128, parts[1][2], 128], BF16) as a11, \
         nc.sbuf_tensor(name + "r0", [128, 512], F32) as r0, \
         nc.sbuf_tensor(name + "r1", [128, 512], F32) as r1, \
         nc.psum_tensor(name + "p0", [128, 512], F32) as p0, \
         nc.psum_tensor(name + "p1", [128, 512], F32) as p1, \
         nc.sbuf_tensor(name + "o0", [128, 512], F32) as ob0, \
         nc.sbuf_tensor(name + "o1", [128, 512], F32) as ob1, \
         nc.sbuf_tensor(name + "bb", [128, D], F32) as bias_bc:
        w_sb = [w0, w1]
        attw = [af0, af1]
        act_sb = [a10, a11]
        rb, ps, ob = [r0, r1], [p0, p1], [ob0, ob1]
        c0 = {s: s.n for s in sems}
        bias_row = g[bias_bc_key]
        n_ch_tot = sum(p[2] for p in parts)

        @blk.sync
        def _(sync):
            for s_ in ccw:
                sync.wait_ge(s_.h, s_.n)
            for oh in range(4):
                if oh >= 1:
                    sync.wait_ge(mm.h, c0[mm] + oh * NT128)
                for pi, (actT, w_bf, nch, trw) in enumerate(parts):
                    for c in range(nch):
                        if trw:
                            wl.dma(sync.dma_start(
                                out=w_sb[pi][:, c, :],
                                in_=w_bf[oh * 512:(oh + 1) * 512, c * 128:(c + 1) * 128],
                                transpose=True))
                        else:
                            wl.dma(sync.dma_start(
                                out=w_sb[pi][:, c, :],
                                in_=w_bf[c * 128:(c + 1) * 128, oh * 512:(oh + 1) * 512]))
                for tt in range(NT128):
                    i = oh * NT128 + tt
                    iw = oh * 8 + tt // 4
                    if tt % 4 == 0:
                        if iw >= 2:
                            sync.wait_ge(mm.h, c0[mm] + (iw - 1) * 4)
                        for c in range(parts[0][2]):
                            afl.dma(sync.dma_start(
                                out=attw[iw % 2][:, c, :],
                                in_=parts[0][0][(tt // 4) * 512 + oh * 0:
                                                (tt // 4) * 512 + 512,
                                                c * 128:(c + 1) * 128],
                                transpose=True))
                    if i >= 2:
                        sync.wait_ge(mm.h, c0[mm] + i - 1)
                    actT1 = parts[1][0]
                    al.dma(sync.dma_start(
                        out=act_sb[i % 2][:, :, :],
                        in_=actT1.rearrange("(c p) t -> p c t", p=128)[:, :, tt * 128:(tt + 1) * 128]))
                    al.dma(sync.dma_start(
                        out=rb[i % 2][:, :],
                        in_=resid_src[tt * 128:(tt + 1) * 128, oh * 512:(oh + 1) * 512]))

        @blk.tensor
        def _(tensor):
            for q in range(4):
                if q >= 2:
                    tensor.wait_ge(ev.h, c0[ev] + q - 1)
                ev.n += 1
                tensor.matmul(ps[q % 2][:, :], g["ones_r"][0:1, :],
                              bias_row[0:1, q * 512:(q + 1) * 512],
                              start=True, stop=True).then_inc(ev.h, 1)
            for oh in range(4):
                tensor.wait_ge(wl.h, c0[wl] + 16 * n_ch_tot * (oh + 1))
                for tt in range(NT128):
                    i = oh * NT128 + tt
                    iw = oh * 8 + tt // 4
                    tensor.wait_ge(al.h, c0[al] + 32 * (i + 1))
                    tensor.wait_ge(afl.h, c0[afl] + 16 * 16 * (iw + 1))
                    if i < 2:
                        tensor.wait_ge(ev.h, c0[ev] + 5)
                    if i >= 2:
                        tensor.wait_ge(ev.h, c0[ev] + 5 + i - 1)
                    first = True
                    for c in range(parts[0][2]):
                        m = tensor.matmul(ps[i % 2][:, :],
                                          attw[iw % 2][:, c, (tt % 4) * 128:(tt % 4) * 128 + 128],
                                          w_sb[0][:, c, :],
                                          start=first, stop=False)
                        first = False
                    for c in range(parts[1][2]):
                        last = (c == parts[1][2] - 1)
                        m = tensor.matmul(ps[i % 2][:, :],
                                          act_sb[i % 2][:, c, :],
                                          w_sb[1][:, c, :],
                                          start=False, stop=last)
                        if last:
                            mm.inc(m)

        @blk.vector
        def _(vector):
            for q in range(4):
                vector.wait_ge(ev.h, c0[ev] + q + 1)
                vector.tensor_copy(bias_bc[:, q * 512:(q + 1) * 512], ps[q % 2][:, :])
            vector.engine_nop().then_inc(ev.h, 1)
            c0[ev] += 5
            ev.n += 1
            for oh in range(4):
                for tt in range(NT128):
                    i = oh * NT128 + tt
                    vector.wait_ge(mm.h, c0[mm] + i + 1)
                    if i >= 2:
                        vector.wait_ge(out.h, c0[out] + 16 * (i - 1))
                    vector.scalar_tensor_tensor(
                        ob[i % 2][:, :], ps[i % 2][:, :], 1.0, rb[i % 2][:, :],
                        op0=ALU.mult, op1=ALU.add)
                    ev.inc(vector.tensor_tensor(
                        ob[i % 2][:, :], ob[i % 2][:, :],
                        bias_bc[:, oh * 512:(oh + 1) * 512], ALU.add))

        @blk.gpsimd
        def _(gpsimd):
            for oh in range(4):
                for tt in range(NT128):
                    i = oh * NT128 + tt
                    gpsimd.wait_ge(ev.h, c0[ev] + i + 1)
                    out.dma(gpsimd.dma_start(
                        out=dst[tt * 128:(tt + 1) * 128, oh * 512:(oh + 1) * 512],
                        in_=ob[i % 2][:, :]))
            gpsimd.wait_ge(out.h, out.n)
    sp.put(*sems)


def p_proj_l1qk(nc, pes, sp, g, name, w_bf, xnT_src, out_dst, o_chunks, sqrt_scale, rope_src, ccw=None):
    ocols = o_chunks * 128
    G = max(1, o_chunks // 4)
    sems = [sp.get() for _ in range(11)]
    wl, xl, mm, rpd, sqd, summ, rse, rcd, bcm, apd, out = sems
    from contextlib import ExitStack as _ES
    _pes = _ES()
    blk = _pes.enter_context(nc.Block(name))
    w_sb = _pes.enter_context(nc.sbuf_tensor(name + "w", [128, NCH, ocols], BF16))
    xn_sb = [_pes.enter_context(nc.sbuf_tensor(f"{name}x{i}", [128, NCH, 512], BF16))
             for i in range(2)]
    ps = [_pes.enter_context(nc.psum_tensor(f"{name}p{i}", [128, 512], F32))
          for i in range(3)]
    ro_all = _pes.enter_context(nc.sbuf_tensor(name + "ro", [128, o_chunks, 512], BF16))
    rsq_all = _pes.enter_context(nc.sbuf_tensor(name + "rq", [128, o_chunks, 512], BF16))
    rc2 = _pes.enter_context(nc.sbuf_tensor(name + "rc", [128, S], F32))
    rs2 = _pes.enter_context(nc.sbuf_tensor(name + "rs", [128, S], F32))
    tmpc = _pes.enter_context(nc.sbuf_tensor(name + "tc", [128, 512], F32))
    tmps = _pes.enter_context(nc.sbuf_tensor(name + "ts", [128, 512], F32))
    rps = [_pes.enter_context(nc.psum_tensor(f"{name}rp{i}", [1, 512], F32))
           for i in range(2)]
    rrow = _pes.enter_context(nc.sbuf_tensor(name + "rr", [1, 2048], F32))
    rstd = _pes.enter_context(nc.sbuf_tensor(name + "rx", [1, 2048], F32))
    rbc = [_pes.enter_context(nc.psum_tensor(f"{name}rh{i}", [128, 512], F32))
           for i in range(2)]
    eb = [_pes.enter_context(nc.sbuf_tensor(f"{name}e{i}", [128, 512], BF16))
          for i in range(2)]
    ones_b = _pes.enter_context(nc.sbuf_tensor(name + "ob", [128, 1], BF16))
    if True:
        c0 = {s: s.n for s in sems}
        GS = o_chunks // G

        @blk.sync
        def _(sync):
            for s_ in (ccw or ()):
                sync.wait_ge(s_.h, s_.n)
            wl.dma(sync.dma_start(out=rc2[0:64, :], in_=rope_src[0][:, :]))
            wl.dma(sync.dma_start(out=rc2[64:128, :], in_=rope_src[0][:, :]))
            wl.dma(sync.dma_start(out=rs2[0:64, :], in_=rope_src[1][:, :]))
            wl.dma(sync.dma_start(out=rs2[64:128, :], in_=rope_src[1][:, :]))
            for c in range(NCH):
                wl.dma(sync.dma_start(out=w_sb[:, c, :],
                                      in_=w_bf[c * 128:(c + 1) * 128, :]))
            for tt in range(NT512):
                if tt >= 2:
                    sync.wait_ge(mm.h, c0[mm] + (tt - 1) * o_chunks)
                xl.dma(sync.dma_start(
                    out=xn_sb[tt % 2][:, :, :],
                    in_=xnT_src.rearrange("(c p) t -> p c t", p=128)[:, :, tt * 512:(tt + 1) * 512]))

        @blk.tensor
        def _(tensor):
            tensor.wait_ge(wl.h, wl.n)
            for tt in range(NT512):
                tensor.wait_ge(xl.h, c0[xl] + 16 * (tt + 1))
                for oc in range(o_chunks):
                    i = tt * o_chunks + oc
                    if i >= 3:
                        tensor.wait_ge(rpd.h, c0[rpd] + i - 1)
                    for c in range(NCH):
                        m = tensor.matmul(ps[i % 3][:, :],
                                          w_sb[:, c, oc * 128:(oc + 1) * 128],
                                          xn_sb[tt % 2][:, c, :],
                                          start=(c == 0), stop=(c == NCH - 1))
                        if c == NCH - 1:
                            mm.inc(m)
                for gidx in range(G):
                    gg = tt * G + gidx
                    for k in range(GS):
                        oc = gidx * GS + k
                        i = tt * o_chunks + oc
                        tensor.wait_ge(sqd.h, c0[sqd] + i + 1)
                        if i >= 2:
                            tensor.wait_ge(rse.h, c0[rse] + i - 1)
                        summ.inc(tensor.matmul(rps[k % 2][0:1, :], ones_b[:, 0:1],
                                               rsq_all[:, oc, :], start=True, stop=True))
                    tensor.wait_ge(rcd.h, c0[rcd] + 3 * (gg + 1))
                    for k in range(GS):
                        oc = gidx * GS + k
                        j = tt * o_chunks + oc
                        if j >= 2:
                            tensor.wait_ge(apd.h, c0[apd] + j - 1)
                        bcm.inc(tensor.matmul(rbc[k % 2][:, :], g["ones_r"][0:1, :],
                                              rstd[0:1, k * 512:(k + 1) * 512],
                                              start=True, stop=True))

        @blk.vector
        def _(vector):
            vector.memset(ones_b[:, 0:1], 1.0)
            vector.engine_nop().then_inc(rpd.h, 1)
            rpd.n += 1
            for tt in range(NT512):
                for oc in range(o_chunks):
                    i = tt * o_chunks + oc
                    vector.wait_ge(mm.h, c0[mm] + i + 1)
                    if tt >= 1:
                        vector.wait_ge(sqd.h,
                                       c0[sqd] + (tt - 1) * o_chunks + oc + 1)
                    p = ps[i % 3]
                    t0c = tt * 512
                    vector.tensor_tensor(tmpc[:, :], p[:, :], rc2[:, t0c:t0c + 512], ALU.mult)
                    vector.tensor_tensor(tmps[0:64, :], p[64:128, :],
                                         rs2[0:64, t0c:t0c + 512], ALU.mult)
                    vector.tensor_tensor(tmps[64:128, :], p[0:64, :],
                                         rs2[64:128, t0c:t0c + 512], ALU.mult)
                    vector.tensor_add(ro_all[0:64, oc, :], tmpc[0:64, :], tmps[0:64, :])
                    rpd.inc(vector.tensor_sub(ro_all[64:128, oc, :], tmpc[64:128, :],
                                              tmps[64:128, :]))
                for gidx in range(G):
                    gg = tt * G + gidx
                    vector.wait_ge(rse.h, c0[rse] + gg * GS + GS)
                    rcd.inc(vector.tensor_scalar(rrow[0:1, 0:GS * 512], rrow[0:1, 0:GS * 512],
                                                 1.0 / DH, 1e-6,
                                                 op0=ALU.mult, op1=ALU.add))
                    vector.wait_ge(rcd.h, rcd.n)
                    rcd.inc(vector.reciprocal(rrow[0:1, 0:GS * 512], rrow[0:1, 0:GS * 512]))
                    rcd.n += 1
                    for k in range(GS):
                        oc = gidx * GS + k
                        j = tt * o_chunks + oc
                        vector.wait_ge(bcm.h, c0[bcm] + gg * GS + k + 1)
                        if j >= 2:
                            vector.wait_ge(out.h, c0[out] + 16 * (j - 1))
                        apd.inc(vector.tensor_tensor(eb[j % 2][:, :],
                                                     ro_all[:, oc, :], rbc[k % 2][:, :],
                                                     ALU.mult))

        @blk.scalar
        def _(scalar):
            for tt in range(NT512):
                for oc in range(o_chunks):
                    i = tt * o_chunks + oc
                    scalar.wait_ge(rpd.h, c0[rpd] + i + 2)
                    if tt >= 1:
                        scalar.wait_ge(summ.h,
                                       c0[summ] + (tt - 1) * o_chunks + oc + 1)
                    sqd.inc(scalar.activation(rsq_all[:, oc, :], ro_all[:, oc, :],
                                              AF.Square))
                for gidx in range(G):
                    gg = tt * G + gidx
                    for k in range(GS):
                        oc = gidx * GS + k
                        i = tt * o_chunks + oc
                        scalar.wait_ge(summ.h, c0[summ] + i + 1)
                        rse.inc(scalar.copy(rrow[0:1, k * 512:(k + 1) * 512],
                                            rps[k % 2][0:1, :]))
                    scalar.wait_ge(rcd.h, c0[rcd] + 3 * gg + 2)
                    if gg >= 1:
                        scalar.wait_ge(bcm.h, c0[bcm] + gg * GS)
                    scalar.activation(rstd[0:1, 0:GS * 512], rrow[0:1, 0:GS * 512],
                                      AF.Sqrt,
                                      scale=float(sqrt_scale)).then_inc(rcd.h, 1)

        @blk.gpsimd
        def _(gpsimd):
            for tt in range(NT512):
                for oc in range(o_chunks):
                    j = tt * o_chunks + oc
                    gpsimd.wait_ge(apd.h, c0[apd] + j + 1)
                    out.dma(gpsimd.dma_start(
                        out=out_dst[oc * 128:(oc + 1) * 128, tt * 512:(tt + 1) * 512],
                        in_=eb[j % 2][:, :]))
            gpsimd.wait_ge(out.h, out.n)
    _pes.close()
    sp.put(*sems)


PHASES = ["consts", "cast", "wgen", "ln0", "q0", "k0", "negm", "v0", "attn0",
          "ffn1", "comb0", "ln1", "q1", "k1", "v1", "attn1", "comb1"]


def build_nc(upto=None, dbg=()):
    nc = bass.Bass("TRN2", target_bir_lowering=False)
    from contextlib import ExitStack
    es = ExitStack()

    t = {}
    t["x"] = nc.dram_tensor("x", [S, D], F32, kind="ExternalInput").ap()
    for k, cols in [("wq0", D), ("wk0", D), ("wv0", D), ("wo0", D),
                    ("wq1", D), ("wk1", 512), ("wv1", 512),
                    ("wo1", D), ("w1ffn", D)]:
        t[k + "s"] = nc.dram_tensor(k + "s", [256, cols], F32, kind="ExternalInput").ap()
    for k, shp in [("bq0", [128, 16]), ("bk0", [128, 16]), ("bv0", [128, 16]),
                   ("bo0", [1, D]), ("b1", [1, D]), ("lut", [1, 16]),
                   ("tri", [128, 128]), ("ropecos", [64, S]), ("ropesin", [64, S])]:
        t[k] = nc.dram_tensor(k, shp, F32, kind="ExternalInput").ap()
    t["ident"] = nc.dram_tensor("ident", [128, 128], BF16, kind="ExternalInput").ap()
    t["gidxs"] = nc.dram_tensor("gidxs", [256, IFF], I32, kind="ExternalInput").ap()
    t["uidxs"] = nc.dram_tensor("uidxs", [256, IFF], I32, kind="ExternalInput").ap()
    t["didxs"] = nc.dram_tensor("didxs", [688, D], I32, kind="ExternalInput").ap()
    t["y"] = nc.dram_tensor("y", [S, D], F32, kind="ExternalOutput").ap()

    scr = [("cast_loc", [256, 15360], BF16), ("cast_full", [D, 15360], BF16),
           ("wgu_loc", [256, 2 * IFF], BF16), ("wgu_full", [D, 2 * IFF], BF16),
           ("wd_loc", [688, D], BF16), ("wd_bf", [IFF, D], BF16),
           ("xnT", [D, S], BF16), ("xnT2", [D, S], BF16),
           ("qTs", [D, S], BF16), ("kTs", [D, S], BF16),
           ("qTs2", [D, S], BF16), ("kTs2", [512, S], BF16),
           ("vtok", [S, D], BF16), ("vtok2", [S, 512], BF16),
           ("attnT", [S, D], BF16), ("attnT2", [S, D], BF16),
           ("hT", [IFF, S], BF16), ("x1", [S, D], F32)]
    for k, shp, dt in scr:
        t[k] = nc.dram_tensor(k, shp, dt, kind="Internal").ap()
    off = 0
    for key, cols in [("wq0", D), ("wk0", D), ("wv0", D), ("wo0", D), ("wq1", D),
                      ("wk1", 512), ("wv1", 512), ("wo1", D), ("bw1", D)]:
        t[key + "_bf"] = t["cast_full"][:, off:off + cols]
        off += cols
    t["wg_bf"] = t["wgu_full"][:, 0:IFF]
    t["wu_bf"] = t["wgu_full"][:, IFF:2 * IFF]

    g = {}
    g["ident"] = es.enter_context(nc.sbuf_tensor("g_id", [128, 128], BF16))
    g["tri"] = es.enter_context(nc.sbuf_tensor("g_tri", [128, 128], F32))
    g["bq"] = es.enter_context(nc.sbuf_tensor("g_bq", [128, 16], F32))
    g["bk"] = es.enter_context(nc.sbuf_tensor("g_bk", [128, 16], F32))
    g["bv"] = es.enter_context(nc.sbuf_tensor("g_bv", [128, 16], F32))
    g["bo_r"] = es.enter_context(nc.sbuf_tensor("g_bor", [1, D], F32))
    g["b1_r"] = es.enter_context(nc.sbuf_tensor("g_b1r", [1, D], F32))
    g["ones_f"] = es.enter_context(nc.sbuf_tensor("g_onf", [128, 1], F32))
    g["ones_r"] = es.enter_context(nc.sbuf_tensor("g_onr", [1, 128], F32))
    g["normcol"] = es.enter_context(nc.sbuf_tensor("g_nc", [1, 256], F32))
    g["negm"] = es.enter_context(nc.sbuf_tensor("g_ngm", [128, 1], F32))

    sp = SemPool(es, nc)
    stop = PHASES.index(upto) if upto else len(PHASES) - 1

    def run(label, fn):
        if PHASES.index(label) <= stop:
            fn()

    cc_cast = Sem(es, nc, "cc_cast")
    cc_wgen = Sem(es, nc, "cc_wgen")

    from contextlib import ExitStack as _ES
    pre_es = _ES()
    with nc.Block("prefix") as preblk:
        run("consts", lambda: p_consts(nc, pre_es, sp, t, g, preblk))
        if stop >= PHASES.index("ln0"):
            p_layernorm(nc, pre_es, sp, g, "ln0", t["x"], t["xnT"], blk=preblk)
        run("cast", lambda: p_cast(nc, pre_es, sp, t, preblk, cc_cast))
        run("wgen", lambda: p_wgen(nc, pre_es, sp, t, preblk, cc_wgen))
    pre_es.close()
    g["cc_cast"] = cc_cast
    g["cc_wgen"] = cc_wgen
    CCW = (cc_cast, cc_wgen)
    run("q0", lambda: p_proj(nc, None, sp, g, "q0", t["wq0_bf"], t["xnT"], t["qTs"],
                             16, True, bias_col="bq", scale=L0_SCALE, norm_base=0,
                             ccw=CCW))
    run("k0", lambda: p_proj(nc, None, sp, g, "k0", t["wk0_bf"], t["xnT"], t["kTs"],
                             16, True, bias_col="bk", scale=L0_SCALE, norm_base=128,
                             ccw=CCW))
    run("negm", lambda: p_negm(nc, None, sp, g))
    run("v0", lambda: p_proj(nc, None, sp, g, "v0", t["wv0_bf"], t["xnT"], t["vtok"],
                             16, True, bias_col="bv", token_major=True, ccw=CCW))
    run("attn0", lambda: p_attn(nc, None, sp, g, "at0", t["qTs"], t["kTs"], t["vtok"],
                                t["attnT"], H, None))
    run("ffn1", lambda: p_ffn1(nc, None, sp, g, t, ccw=CCW))
    run("comb0", lambda: p_combine(nc, None, sp, g, "cb0",
                                   [(t["attnT"], t["wo0_bf"], NCH, True),
                                    (t["hT"], t["wd_bf"], ICH, False)],
                                   t["x"], "bo_r", t["x1"], ccw=CCW))
    run("ln1", lambda: p_layernorm(nc, None, sp, g, "ln1", t["x1"], t["xnT2"]))
    run("q1", lambda: p_proj_l1qk(nc, None, sp, g, "q1", t["wq1_bf"], t["xnT2"],
                                  t["qTs2"], 16, 1.0,
                                  (t["ropecos"], t["ropesin"]), ccw=CCW))
    run("k1", lambda: p_proj_l1qk(nc, None, sp, g, "k1", t["wk1_bf"], t["xnT2"],
                                  t["kTs2"], 4, L1_KSCALE2,
                                  (t["ropecos"], t["ropesin"]), ccw=CCW))
    run("v1", lambda: p_proj(nc, None, sp, g, "v1", t["wv1_bf"], t["xnT2"], t["vtok2"],
                             4, False, token_major=True, ccw=CCW))
    run("attn1", lambda: p_attn(nc, None, sp, g, "at1", t["qTs2"], t["kTs2"], t["vtok2"],
                                t["attnT2"], 4, L1_NEG_BOUND))
    run("comb1", lambda: p_combine(nc, None, sp, g, "cb1",
                                   [(t["attnT2"], t["wo1_bf"], NCH, False),
                                    (t["xnT2"], t["bw1_bf"], NCH, True)],
                                   t["x1"], "b1_r", t["y"], ccw=CCW))

    if dbg:
        dsem = sp.get()
        with nc.Block("pdbg") as blk:
            @blk.gpsimd
            def _(gpsimd):
                gpsimd.wait_ge(cc_cast.h, cc_cast.n)
                gpsimd.wait_ge(cc_wgen.h, cc_wgen.n)
                for nm in dbg:
                    src = t[nm]
                    dt = {"x1": F32}.get(nm, BF16)
                    od = nc.dram_tensor("dbg_" + nm, list(src.shape), dt,
                                        kind="ExternalOutput").ap()
                    dsem.dma(gpsimd.dma_start(out=od[:, :], in_=src[:, :]))
                gpsimd.wait_ge(dsem.h, dsem.n)
        sp.put(dsem)

    es.close()
    return nc


def _host_consts():
    c = {}
    c["ident"] = np.eye(128, dtype=ml_dtypes.bfloat16)
    kk = np.arange(128)[:, None]
    qq = np.arange(128)[None, :]
    c["tri"] = np.where(qq >= kk, np.float32(0.0), np.float32(MASK)).astype(np.float32)
    half = 64
    inv = (np.float32(1.0) / (np.float32(100000.0)
           ** (np.arange(half, dtype=np.float32) / np.float32(half)))).astype(np.float32)
    freqs = (np.arange(S, dtype=np.float32)[:, None] * inv[None, :]).astype(np.float32)
    c["ropecos"] = np.ascontiguousarray(np.cos(freqs).astype(np.float32).T)
    c["ropesin"] = np.ascontiguousarray(np.sin(freqs).astype(np.float32).T)
    return c


def host_in_maps(inputs):
    ii = {k: np.asarray(v) for k, v in inputs.items()}
    X = ii["L0_attn_X"].astype(np.float32, copy=False)

    common = dict(_host_consts())
    for k, src in [("bq0", "L0_attn_bq"), ("bk0", "L0_attn_bk"), ("bv0", "L0_attn_bv")]:
        common[k] = np.ascontiguousarray(ii[src].astype(np.float32).reshape(16, 128).T)
    common["bo0"] = ii["L0_attn_bo"].astype(np.float32).reshape(1, D)
    common["lut"] = ii["L0_ffn_lut"].astype(np.float32).reshape(1, 16)
    common["b1"] = ii["L1_ffn_bias"].astype(np.float32).reshape(1, D)
    for k in list(common):
        common[k] = np.ascontiguousarray(np.asarray(common[k]))

    full = {"wq0": ii["L0_attn_Wq"], "wk0": ii["L0_attn_Wk"],
            "wv0": ii["L0_attn_Wv"], "wo0": ii["L0_attn_Wo"],
            "wq1": ii["L1_attn_wq"], "wk1": ii["L1_attn_wk"],
            "wv1": ii["L1_attn_wv"], "wo1": ii["L1_attn_wo"],
            "w1ffn": ii["L1_ffn_weight"]}
    full = {k: np.asarray(v, np.float32) for k, v in full.items()}
    gT = np.ascontiguousarray(ii["L0_ffn_gate_idx"].astype(np.int32).T)
    uT = np.ascontiguousarray(ii["L0_ffn_up_idx"].astype(np.int32).T)
    dT = np.ascontiguousarray(ii["L0_ffn_down_idx"].astype(np.int32).T)

    in_maps = []
    for b in range(8):
        m = dict(common, x=np.ascontiguousarray(X[b]))
        for k, v in full.items():
            m[k + "s"] = np.ascontiguousarray(v[b * 256:(b + 1) * 256])
        m["gidxs"] = np.ascontiguousarray(gT[b * 256:(b + 1) * 256])
        m["uidxs"] = np.ascontiguousarray(uT[b * 256:(b + 1) * 256])
        m["didxs"] = np.ascontiguousarray(dT[b * 688:(b + 1) * 688])
        in_maps.append(m)
    return in_maps


def kernel(**inputs):
    in_maps = host_in_maps(inputs)
    nc = build_nc()
    res = run_bass_kernel_spmd(nc, in_maps, core_ids=list(range(8)))
    return np.stack([np.asarray(res.results[b]["y"], dtype=np.float32)
                     for b in range(8)], axis=0)
